# version 02
# baseline (speedup 1.0000x reference)
import numpy as np
from contextlib import ExitStack

import concourse.bass as bass
import concourse.mybir as mybir
from concourse.bass_utils import run_bass_kernel_spmd

F32 = mybir.dt.float32
BF16 = mybir.dt.bfloat16
U32 = mybir.dt.uint32
I32 = mybir.dt.int32
AF = mybir.ActivationFunctionType
ALU = mybir.AluOpType
AX = mybir.AxisListType

B, T, D = 4, 4096, 2048
KC = D // 128
S_SEL = 2048
ISCALE = float(1.0 / np.sqrt(128.0))


class Prog:
    def __init__(self):
        self.nc = bass.Bass("TRN2", target_bir_lowering=False)
        self.ctx = ExitStack()
        self.cnt = {}
        self.sems = {}
        self.block = None

    def start(self):
        self.block = self.ctx.enter_context(self.nc.Block())
        self.sem("fence")

    def sem(self, name):
        if name not in self.sems:
            self.sems[name] = self.ctx.enter_context(self.nc.semaphore(name))
            self.cnt[name] = 0
        return self.sems[name]

    def inc(self, instr, name, n=1):
        instr.then_inc(self.sem(name), n)
        self.cnt[name] += n
        return self.cnt[name]

    def val(self, name):
        return self.cnt.get(name, 0)

    def w(self, eng, name, v=None):
        vv = self.cnt.get(name, 0) if v is None else v
        if vv > 0:
            eng.wait_ge(self.sem(name), vv)

    def sbuf(self, name, shape, dt, stack=None):
        return (stack or self.ctx).enter_context(self.nc.sbuf_tensor(name, shape, dt))

    def psum(self, name, shape, dt=F32, stack=None):
        return (stack or self.ctx).enter_context(self.nc.psum_tensor(name, shape, dt))

    def dram(self, name, shape, dt, **kw):
        return self.nc.dram_tensor(name, shape, dt, **kw).ap()

    def inp(self, name, shape, dt):
        return self.dram(name, shape, dt, kind="ExternalInput")

    def outp(self, name, shape, dt):
        return self.dram(name, shape, dt, kind="ExternalOutput")

    def barrier(self, names):
        nc = self.nc
        for eng in (nc.tensor, nc.vector, nc.scalar, nc.gpsimd, nc.sync):
            for n in names:
                if self.val(n) > 0:
                    eng.wait_ge(self.sem(n), self.cnt[n])

    def release(self, prefixes):
        nc = self.nc
        names = [n for n in self.sems
                 if any(n.startswith(px) for px in prefixes)]
        if not names:
            return
        handles = [self.sems[n] for n in names]
        nc.clear_and_free_semaphores(handles)
        for n in names:
            del self.sems[n]
            del self.cnt[n]
        v = self.inc(nc.gpsimd.engine_nop(), "fence", 1)
        for eng in (nc.tensor, nc.vector, nc.scalar, nc.sync):
            eng.wait_ge(self.sem("fence"), v)

    def close(self):
        self.ctx.close()




def emit_ln(p, xb, xn, sc, swrt, ntok, pfx):
    nc = p.nc
    with ExitStack() as st:
        sxt = [p.sbuf(f"{pfx}x{i}", [128, D], F32, st) for i in range(2)]
        sxn = [p.sbuf(f"{pfx}n{i}", [128, D], F32, st) for i in range(2)]
        sscr = [p.sbuf(f"{pfx}scr{i}", [128, D], F32, st) for i in range(2)]
        stat = [p.sbuf(f"{pfx}st{i}", [128, 6], F32, st) for i in range(2)]
        scol = [p.sbuf(f"{pfx}sc{i}", [128, 1], F32, st) for i in range(2)]
        SQ, VV = pfx + "sq", pfx + "vv"
        vv_end, sq_end = [], []

        for i in range(ntok // 128):
            bi = i % 2
            LD, SO = f"{pfx}ld{bi}", f"{pfx}so{bi}"
            if i >= 2:
                p.w(nc.sync, VV, vv_end[i - 2])
                p.w(nc.sync, SQ, sq_end[i - 2])
            ldv = p.inc(nc.sync.dma_start(out=sxt[bi][:], in_=xb[i * 128:(i + 1) * 128, :]), LD, 16)

            p.w(nc.scalar, LD, ldv)
            if i >= 2:
                p.w(nc.scalar, VV, vv_end[i - 2])
            sqv = p.inc(nc.scalar.activation(sscr[bi][:], sxt[bi][:], AF.Square,
                                             accum_out=stat[bi][:, 4:5]), SQ, 1)

            p.w(nc.vector, LD, ldv)
            if i >= 2:
                p.w(nc.vector, SO)
            v = p.inc(nc.vector.reduce_sum(stat[bi][:, 5:6], sxt[bi][:], axis=AX.X), VV, 1)
            p.w(nc.vector, VV, v)
            nc.vector.tensor_scalar_mul(stat[bi][:, 0:1], stat[bi][:, 5:6], 1.0 / D)
            p.w(nc.vector, SQ, sqv)
            v = p.inc(nc.vector.tensor_scalar_mul(stat[bi][:, 1:2], stat[bi][:, 4:5], 1.0 / D), VV, 1)
            p.w(nc.vector, VV, v)
            v = p.inc(nc.vector.tensor_tensor(stat[bi][:, 2:3], stat[bi][:, 0:1], stat[bi][:, 0:1], ALU.mult), VV, 1)
            p.w(nc.vector, VV, v)
            v = p.inc(nc.vector.tensor_tensor(stat[bi][:, 3:4], stat[bi][:, 1:2], stat[bi][:, 2:3], ALU.subtract), VV, 1)
            p.w(nc.vector, VV, v)
            v = p.inc(nc.vector.tensor_scalar_add(stat[bi][:, 3:4], stat[bi][:, 3:4], 1e-5), VV, 1)

            p.w(nc.scalar, VV, v)
            sqv = p.inc(nc.scalar.activation(stat[bi][:, 2:3], stat[bi][:, 3:4], AF.Sqrt), SQ, 1)

            p.w(nc.vector, SQ, sqv)
            v = p.inc(nc.vector.reciprocal(stat[bi][:, 2:3], stat[bi][:, 2:3]), VV, 1)
            p.w(nc.vector, VV, v)
            v = p.inc(nc.vector.tensor_tensor(stat[bi][:, 1:2], stat[bi][:, 2:3], stat[bi][:, 2:3], ALU.mult), VV, 1)
            p.w(nc.vector, VV, v)
            v = p.inc(nc.vector.tensor_tensor(stat[bi][:, 1:2], stat[bi][:, 1:2], stat[bi][:, 3:4], ALU.mult), VV, 1)
            p.w(nc.vector, VV, v)
            v = p.inc(nc.vector.tensor_scalar(stat[bi][:, 1:2], stat[bi][:, 1:2], -0.5, 1.5, ALU.mult, ALU.add), VV, 1)
            p.w(nc.vector, VV, v)
            v = p.inc(nc.vector.tensor_tensor(stat[bi][:, 3:4], stat[bi][:, 2:3], stat[bi][:, 1:2], ALU.mult), VV, 1)
            p.w(nc.vector, VV, v)
            v = p.inc(nc.vector.tensor_scalar(sxn[bi][:], sxt[bi][:], stat[bi][:, 0:1], stat[bi][:, 3:4],
                                              ALU.subtract, ALU.mult), VV, 1)
            if swrt is not None:
                p.w(nc.vector, VV, v)
                v = p.inc(nc.vector.tensor_tensor(sscr[bi][:], sxn[bi][:], swrt[:], ALU.mult), VV, 1)
                p.w(nc.vector, VV, v)
                v = p.inc(nc.vector.reduce_sum(sscr[bi][:, 0:16],
                                               sscr[bi][:].rearrange("p (c q) -> p c q", c=16), axis=AX.X), VV, 1)
                p.w(nc.vector, VV, v)
                v = p.inc(nc.vector.reduce_sum(scol[bi][:], sscr[bi][:, 0:16], axis=AX.X), VV, 1)

            p.w(nc.sync, VV, v)
            p.inc(nc.sync.dma_start(out=xn[i * 128:(i + 1) * 128, :], in_=sxn[bi][:]), SO, 16)
            if swrt is not None:
                p.inc(nc.sync.dma_start(out=sc[i * 128:(i + 1) * 128, :], in_=scol[bi][:, :]), SO, 16)
            vv_end.append(p.val(VV))
            sq_end.append(p.val(SQ))
        p.barrier([pfx + "so0", pfx + "so1"])
        p.release([pfx])


def emit_rms(p, xb, dst, wrep, eps, ntok, pfx):
    nc = p.nc
    with ExitStack() as st:
        sxt = [p.sbuf(f"{pfx}x{i}", [128, D], F32, st) for i in range(2)]
        sxn = [p.sbuf(f"{pfx}n{i}", [128, D], F32, st) for i in range(2)]
        sscr = p.sbuf(f"{pfx}scr", [128, D], F32, st)
        stat = [p.sbuf(f"{pfx}st{i}", [128, 2], F32, st) for i in range(2)]
        SQ, VV = pfx + "sq", pfx + "vv"
        vv_end, sq_end = [], []
        for i in range(ntok // 128):
            bi = i % 2
            LD, SO = f"{pfx}ld{bi}", f"{pfx}so{bi}"
            if i >= 2:
                p.w(nc.sync, VV, vv_end[i - 2])
                p.w(nc.sync, SQ, sq_end[i - 2])
            ldv = p.inc(nc.sync.dma_start(out=sxt[bi][:], in_=xb[i * 128:(i + 1) * 128, :]), LD, 16)
            p.w(nc.scalar, LD, ldv)
            if i >= 1:
                p.w(nc.scalar, VV)
            sqv = p.inc(nc.scalar.activation(sscr[:], sxt[bi][:], AF.Square,
                                             accum_out=stat[bi][:, 0:1]), SQ, 1)
            p.w(nc.vector, SQ, sqv)
            v = p.inc(nc.vector.tensor_scalar(stat[bi][:, 0:1], stat[bi][:, 0:1], 1.0 / D, eps,
                                              ALU.mult, ALU.add), VV, 1)
            p.w(nc.scalar, VV, v)
            sqv = p.inc(nc.scalar.activation(stat[bi][:, 1:2], stat[bi][:, 0:1], AF.Sqrt), SQ, 1)
            p.w(nc.vector, SQ, sqv)
            if i >= 2:
                p.w(nc.vector, SO)
            v = p.inc(nc.vector.reciprocal(stat[bi][:, 1:2], stat[bi][:, 1:2]), VV, 1)
            p.w(nc.vector, VV, v)
            v = p.inc(nc.vector.scalar_tensor_tensor(sxn[bi][:], sxt[bi][:], stat[bi][:, 1:2],
                                                     wrep[:], ALU.mult, ALU.mult), VV, 1)
            p.w(nc.sync, VV, v)
            p.inc(nc.sync.dma_start(out=dst[i * 128:(i + 1) * 128, :], in_=sxn[bi][:]), SO, 16)
            vv_end.append(p.val(VV))
            sq_end.append(p.val(SQ))
        p.barrier([pfx + "so0", pfx + "so1"])
        p.release([pfx])


def emit_tokmaj_to_fm(p, src, dst_v, C, ntok, pfx, nchunk=KC, dt=BF16):
    nc = p.nc
    ident = C["sident"] if dt == BF16 else C["sidentf"]
    grp = 8 if dt == BF16 else 4
    with ExitStack() as st:
        stok = [p.sbuf(f"{pfx}x{i}", [128, nchunk * 128], dt, st) for i in range(2)]
        sfm = [p.sbuf(f"{pfx}f{i}", [128, nchunk * 128], dt, st) for i in range(2)]
        nh = (nchunk + grp - 1) // grp
        ptp = [p.psum(f"{pfx}p{i}", [128, grp * 128], dt, st) for i in range(min(nh, 2) if nh > 1 else 1)]
        TP, EV = pfx + "tp", pfx + "ev"

        for i in range(ntok // 128):
            bi = i % 2
            LD, SO = f"{pfx}ld{bi}", f"{pfx}so{bi}"
            if i >= 2:
                p.w(nc.gpsimd, TP, nh * (i - 1))
            ldv = p.inc(nc.gpsimd.dma_start(out=stok[bi][:], in_=src[i * 128:(i + 1) * 128, :]), LD, 16)

            p.w(nc.tensor, LD, ldv)
            for h in range(nh):
                hb = h % len(ptp)
                prev = nh * (i - 1) + h + 1 if i >= 1 else (h - len(ptp) + 1 if h >= len(ptp) else 0)
                if h >= len(ptp):
                    prev = nh * i + h - len(ptp) + 1
                if prev > 0:
                    p.w(nc.tensor, EV, prev)
                for c in range(min(grp, nchunk - h * grp)):
                    cc = h * grp + c
                    ins = nc.tensor.transpose(ptp[hb][:, c * 128:(c + 1) * 128],
                                              stok[bi][:, cc * 128:(cc + 1) * 128], ident[:])
                p.inc(ins, TP, 1)
            tpv = p.val(TP)

            for h in range(nh):
                hb = h % len(ptp)
                p.w(nc.vector, TP, tpv - nh + 1 + h)
                if i >= 2:
                    p.w(nc.vector, SO)
                w = min(grp, nchunk - h * grp) * 128
                p.inc(nc.vector.tensor_copy(sfm[bi][:, h * grp * 128:h * grp * 128 + w], ptp[hb][:, 0:w]), EV, 1)
            evv = p.val(EV)

            p.w(nc.sync, EV, evv)
            for c in range(nchunk):
                p.inc(nc.sync.dma_start(out=dst_v[c, :, i * 128:(i + 1) * 128],
                                        in_=sfm[bi][:, c * 128:(c + 1) * 128]), SO, 16)
        p.barrier([pfx + "so0", pfx + "so1"])
        p.release([pfx])


def emit_qkv_rope(p, C, xnf_v, qf_v, kf_v, vt, Tn, nq, nkv, bq_t, bk_t, bv_t,
                  wq_v, wk_v, wv_v, cos_src, sin_src, nrm_slot0, pfx):
    nc = p.nc
    NB = Tn // 512
    snrm = C["snrm"]
    with ExitStack() as st:
        swq = p.sbuf(pfx + "wq", [128, KC * nq * 128], BF16, st)
        swk = p.sbuf(pfx + "wk", [128, KC * nkv * 128], BF16, st)
        swv = p.sbuf(pfx + "wv", [128, KC * nkv * 128], BF16, st)
        scosb = p.sbuf(pfx + "cos", [128, Tn], BF16, st)
        ssinf = p.sbuf(pfx + "sin", [128, Tn], F32, st)
        sxf = [p.sbuf(f"{pfx}x{i}", [128, KC * 512], BF16, st) for i in range(2)]
        srcb = [p.sbuf(f"{pfx}s{i}", [128, 512], BF16, st) for i in range(2)]
        sta = [p.sbuf(f"{pfx}t{i}", [128, 512], BF16, st) for i in range(2)]
        stb = [p.sbuf(f"{pfx}u{i}", [128, 512], BF16, st) for i in range(2)]
        sout = [p.sbuf(f"{pfx}o{i}", [128, 512], BF16, st) for i in range(2)]
        sqsc = p.sbuf(pfx + "q", [128, 512], F32, st)
        svtt = [p.sbuf(f"{pfx}v{i}", [128, 512], BF16, st) for i in range(2)]
        pk = [p.psum(f"{pfx}pk{i}", [128, 512], F32, st) for i in range(2)]
        prot = [p.psum(f"{pfx}pr{i}", [128, 512], F32, st) for i in range(2)]
        pq = [p.psum(f"{pfx}pq{i}", [128, 512], F32, st) for i in range(2)]
        pn = p.psum(pfx + "pn", [1, 512], F32, st)

        wwq, wwk = nq * 128, nkv * 128
        for k in range(KC):
            p.inc(nc.gpsimd.dma_start(out=swq[:, k * wwq:(k + 1) * wwq], in_=wq_v[k]), pfx + "wld", 16)
            p.inc(nc.gpsimd.dma_start(out=swk[:, k * wwk:(k + 1) * wwk], in_=wk_v[k]), pfx + "wld", 16)
            p.inc(nc.gpsimd.dma_start(out=swv[:, k * wwk:(k + 1) * wwk], in_=wv_v[k]), pfx + "wld", 16)
        p.inc(nc.gpsimd.dma_start(out=scosb[:], in_=cos_src), pfx + "wld", 16)
        p.inc(nc.gpsimd.dma_start(out=ssinf[:], in_=sin_src), pfx + "wld", 16)
        WLD = p.val(pfx + "wld")

        for tb in range(NB):
            bx = tb % 2
            if tb >= 2:
                p.w(nc.sync, pfx + "mm")
                p.w(nc.sync, pfx + "vm")
            for k in range(KC):
                p.inc(nc.sync.dma_start(out=sxf[bx][:, k * 512:(k + 1) * 512],
                                        in_=xnf_v[k, :, tb * 512:(tb + 1) * 512]), pfx + "ld", 16)
            ldv = p.val(pfx + "ld")
            cosw = scosb[:, tb * 512:(tb + 1) * 512]
            sinw = ssinf[:, tb * 512:(tb + 1) * 512]

            def rope_chunk(m, wsb, wwid, bias_ap, dst_chunk_v, nslot):
                pidx = m % 2
                p.w(nc.tensor, pfx + "ld", ldv)
                if tb == 0 and m == 0:
                    p.w(nc.tensor, pfx + "wld", WLD)
                p.w(nc.tensor, pfx + "sc")
                for k in range(KC):
                    ins = nc.tensor.matmul(pk[pidx][:],
                                           wsb[:, k * wwid + m * 128:k * wwid + (m + 1) * 128],
                                           sxf[bx][:, k * 512:(k + 1) * 512],
                                           start=(k == 0), stop=(k == KC - 1))
                mmv = p.inc(ins, pfx + "mm", 1)
                p.w(nc.scalar, pfx + "mm", mmv)
                p.w(nc.scalar, pfx + "rp")
                p.w(nc.scalar, pfx + "rt")
                scv = p.inc(nc.scalar.activation(srcb[pidx][:], pk[pidx][:], AF.Identity, bias=bias_ap), pfx + "sc", 1)
                p.w(nc.scalar, pfx + "nq")
                scv2 = p.inc(nc.scalar.activation(sqsc[:], srcb[pidx][:], AF.Square), pfx + "sc", 1)
                p.w(nc.tensor, pfx + "sc", scv2)
                p.w(nc.tensor, pfx + "nm")
                nv = p.inc(nc.tensor.matmul(pn[:], C["sonesf"][:, 0:1], sqsc[:], start=True, stop=True), pfx + "nq", 1)
                p.w(nc.vector, pfx + "nq", nv)
                p.inc(nc.vector.reduce_max(snrm[0:1, nslot * NB + tb:nslot * NB + tb + 1],
                                           pn[0:1, :], axis=AX.X), pfx + "nm", 1)
                p.w(nc.tensor, pfx + "rp")
                rv = p.inc(nc.tensor.matmul(prot[pidx][:], C["spmat"][:], srcb[pidx][:],
                                            start=True, stop=True), pfx + "rt", 1)
                p.w(nc.vector, pfx + "sc", scv)
                p.w(nc.vector, pfx + "so0")
                p.w(nc.vector, pfx + "so1")
                v = p.inc(nc.vector.tensor_tensor(sta[pidx][:], srcb[pidx][:], cosw, ALU.mult), pfx + "rp", 1)
                p.w(nc.vector, pfx + "rt", rv)
                v = p.inc(nc.vector.tensor_tensor(stb[pidx][:], prot[pidx][:], sinw, ALU.mult), pfx + "rp", 1)
                p.w(nc.vector, pfx + "rp", v)
                v = p.inc(nc.vector.tensor_tensor(sout[pidx][:], sta[pidx][:], stb[pidx][:], ALU.add), pfx + "rp", 1)
                p.w(nc.sync, pfx + "rp", v)
                p.inc(nc.sync.dma_start(out=dst_chunk_v[:, tb * 512:(tb + 1) * 512],
                                        in_=sout[pidx][:]), f"{pfx}so{pidx}", 16)

            for m in range(nkv):
                rope_chunk(m, swk, wwk, bk_t[:, m:m + 1], kf_v[m], nq + m)
            for m in range(nq):
                rope_chunk(m, swq, wwq, bq_t[:, m:m + 1], qf_v[m], m)

            nvb = (nkv * 128) // 512 if (nkv * 128) >= 512 else 1
            vw = nkv * 128
            for ts in range(4):
                pidx = ts % 2
                for nb in range(max(1, vw // 512)):
                    wlo = nb * 512
                    wwid2 = min(512, vw - wlo)
                    p.w(nc.tensor, pfx + "ve")
                    for k in range(KC):
                        ins = nc.tensor.matmul(pq[pidx][:, 0:wwid2],
                                               sxf[bx][:, k * 512 + ts * 128:k * 512 + (ts + 1) * 128],
                                               swv[:, k * vw + wlo:k * vw + wlo + wwid2],
                                               start=(k == 0), stop=(k == KC - 1))
                    mmv = p.inc(ins, pfx + "vm", 1)
                    p.w(nc.vector, pfx + "vm", mmv)
                    p.w(nc.vector, pfx + "vo0")
                    p.w(nc.vector, pfx + "vo1")
                    v = p.inc(nc.vector.tensor_tensor(svtt[pidx][:, 0:wwid2], pq[pidx][:, 0:wwid2],
                                                      bv_t[:, wlo:wlo + wwid2], ALU.add), pfx + "ve", 1)
                    p.w(nc.sync, pfx + "ve", v)
                    p.inc(nc.sync.dma_start(
                        out=vt[tb * 512 + ts * 128:tb * 512 + (ts + 1) * 128, wlo:wlo + wwid2],
                        in_=svtt[pidx][:, 0:wwid2]), f"{pfx}vo{pidx}", 16)
        p.barrier([pfx + "so0", pfx + "so1", pfx + "vo0", pfx + "vo1", pfx + "nm"])
        p.release([pfx + c for c in ("wld", "ld", "mm", "sc", "nq", "nm", "rt", "rp", "so", "vm", "ve", "vo")])


def emit_bounds(p, C, nq, nkv, rep, Tn, dst_bnd, nrm_slot0, pfx):
    nc = p.nc
    NB = Tn // 512
    snrm = C["snrm"]
    base = (nrm_slot0 + nq + nkv) * NB
    with ExitStack() as st:
        ptmp = p.psum(pfx + "p", [128, nq], F32, st)
        for h in range(nq):
            nc.vector.reduce_max(snrm[0:1, base + h:base + h + 1],
                                 snrm[0:1, (nrm_slot0 + h) * NB:(nrm_slot0 + h) * NB + NB], axis=AX.X)
        for m in range(nkv):
            nc.vector.reduce_max(snrm[0:1, base + nq + m:base + nq + m + 1],
                                 snrm[0:1, (nrm_slot0 + nq + m) * NB:(nrm_slot0 + nq + m) * NB + NB], axis=AX.X)
        v = p.inc(nc.vector.engine_nop(), pfx + "v", 1)
        p.w(nc.vector, pfx + "v", v)
        for h in range(nq):
            nc.vector.tensor_tensor(snrm[0:1, base + nq + nkv + h:base + nq + nkv + h + 1],
                                    snrm[0:1, base + h:base + h + 1],
                                    snrm[0:1, base + nq + h // rep:base + nq + h // rep + 1], ALU.mult)
        v = p.inc(nc.vector.engine_nop(), pfx + "v", 1)
        p.w(nc.scalar, pfx + "v", v)
        b2 = base + nq + nkv
        sv = p.inc(nc.scalar.activation(snrm[0:1, b2 + nq:b2 + 2 * nq], snrm[0:1, b2:b2 + nq], AF.Sqrt), pfx + "s", 1)
        p.w(nc.vector, pfx + "s", sv)
        v = p.inc(nc.vector.tensor_scalar_mul(snrm[0:1, b2:b2 + nq], snrm[0:1, b2 + nq:b2 + 2 * nq], -ISCALE), pfx + "v", 1)
        p.w(nc.tensor, pfx + "v", v)
        mv = p.inc(nc.tensor.matmul(ptmp[:], C["sonesr"][0:1, :], snrm[0:1, b2:b2 + nq], start=True, stop=True), pfx + "m", 1)
        p.w(nc.vector, pfx + "m", mv)
        p.inc(nc.vector.tensor_copy(dst_bnd[:], ptmp[:]), pfx + "v", 1)
        p.barrier([pfx + "v"])
        p.release([pfx])


def emit_attention(p, C, qf_v, kf_v, vt, of_v, Tn, nq, nkv, sbnd, pfx):
    nc = p.nc
    rep = nq // nkv
    NQC = Tn // 512
    with ExitStack() as st:
        sqf = p.sbuf(pfx + "q", [128, nq * 512], BF16, st)
        skb = [p.sbuf(f"{pfx}k{i}", [128, 512], BF16, st) for i in range(2)]
        svb = [p.sbuf(f"{pfx}v{i}", [128, 4 * 128], BF16, st) for i in range(2)]
        spp = [p.sbuf(f"{pfx}p{i}", [128, 512], BF16, st) for i in range(4)]
        ssum = p.sbuf(pfx + "sum", [1, 512], F32, st)
        srcp = p.sbuf(pfx + "rcp", [128, 512], F32, st)
        sof = p.sbuf(pfx + "of", [128, 2 * 512], BF16, st)
        ps = [p.psum(f"{pfx}ps{i}", [128, 512], F32, st) for i in range(2)]
        po = [p.psum(f"{pfx}po{i}", [128, 512], F32, st) for i in range(2)]
        psm = [p.psum(f"{pfx}pm{i}", [1, 512], F32, st) for i in range(2)]
        pbc = p.psum(pfx + "bc", [128, 512], F32, st)

        kv_hist = []
        for qc in range(NQC):
            p.w(nc.sync, pfx + "oo")
            for h in range(nq):
                p.inc(nc.sync.dma_start(out=sqf[:, h * 512:(h + 1) * 512],
                                        in_=qf_v[h, :, qc * 512:(qc + 1) * 512]), pfx + "ql", 16)
            qlv = p.val(pfx + "ql")
            for g in range(nq // 2):
                kvh = (2 * g) // rep
                for kb in range(qc + 1):
                    bz = len(kv_hist) % 2
                    AKL = f"{pfx}kl{bz}"
                    if len(kv_hist) >= 2:
                        p.w(nc.gpsimd, pfx + "sp", kv_hist[-2][0])
                        p.w(nc.gpsimd, pfx + "pv", kv_hist[-2][1])
                    p.inc(nc.gpsimd.dma_start(out=skb[bz][:],
                                              in_=kf_v[kvh, :, kb * 512:(kb + 1) * 512]), AKL, 16)
                    vt_v = vt[kb * 512:(kb + 1) * 512, kvh * 128:(kvh + 1) * 128].rearrange(
                        "(s p) q -> s p q", p=128)
                    for s4 in range(4):
                        p.inc(nc.gpsimd.dma_start(out=svb[bz][:, s4 * 128:(s4 + 1) * 128],
                                                  in_=vt_v[s4]), AKL, 16)
                    klv = p.val(AKL)
                    for hh in range(2):
                        h = 2 * g + hh
                        for ks in range(4):
                            pb = ks % 2
                            p.w(nc.tensor, AKL, klv)
                            if g == 0 and kb == 0 and hh == 0 and ks == 0:
                                p.w(nc.tensor, pfx + "ql", qlv)
                            p.w(nc.tensor, pfx + "ex")
                            if kb == 0 and ks == 0:
                                p.w(nc.tensor, pfx + "ev")
                            p.inc(nc.tensor.matmul(ps[pb][:], skb[bz][:, ks * 128:(ks + 1) * 128],
                                                   sqf[:, h * 512:(h + 1) * 512],
                                                   start=True, stop=True), pfx + "sp", 1)
                            smv = p.val(pfx + "sp")
                            p.w(nc.scalar, pfx + "sp", smv)
                            p.w(nc.scalar, pfx + "pv")
                            p.w(nc.scalar, pfx + "sm")
                            exv = p.inc(nc.scalar.activation(spp[ks][:], ps[pb][:], AF.Exp,
                                                             bias=sbnd[:, h:h + 1], scale=ISCALE), pfx + "ex", 1)
                            if kb == qc:
                                p.w(nc.vector, pfx + "ex", exv)
                                exv = p.inc(nc.vector.tensor_tensor(spp[ks][:], spp[ks][:],
                                                                    C["stri"][:, ks * 512:(ks + 1) * 512],
                                                                    ALU.mult), pfx + "ex", 1)
                            p.w(nc.tensor, pfx + "ex", exv)
                            first = (kb == 0 and ks == 0)
                            last = (kb == qc and ks == 3)
                            p.inc(nc.tensor.matmul(po[hh][:], svb[bz][:, ks * 128:(ks + 1) * 128],
                                                   spp[ks][:], start=first, stop=last), pfx + "pv", 1)
                            p.inc(nc.tensor.matmul(psm[hh][:], C["sonesc"][:, 0:1],
                                                   spp[ks][:], start=first, stop=last), pfx + "sm", 1)
                    kv_hist.append((p.val(pfx + "sp"), p.val(pfx + "pv")))
                for hh in range(2):
                    p.w(nc.vector, pfx + "pv")
                    p.w(nc.vector, pfx + "sm")
                    p.w(nc.vector, pfx + "bc")
                    v = p.inc(nc.vector.tensor_copy(ssum[0:1, :], psm[hh][0:1, :]), pfx + "ev", 1)
                    p.w(nc.vector, pfx + "ev", v)
                    v = p.inc(nc.vector.reciprocal(ssum[0:1, :], ssum[0:1, :]), pfx + "ev", 1)
                    p.w(nc.tensor, pfx + "ev", v)
                    bv = p.inc(nc.tensor.matmul(pbc[:], C["sonesr"][0:1, :], ssum[0:1, :],
                                                start=True, stop=True), pfx + "bc", 1)
                    p.w(nc.vector, pfx + "bc", bv)
                    v = p.inc(nc.vector.tensor_copy(srcp[:], pbc[:]), pfx + "ev", 1)
                    p.w(nc.vector, pfx + "ev", v)
                    p.w(nc.vector, pfx + "oo")
                    v = p.inc(nc.vector.tensor_tensor(sof[:, hh * 512:(hh + 1) * 512],
                                                      po[hh][:], srcp[:], ALU.mult), pfx + "ev", 1)
                evv = p.val(pfx + "ev")
                p.w(nc.sync, pfx + "ev", evv)
                for hh in range(2):
                    p.inc(nc.sync.dma_start(out=of_v[2 * g + hh, :, qc * 512:(qc + 1) * 512],
                                            in_=sof[:, hh * 512:(hh + 1) * 512]), pfx + "oo", 16)
        p.barrier([pfx + "oo"])
        p.release([pfx])


def emit_proj_tok(p, C, src_v, w_v, nin_c, bias_t, dst, Tn, nout, pfx, add_src=None):
    nc = p.nc
    NBL = nout // 512
    with ExitStack() as st:
        sw = p.sbuf(pfx + "w", [128, nin_c * nout], BF16, st)
        sob = [p.sbuf(f"{pfx}b{i}", [128, nin_c * 128], BF16, st) for i in range(2)]
        sso = [p.sbuf(f"{pfx}s{i}", [128, NBL * 512], F32, st) for i in range(2)]
        sad = [p.sbuf(f"{pfx}a{i}", [128, nout], F32, st) for i in range(2)]
        pso = [p.psum(f"{pfx}p{i}", [128, 512], F32, st) for i in range(4)]
        for k in range(nin_c):
            p.inc(nc.gpsimd.dma_start(out=sw[:, k * nout:(k + 1) * nout], in_=w_v[k]), pfx + "wl", 16)
        WOL = p.val(pfx + "wl")

        for i in range(Tn // 128):
            bi = i % 2
            LD, SO, AL = f"{pfx}ld{bi}", f"{pfx}so{bi}", f"{pfx}al{bi}"
            if i >= 2:
                p.w(nc.sync, pfx + "mm")
            for c in range(nin_c):
                p.inc(nc.sync.dma_start(out=sob[bi][:, c * 128:(c + 1) * 128],
                                        in_=src_v[c, :, i * 128:(i + 1) * 128]), LD, 16)
            olv = p.val(LD)
            if add_src is not None:
                if i >= 2:
                    p.w(nc.sync, pfx + "ev")
                alv = p.inc(nc.sync.dma_start(out=sad[bi][:], in_=add_src[i * 128:(i + 1) * 128, :]), AL, 16)
            p.w(nc.tensor, LD, olv)
            if i == 0:
                p.w(nc.tensor, pfx + "wl", WOL)
            for n in range(NBL):
                p.w(nc.tensor, pfx + "ev")
                for k in range(nin_c):
                    ins = nc.tensor.matmul(pso[n % 4][:], sob[bi][:, k * 128:(k + 1) * 128],
                                           sw[:, k * nout + n * 512:k * nout + (n + 1) * 512],
                                           start=(k == 0), stop=(k == nin_c - 1))
                p.inc(ins, pfx + "mm", 1)
            mmv = p.val(pfx + "mm")
            for n in range(NBL):
                p.w(nc.vector, pfx + "mm", mmv - NBL + 1 + n)
                if i >= 2:
                    p.w(nc.vector, SO)
                if bias_t is not None:
                    v = p.inc(nc.vector.tensor_tensor(sso[bi][:, n * 512:(n + 1) * 512],
                                                      pso[n % 4][:], bias_t[:, n * 512:(n + 1) * 512],
                                                      ALU.add), pfx + "ev", 1)
                elif add_src is not None:
                    p.w(nc.vector, AL, alv)
                    v = p.inc(nc.vector.tensor_tensor(sso[bi][:, n * 512:(n + 1) * 512],
                                                      pso[n % 4][:], sad[bi][:, n * 512:(n + 1) * 512],
                                                      ALU.add), pfx + "ev", 1)
                else:
                    v = p.inc(nc.vector.tensor_copy(sso[bi][:, n * 512:(n + 1) * 512], pso[n % 4][:]), pfx + "ev", 1)
                p.w(nc.sync, pfx + "ev", v)
                p.inc(nc.sync.dma_start(out=dst[i * 128:(i + 1) * 128, n * 512:(n + 1) * 512],
                                        in_=sso[bi][:, n * 512:(n + 1) * 512]), SO, 16)
        p.barrier([pfx + "so0", pfx + "so1"])
        p.release([pfx])




def load_common(p, io):
    nc = p.nc
    C = {}
    C["sident"] = p.sbuf("sident", [128, 128], BF16)
    C["sidentf"] = p.sbuf("sidentf", [128, 128], F32)
    C["stri"] = p.sbuf("stri", [128, 4 * 512], BF16)
    C["sonesc"] = p.sbuf("sonesc", [128, 1], BF16)
    C["sonesf"] = p.sbuf("sonesf", [128, 1], F32)
    C["sonesr"] = p.sbuf("sonesr", [1, 128], F32)
    C["spmat"] = p.sbuf("spmat", [128, 128], BF16)
    C["snrm"] = p.sbuf("snrm", [1, 512], F32)
    C["sconst"] = p.sbuf("sconst", [128, 8], F32)
    nc.const_aps.register(F32, 1e-5, C["sconst"][:, 0:1])
    nc.const_aps.register(F32, 1e-6, C["sconst"][:, 1:2])
    nc.const_aps.register(F32, 0.5, C["sconst"][:, 2:3])
    nc.const_aps.register(F32, 1.5, C["sconst"][:, 3:4])
    nc.const_aps.register(F32, -0.5, C["sconst"][:, 4:5])
    for nm in ("ident", "identf", "tri", "ones_col", "ones_colf", "ones_row", "pmat", "consts"):
        key = {"ident": "sident", "identf": "sidentf", "tri": "stri",
               "ones_col": "sonesc", "ones_colf": "sonesf", "ones_row": "sonesr",
               "pmat": "spmat", "consts": "sconst"}[nm]
        p.inc(nc.sync.dma_start(out=C[key][:], in_=io[nm]), "pre", 16)
    return C


def emit_ab(p, C, io):
    nc = p.nc
    xnf_v = io["xnf"].rearrange("(c p) t -> c p t", p=128)
    qf_v = io["qf"].rearrange("(c p) t -> c p t", p=128)
    kf_v = io["kf"].rearrange("(c p) t -> c p t", p=128)
    of_v = io["of"].rearrange("(c p) t -> c p t", p=128)

    with ExitStack() as stA:
        swrt = p.sbuf("swrt", [128, D], F32, stA)
        sbq = p.sbuf("sbq", [128, 16], F32, stA)
        sbk = p.sbuf("sbk", [128, 4], F32, stA)
        sbv = p.sbuf("sbv", [128, 512], F32, stA)
        sbo = p.sbuf("sbo", [128, D], F32, stA)
        sbnd = p.sbuf("sbnd", [128, 16], F32, stA)
        for nm, dst in (("wrt", swrt), ("bq", sbq), ("bk", sbk), ("bv", sbv), ("bo", sbo)):
            p.inc(nc.sync.dma_start(out=dst[:], in_=io[nm]), "preab", 16)
        p.barrier(["preab"])

        emit_ln(p, io["xb"], io["xn"], io["sc"], swrt, T, "s1")
        if "ccsc_hook" in io:
            io["ccsc_hook"]()
        emit_tokmaj_to_fm(p, io["xn"], xnf_v, C, T, "tp")
        emit_qkv_rope(p, C, xnf_v, qf_v, kf_v, io["vt"], T, 16, 4,
                      sbq, sbk, sbv,
                      io["wq"].rearrange("(kc p) m -> kc p m", p=128),
                      io["wk"].rearrange("(kc p) m -> kc p m", p=128),
                      io["wv"].rearrange("(kc p) m -> kc p m", p=128),
                      io["cosd"], io["sind"], 0, "pj")
        emit_bounds(p, C, 16, 4, 4, T, sbnd, 0, "bd")
        emit_attention(p, C, qf_v, kf_v, io["vt"], of_v, T, 16, 4, sbnd, "at")
        emit_proj_tok(p, C, of_v, io["wo"].rearrange("(kc p) m -> kc p m", p=128),
                      KC, sbo, io["s0"], T, D, "op")
        p.release(["preab"])






def emit_topk(p, C, io, sc_batches, idxsel, pfx="tk"):
    nc = p.nc
    with ExitStack() as st:
        ssc = p.sbuf(pfx + "sc", [128, 128], F32, st)
        sE4 = p.sbuf(pfx + "e4", [4, 128], F32, st)
        sE4T = p.sbuf(pfx + "e4t", [128, 4], F32, st)
        slh = p.sbuf(pfx + "lh", [4, 8], F32, st)
        smid = p.sbuf(pfx + "mid", [128, 1], F32, st)
        scmp = p.sbuf(pfx + "cmp", [128, 128], F32, st)
        scnt = p.sbuf(pfx + "cnt", [128, 1], F32, st)
        sone = p.sbuf(pfx + "one", [128, 128], F32, st)
        spfx = p.sbuf(pfx + "pf", [128, 128], F32, st)
        srow = p.sbuf(pfx + "row", [1, 128], F32, st)
        srow2 = p.sbuf(pfx + "row2", [1, 128], F32, st)
        soff = p.sbuf(pfx + "off", [128, 1], F32, st)
        sit = p.sbuf(pfx + "it", [128, 128], F32, st)
        spk = p.sbuf(pfx + "pk", [128, 512], F32, st)
        sdst = p.sbuf(pfx + "dst", [128, 128], F32, st)
        sdstu = p.sbuf(pfx + "dstu", [128, 128], U32, st)
        sboff = p.sbuf(pfx + "boff", [128, 1], F32, st)
        sboff2 = p.sbuf(pfx + "boff2", [128, 1], F32, st)
        smsk = p.sbuf(pfx + "msk", [4, 2], I32, st)
        scmpi = p.sbuf(pfx + "cmi", [128, 128], I32, st)
        pcnt = p.psum(pfx + "pc", [4, 1], F32, st)
        pbc = p.psum(pfx + "pb", [128, 1], F32, st)
        tmpd = p.dram(pfx + "tmpd", [128], F32)

        VQ = pfx + "v"
        for b in range(4):
            p.inc(nc.sync.dma_start(out=ssc[32 * b:32 * (b + 1), :],
                                    in_=sc_batches[b]), pfx + "ld", 16)
        p.inc(nc.sync.dma_start(out=sE4[:], in_=io["e4"]), pfx + "ld", 16)
        p.inc(nc.sync.dma_start(out=sE4T[:], in_=io["e4t"]), pfx + "ld", 16)
        p.inc(nc.sync.dma_start(out=sboff[:], in_=io["boff512"]), pfx + "ld", 16)
        p.inc(nc.sync.dma_start(out=sboff2[:], in_=io["boff4096"]), pfx + "ld", 16)
        LDV = p.val(pfx + "ld")
        p.w(nc.vector, pfx + "ld", LDV)
        nc.vector.memset(slh[:], 0.0)
        nc.vector.memset(slh[:, 0:1], -20.0)
        nc.vector.memset(slh[:, 1:2], 20.0)
        nc.vector.memset(slh[:, 2:3], 0.0)
        v = p.inc(nc.vector.memset(sone[:], 1.0), VQ, 1)
        p.w(nc.gpsimd, VQ, v)
        gv = p.inc(nc.gpsimd.iota(sit[:], pattern=[[1, 128]], base=0, channel_multiplier=128,
                                  allow_small_or_imprecise_dtypes=True), pfx + "g", 1)

        def count_pass(mid_col):
            p.w(nc.tensor, VQ)
            p.w(nc.tensor, pfx + "cp")
            bv = p.inc(nc.tensor.matmul(pbc[:], sE4[:, :], mid_col, start=True, stop=True), pfx + "bm", 1)
            p.w(nc.vector, pfx + "bm", bv)
            v = p.inc(nc.vector.tensor_copy(smid[:], pbc[:]), pfx + "cp", 1)
            p.w(nc.vector, pfx + "cp", v)
            v = p.inc(nc.vector.tensor_scalar(scmp[:], ssc[:], smid[:, 0:1], None, ALU.is_gt), VQ, 1)
            p.w(nc.vector, VQ, v)
            v = p.inc(nc.vector.reduce_sum(scnt[:], scmp[:], axis=AX.X), VQ, 1)
            p.w(nc.tensor, VQ, v)
            p.w(nc.tensor, pfx + "ct")
            cv = p.inc(nc.tensor.matmul(pcnt[:], sE4T[:], scnt[:, 0:1], start=True, stop=True), pfx + "cm", 1)
            p.w(nc.vector, pfx + "cm", cv)
            v = p.inc(nc.vector.tensor_copy(slh[:, 3:4], pcnt[:]), pfx + "ct", 1)
            p.w(nc.vector, pfx + "ct", v)
            return v

        for it in range(34):
            count_pass(slh[:, 2:3])
            v = p.inc(nc.vector.tensor_scalar(smsk[:, 0:1], slh[:, 3:4], 512.0, None, ALU.is_ge), VQ, 1)
            p.w(nc.vector, VQ, v)
            v = p.inc(nc.vector.tensor_scalar(smsk[:, 1:2], slh[:, 3:4], 512.0, None, ALU.is_lt), VQ, 1)
            p.w(nc.vector, VQ, v)
            v = p.inc(nc.vector.select(slh[:, 0:1], smsk[:, 0:1], slh[:, 2:3], slh[:, 0:1]), VQ, 1)
            p.w(nc.vector, VQ, v)
            v = p.inc(nc.vector.select(slh[:, 1:2], smsk[:, 1:2], slh[:, 2:3], slh[:, 1:2]), VQ, 1)
            p.w(nc.vector, VQ, v)
            v = p.inc(nc.vector.tensor_tensor(slh[:, 2:3], slh[:, 0:1], slh[:, 1:2], ALU.add), VQ, 1)
            p.w(nc.vector, VQ, v)
            v = p.inc(nc.vector.tensor_scalar_mul(slh[:, 2:3], slh[:, 2:3], 0.5), VQ, 1)
        count_pass(slh[:, 0:1])
        v = p.inc(nc.vector.tensor_scalar(smsk[:, 0:1], slh[:, 3:4], 512.0, None, ALU.is_equal), VQ, 1)
        p.w(nc.vector, VQ, v)
        v = p.inc(nc.vector.select(slh[:, 4:5], smsk[:, 0:1], slh[:, 0:1], slh[:, 1:2]), VQ, 1)
        p.w(nc.tensor, VQ, v)
        p.w(nc.tensor, pfx + "cp")
        bv = p.inc(nc.tensor.matmul(pbc[:], sE4[:, :], slh[:, 4:5], start=True, stop=True), pfx + "bm", 1)
        p.w(nc.vector, pfx + "bm", bv)
        v = p.inc(nc.vector.tensor_copy(smid[:], pbc[:]), pfx + "cp", 1)
        p.w(nc.vector, pfx + "cp", v)
        v = p.inc(nc.vector.tensor_scalar(scmp[:], ssc[:], smid[:, 0:1], None, ALU.is_gt), VQ, 1)
        p.w(nc.vector, VQ, v)
        v = p.inc(nc.vector.tensor_tensor_scan(spfx[:], sone[:], scmp[:], 0.0, ALU.mult, ALU.add), VQ, 1)
        p.w(nc.sync, VQ, v)
        dv = p.inc(nc.sync.dma_start(out=tmpd, in_=spfx[:, 127:128]), pfx + "d", 16)
        p.w(nc.sync, pfx + "d", dv)
        dv = p.inc(nc.sync.dma_start(out=srow[0:1, :], in_=tmpd.rearrange("(a b) -> a b", a=1)), pfx + "d", 16)
        p.w(nc.vector, pfx + "d", dv)
        vv = 0
        for b in range(4):
            vv = p.inc(nc.vector.tensor_tensor_scan(srow2[0:1, 32 * b:32 * (b + 1)],
                                                    sone[0:1, 0:32], srow[0:1, 32 * b:32 * (b + 1)],
                                                    0.0, ALU.mult, ALU.add), VQ, 1)
        p.w(nc.vector, VQ, vv)
        v = p.inc(nc.vector.tensor_tensor(srow2[0:1, :], srow2[0:1, :], srow[0:1, :], ALU.subtract), VQ, 1)
        p.w(nc.sync, VQ, v)
        dv = p.inc(nc.sync.dma_start(out=tmpd, in_=srow2[0:1, :]), pfx + "d", 16)
        p.w(nc.sync, pfx + "d", dv)
        dv = p.inc(nc.sync.dma_start(out=soff[:], in_=tmpd.rearrange("(a b) -> a b", b=1)), pfx + "d", 16)
        p.w(nc.vector, pfx + "d", dv)
        v = p.inc(nc.vector.scalar_tensor_tensor(spfx[:], spfx[:], soff[:, 0:1], scmp[:],
                                                 ALU.add, ALU.subtract), VQ, 1)
        p.w(nc.vector, VQ, v)
        v = p.inc(nc.vector.tensor_scalar(spfx[:], spfx[:], sboff[:, 0:1], None, ALU.add), VQ, 1)
        p.w(nc.vector, VQ, v)
        v = p.inc(nc.vector.memset(sdst[:], 1.0e9), VQ, 1)
        p.w(nc.vector, VQ, v)
        v = p.inc(nc.vector.tensor_copy(scmpi[:], scmp[:]), VQ, 1)
        p.w(nc.vector, VQ, v)
        v = p.inc(nc.vector.select(sdst[:], scmpi[:], spfx[:], sdst[:]), VQ, 1)
        p.w(nc.vector, VQ, v)
        v = p.inc(nc.vector.tensor_copy(sdstu[:], sdst[:]), VQ, 1)
        spk_v = spk[:].rearrange("p (q f) -> p q f", f=4)
        nc.vector.memset(spk[:], 0.0)
        nc.vector.tensor_copy(spk_v[:, :, 0:1], ssc[:].rearrange("p (q o) -> p q o", o=1))
        p.w(nc.vector, pfx + "g", gv)
        nc.vector.tensor_copy(spk_v[:, :, 1:2], sit[:].rearrange("p (q o) -> p q o", o=1))
        v = p.inc(nc.vector.tensor_scalar(scnt[:].tensor[:, 0:1] if False else scmp[:], sit[:], sboff2[:, 0:1], None, ALU.subtract), VQ, 1)
        p.w(nc.vector, VQ, v)
        v = p.inc(nc.vector.tensor_copy(spk_v[:, :, 2:3], scmp[:].rearrange("p (q o) -> p q o", o=1)), VQ, 1)
        if "tkdbg" in io:
            p.w(nc.sync, VQ, v)
            p.inc(nc.sync.dma_start(out=io["tkdbg"][:, 0:128], in_=scmp[:]), pfx + "dbg", 16)
            p.inc(nc.sync.dma_start(out=io["tkdbg"][:, 128:256], in_=spfx[:]), pfx + "dbg", 16)
            p.inc(nc.sync.dma_start(out=io["tkdbg"][:, 256:384], in_=sdst[:]), pfx + "dbg", 16)
            p.inc(nc.sync.dma_start(out=io["tkdbg"][0:4, 384:392], in_=slh[:]), pfx + "dbg", 16)
            p.inc(nc.sync.dma_start(out=io["tkdbg"][:, 392:393], in_=soff[:, 0:1], allow_slow_non_contiguous=True), pfx + "dbg", 16)
            p.inc(nc.sync.dma_start(out=io["tkdbg"][:, 400:528], in_=sit[:]), pfx + "dbg", 16)
            p.inc(nc.sync.dma_start(out=io["tkdbg"][:, 528:656], in_=ssc[:]), pfx + "dbg", 16)
        p.w(nc.gpsimd, VQ, v)
        bcreg = nc.gpsimd.alloc_register(pfx + "bc")
        nc.gpsimd.reg_mov(bcreg, 2047)
        for q in range(128):
            p.inc(nc.gpsimd.indirect_dma_start(
                out=idxsel, out_offset=bass.IndirectOffsetOnAxis(sdstu[:, q:q + 1], 0),
                in_=spk[:, 4 * q:4 * q + 4], in_offset=None,
                bounds_check=bcreg, oob_is_err=False), pfx + "sca", 16)
        nc.gpsimd.free_register(bcreg)
        p.barrier([pfx + "sca"])
        p.release([pfx])


def emit_gather_rows(p, idxsel, col, src_dram, dst_dram, nrows, width, dt, pfx):
    nc = p.nc
    with ExitStack() as st:
        sidxf = p.sbuf(pfx + "if", [128, 4], F32, st)
        sidxu = p.sbuf(pfx + "iu", [128, 1], U32, st)
        sg = [p.sbuf(f"{pfx}g{i}", [128, width], dt, st) for i in range(2)]
        for rc in range(nrows // 128):
            bi = rc % 2
            GL, GO = f"{pfx}l{bi}", f"{pfx}o{bi}"
            p.w(nc.sync, pfx + "cv")
            dv = p.inc(nc.sync.dma_start(out=sidxf[:], in_=idxsel[rc * 128:(rc + 1) * 128, :]), pfx + "ix", 16)
            p.w(nc.vector, pfx + "ix", dv)
            v = p.inc(nc.vector.tensor_copy(sidxu[:], sidxf[:, col:col + 1]), pfx + "cv", 1)
            p.w(nc.gpsimd, pfx + "cv", v)
            p.w(nc.gpsimd, GO)
            gv = p.inc(nc.gpsimd.indirect_dma_start(
                out=sg[bi][:], out_offset=None,
                in_=src_dram, in_offset=bass.IndirectOffsetOnAxis(sidxu[:, 0:1], 0)), GL, 16)
            p.w(nc.sync, GL, gv)
            p.inc(nc.sync.dma_start(out=dst_dram[rc * 128:(rc + 1) * 128, :], in_=sg[bi][:]), GO, 16)
        p.barrier([pfx + "o0", pfx + "o1"])
        p.release([pfx])


def emit_gather_rows_u32(p, idx_dram, src_dram, dst_dram, nrows, width, dt, pfx):
    nc = p.nc
    with ExitStack() as st:
        sidxu = p.sbuf(pfx + "iu", [128, 1], U32, st)
        sg = [p.sbuf(f"{pfx}g{i}", [128, width], dt, st) for i in range(2)]
        for rc in range(nrows // 128):
            bi = rc % 2
            GL, GO = f"{pfx}l{bi}", f"{pfx}o{bi}"
            p.w(nc.gpsimd, pfx + "ix")
            dv = p.inc(nc.sync.dma_start(out=sidxu[:], in_=idx_dram[rc * 128:(rc + 1) * 128, :]), pfx + "ix", 16)
            p.w(nc.gpsimd, pfx + "ix", dv)
            p.w(nc.gpsimd, GO)
            gv = p.inc(nc.gpsimd.indirect_dma_start(
                out=sg[bi][:], out_offset=None,
                in_=src_dram, in_offset=bass.IndirectOffsetOnAxis(sidxu[:, 0:1], 0)), GL, 16)
            p.w(nc.sync, GL, gv)
            p.inc(nc.sync.dma_start(out=dst_dram[rc * 128:(rc + 1) * 128, :], in_=sg[bi][:]), GO, 16)
        p.barrier([pfx + "o0", pfx + "o1"])
        p.release([pfx])


def emit_mlp(p, C, io, h2f_v):
    nc = p.nc
    I = 8192
    IC = I // 128
    with ExitStack() as stM:
      shh = p.sbuf("mgh", [128, IC * 256], BF16, stM)
      with ExitStack() as stA:
        sh2 = p.sbuf("mgx", [128, KC * 256], BF16, stA)
        sgt = p.sbuf("mgg", [128, 256], F32, stA)
        swgu = [p.sbuf(f"mgw{i}", [128, 2 * 2048], BF16, stA) for i in range(2)]
        pg = [p.psum(f"mgp{i}", [128, 256], F32, stA) for i in range(2)]
        for k in range(KC):
            p.inc(nc.sync.dma_start(out=sh2[:, k * 256:(k + 1) * 256], in_=h2f_v[k, :, :]), "mgxl", 16)
        XLV = p.val("mgxl")
        mm_hist = []
        for m in range(IC):
            bz = m % 2
            GWL = f"mgwl{bz}"
            if m >= 2:
                p.w(nc.gpsimd, "mgmm", mm_hist[m - 2])
            p.inc(nc.gpsimd.dma_start(out=swgu[bz][:, 0:2048], in_=io["wgate_r"][m]), GWL, 16)
            p.inc(nc.gpsimd.dma_start(out=swgu[bz][:, 2048:4096], in_=io["wup_r"][m]), GWL, 16)
            wlv = p.val(GWL)
            p.w(nc.tensor, GWL, wlv)
            if m == 0:
                p.w(nc.tensor, "mgxl", XLV)
            p.w(nc.tensor, "mgsl")
            for k in range(KC):
                ins = nc.tensor.matmul(pg[0][:], swgu[bz][:, k * 128:(k + 1) * 128],
                                       sh2[:, k * 256:(k + 1) * 256],
                                       start=(k == 0), stop=(k == KC - 1))
            p.inc(ins, "mgmm", 1)
            p.w(nc.tensor, "mghh")
            for k in range(KC):
                ins = nc.tensor.matmul(pg[1][:], swgu[bz][:, 2048 + k * 128:2048 + (k + 1) * 128],
                                       sh2[:, k * 256:(k + 1) * 256],
                                       start=(k == 0), stop=(k == KC - 1))
            mmv = p.inc(ins, "mgmm", 1)
            mm_hist.append(mmv)
            p.w(nc.scalar, "mgmm", mmv - 1)
            p.w(nc.scalar, "mghh")
            sv = p.inc(nc.scalar.activation(sgt[:], pg[0][:], AF.Silu), "mgsl", 1)
            p.w(nc.vector, "mgsl", sv)
            p.w(nc.vector, "mgmm", mmv)
            v = p.inc(nc.vector.tensor_tensor(shh[:, m * 256:(m + 1) * 256], sgt[:], pg[1][:],
                                              ALU.mult), "mghh", 1)
        p.barrier(["mghh"])
      with ExitStack() as stB:
        swd = [p.sbuf(f"mgd{i}", [128, 2048], BF16, stB) for i in range(2)]
        sx1 = p.sbuf("mga", [128, D], F32, stB)
        sso = p.sbuf("mgo", [128, D], F32, stB)
        pd = [p.psum(f"mgq{i}", [128, 512], F32, stB) for i in range(8)]
        wd_v = io["wdown"].rearrange("(kc p) m -> kc p m", p=128)
        dm_hist = []
        for m in range(IC):
            bz = m % 2
            DWL = f"mgdl{bz}"
            if m >= 2:
                p.w(nc.gpsimd, "mgdm", dm_hist[m - 2])
            p.inc(nc.gpsimd.dma_start(out=swd[bz][:], in_=wd_v[m]), DWL, 16)
            wlv = p.val(DWL)
            p.w(nc.tensor, DWL, wlv)
            if m == 0:
                p.w(nc.tensor, "mgde")
            for ts2 in range(2):
                for n in range(4):
                    ins = nc.tensor.matmul(pd[ts2 * 4 + n][:], shh[:, m * 256 + ts2 * 128:m * 256 + (ts2 + 1) * 128],
                                           swd[bz][:, n * 512:(n + 1) * 512],
                                           start=(m == 0), stop=(m == IC - 1))
            dmv = p.inc(ins, "mgdm", 1)
            dm_hist.append(dmv)
        DMV = p.val("mgdm")
        for ts2 in range(2):
            ALD = f"mgal{ts2}"
            alv = p.inc(nc.sync.dma_start(out=sx1[:], in_=io["x1own"][ts2 * 128:(ts2 + 1) * 128, :]), ALD, 16)
            for n in range(4):
                p.w(nc.vector, "mgdm", DMV)
                p.w(nc.vector, ALD, alv)
                if n == 0 and ts2 == 1:
                    p.w(nc.vector, "mgoo")
                v = p.inc(nc.vector.tensor_tensor(sso[:, n * 512:(n + 1) * 512], pd[ts2 * 4 + n][:],
                                                  sx1[:, n * 512:(n + 1) * 512], ALU.add), "mgde", 1)
            p.w(nc.sync, "mgde", p.val("mgde"))
            p.inc(nc.sync.dma_start(out=io["procown"][ts2 * 128:(ts2 + 1) * 128, :], in_=sso[:]), "mgoo", 16)
            p.w(nc.sync, "mgde", p.val("mgde"))
        p.barrier(["mgoo"])
        p.release(["mg"])


def emit_gather_own(p, C, io):
    nc = p.nc
    with ExitStack() as st:
        sidxu = p.sbuf("goi", [128, 1], U32, st)
        sg = p.sbuf("gog", [128, D], F32, st)
        sg2 = p.sbuf("gos", [128, 4], F32, st)
        for rc in range(2):
            p.w(nc.sync, "goo")
            dv = p.inc(nc.sync.dma_start(out=sidxu[:], in_=io["ownsel"][rc * 128:(rc + 1) * 128, :]), "goix", 16)
            for (srcd, dstd, wid, stile) in ((io["x1sel"], io["x1own"], D, sg),
                                             (io["selxn"], io["selxo"], D, sg),
                                             (io["idxsel"], io["idxo"], 4, sg2)):
                p.w(nc.gpsimd, "goix", dv)
                p.w(nc.gpsimd, "goo")
                gv = p.inc(nc.gpsimd.indirect_dma_start(
                    out=stile[:, 0:wid], out_offset=None,
                    in_=srcd, in_offset=bass.IndirectOffsetOnAxis(sidxu[:, 0:1], 0)), "gol", 16)
                p.w(nc.sync, "gol", gv)
                p.inc(nc.sync.dma_start(out=dstd[rc * 128:(rc + 1) * 128, :], in_=stile[:, 0:wid]), "goo", 16)
        p.barrier(["goo"])
        p.release(["go"])


def emit_gating(p, C, io):
    nc = p.nc
    with ExitStack() as st:
        spr = p.sbuf("ggp", [128, D], F32, st)
        ssl = p.sbuf("ggs", [128, D], F32, st)
        six = p.sbuf("ggi", [128, 4], F32, st)
        sgg = p.sbuf("ggg", [128, 1], F32, st)
        for rc in range(2):
            p.w(nc.sync, "ggo")
            dv = p.inc(nc.sync.dma_start(out=spr[:], in_=io["procown"][rc * 128:(rc + 1) * 128, :]), "ggl", 16)
            dv = p.inc(nc.sync.dma_start(out=ssl[:], in_=io["selxo"][rc * 128:(rc + 1) * 128, :]), "ggl", 16)
            dv = p.inc(nc.sync.dma_start(out=six[:], in_=io["idxo"][rc * 128:(rc + 1) * 128, :]), "ggl", 16)
            p.w(nc.scalar, "ggl", dv)
            sv = p.inc(nc.scalar.activation(sgg[:], six[:, 0:1], AF.Sigmoid), "ggsg", 1)
            p.w(nc.vector, "ggl", dv)
            p.w(nc.vector, "ggsg", sv)
            v = p.inc(nc.vector.tensor_tensor(spr[:], spr[:], ssl[:], ALU.subtract), "ggv", 1)
            p.w(nc.vector, "ggv", v)
            v = p.inc(nc.vector.tensor_scalar(spr[:], spr[:], sgg[:, 0:1], None, ALU.mult), "ggv", 1)
            p.w(nc.vector, "ggv", v)
            v = p.inc(nc.vector.tensor_tensor(spr[:], spr[:], ssl[:], ALU.add), "ggv", 1)
            p.w(nc.sync, "ggv", v)
            p.inc(nc.sync.dma_start(out=io["gatedo"][rc * 128:(rc + 1) * 128, :], in_=spr[:]), "ggo", 16)
        p.barrier(["ggo"])
        p.release(["gg"])


def emit_mod(p, C, io):
    nc = p.nc
    sbq2 = p.sbuf("sbq2", [128, 16], F32)
    sbk2 = p.sbuf("sbk2", [128, 8], F32)
    sbv2 = p.sbuf("sbv2", [128, 1024], F32)
    sbnd2 = p.sbuf("sbnd2", [128, 16], F32)
    for nm, dst in (("bq2", sbq2), ("bk2", sbk2), ("bv2", sbv2)):
        p.inc(nc.sync.dma_start(out=dst[:], in_=io[nm]), "pre2", 16)
    p.barrier(["pre2"])

    sc_b = io["scall_aps"]
    emit_topk(p, C, io, sc_b, io["idxsel"], "tk")
    emit_gather_rows(p, io["idxsel"], 1, io["x0flat"], io["selx"], S_SEL, D, F32, "gx")
    emit_gather_rows(p, io["idxsel"], 2, io["cosM"], io["costk"], S_SEL, 128, F32, "gc")
    emit_gather_rows(p, io["idxsel"], 2, io["sinM"], io["sintk"], S_SEL, 128, F32, "gs")
    emit_ln(p, io["selx"], io["selxn"], None, None, S_SEL, "c1")
    with ExitStack() as stw:
        sln1w = p.sbuf("sln1w", [128, D], F32, stw)
        v = p.inc(nc.sync.dma_start(out=sln1w[:], in_=io["ln1w"]), "pre3", 16)
        p.w(nc.scalar, "pre3", v)
        p.w(nc.vector, "pre3", v)
        emit_rms(p, io["selxn"], io["hseltok"], sln1w, 1e-6, S_SEL, "c2")
    hself_v = io["hselfm"].rearrange("(c p) t -> c p t", p=128)
    cosf_v = io["cosf2"].rearrange("(c p) t -> c p t", p=128)
    sinf_v = io["sinf2"].rearrange("(c p) t -> c p t", p=128)
    emit_tokmaj_to_fm(p, io["hseltok"], hself_v, C, S_SEL, "c4")
    emit_tokmaj_to_fm(p, io["costk"], cosf_v, C, S_SEL, "c5", nchunk=1)
    emit_tokmaj_to_fm(p, io["sintk"], sinf_v, C, S_SEL, "c6", nchunk=1, dt=F32)
    qf2_v = io["qf2"].rearrange("(c p) t -> c p t", p=128)
    kf2_v = io["kf2"].rearrange("(c p) t -> c p t", p=128)
    of2_v = io["of2"].rearrange("(c p) t -> c p t", p=128)
    emit_qkv_rope(p, C, hself_v, qf2_v, kf2_v, io["vt2"], S_SEL, 16, 8,
                  sbq2, sbk2, sbv2,
                  io["wq2"].rearrange("(kc p) m -> kc p m", p=128),
                  io["wk2"].rearrange("(kc p) m -> kc p m", p=128),
                  io["wv2"].rearrange("(kc p) m -> kc p m", p=128),
                  io["cosf2"], io["sinf2"], 0, "mj")
    emit_bounds(p, C, 16, 8, 2, S_SEL, sbnd2, 0, "me")
    emit_attention(p, C, qf2_v, kf2_v, io["vt2"], of2_v, S_SEL, 16, 8, sbnd2, "ma")
    emit_proj_tok(p, C, of2_v, io["wo2"].rearrange("(kc p) m -> kc p m", p=128),
                  KC, None, io["x1sel"], S_SEL, D, "mo", add_src=io["selxn"])
    emit_gather_own(p, C, io)
    with ExitStack() as stw:
        sln2w = p.sbuf("sln2w", [128, D], F32, stw)
        v = p.inc(nc.sync.dma_start(out=sln2w[:], in_=io["ln2w"]), "pre4", 16)
        p.w(nc.scalar, "pre4", v)
        p.w(nc.vector, "pre4", v)
        emit_rms(p, io["x1own"], io["h2own"], sln2w, 1e-6, 256, "c7")
    h2f_v = io["h2f"].rearrange("(c p) t -> c p t", p=128)
    emit_tokmaj_to_fm(p, io["h2own"], h2f_v, C, 256, "c8")
    emit_mlp(p, C, io, h2f_v)
    emit_gating(p, C, io)
    p.release(["pre2", "pre3", "pre4"])




def emit_combine(p, C, io, copy_first=True):
    nc = p.nc
    if copy_first:
        with ExitStack() as st:
            scp = [p.sbuf(f"cbc{i}", [128, D], F32, st) for i in range(2)]
            for i in range(T // 128):
                bi = i % 2
                CP, CO = f"cbp{bi}", f"cbo{bi}"
                if i >= 2:
                    p.w(nc.sync, CO)
                dv = p.inc(nc.sync.dma_start(out=scp[bi][:], in_=io["xn"][i * 128:(i + 1) * 128, :]), CP, 16)
                p.w(nc.sync, CP, dv)
                p.inc(nc.sync.dma_start(out=io["modw"][i * 128:(i + 1) * 128, :], in_=scp[bi][:]), CO, 16)
            p.barrier(["cbo0", "cbo1"])
            p.release(["cbp", "cbo"])
    with ExitStack() as st:
        sgat = [p.sbuf(f"cbg{i}", [128, D], F32, st) for i in range(2)]
        sif = p.sbuf("cbi", [128, 4], F32, st)
        sdf = p.sbuf("cbd", [128, 1], F32, st)
        sm = p.sbuf("cbm", [128, 1], F32, st)
        sdu = [p.sbuf(f"cbu{i}", [128, 1], U32, st) for i in range(2)]
        sbofft = p.sbuf("cbb2", [128, 1], F32, st)
        p.inc(nc.sync.dma_start(out=sbofft[:], in_=io["bofftok"]), "cbl2", 16)
        BL2 = p.val("cbl2")
        bcreg = nc.gpsimd.alloc_register("cbbc")
        nc.gpsimd.reg_mov(bcreg, T - 1)
        for rc in range(S_SEL // 128):
            bi = rc % 2
            GL, SCT = f"cbgl{bi}", f"cbs{bi}"
            if rc >= 2:
                p.w(nc.sync, SCT)
            dv = p.inc(nc.sync.dma_start(out=sgat[bi][:], in_=io["gated"][rc * 128:(rc + 1) * 128, :]), GL, 16)
            p.w(nc.sync, "cbcv")
            dv2 = p.inc(nc.sync.dma_start(out=sif[:], in_=io["idxsel"][rc * 128:(rc + 1) * 128, :]), GL, 16)
            p.w(nc.vector, GL, p.val(GL))
            if rc == 0:
                p.w(nc.vector, "cbl2", BL2)
            v = p.inc(nc.vector.tensor_scalar(sdf[:], sif[:, 1:2], sbofft[:, 0:1], None, ALU.subtract), "cbv", 1)
            p.w(nc.vector, "cbv", v)
            v = p.inc(nc.vector.tensor_scalar(sm[:], sdf[:], 0.0, None, ALU.is_ge), "cbv", 1)
            p.w(nc.vector, "cbv", v)
            v = p.inc(nc.vector.select(sdf[:], sm[:], sdf[:], sm[:]) if False else
                      nc.vector.scalar_tensor_tensor(sdf[:], sm[:], 1.0e9, sdf[:], ALU.bitwise_not if False else ALU.subtract, ALU.mult) if False else
                      nc.vector.engine_nop(), "cbv", 1)
            p.w(nc.vector, "cbv", v)
            v = p.inc(nc.vector.tensor_scalar(sm[:], sm[:], -1.0, 1.0, ALU.mult, ALU.add), "cbv", 1)
            p.w(nc.vector, "cbv", v)
            v = p.inc(nc.vector.tensor_scalar(sm[:], sm[:], 1.0e9, None, ALU.mult), "cbv", 1)
            p.w(nc.vector, "cbv", v)
            v = p.inc(nc.vector.tensor_tensor(sdf[:], sdf[:], sm[:], ALU.add), "cbv", 1)
            p.w(nc.vector, "cbv", v)
            v = p.inc(nc.vector.tensor_copy(sdu[bi][:], sdf[:]), "cbcv", 1)
            p.w(nc.gpsimd, "cbcv", v)
            p.w(nc.gpsimd, GL, p.val(GL))
            p.inc(nc.gpsimd.indirect_dma_start(
                out=io["modw"], out_offset=bass.IndirectOffsetOnAxis(sdu[bi][:, 0:1], 0),
                in_=sgat[bi][:], in_offset=None,
                bounds_check=bcreg, oob_is_err=False), SCT, 16)
        nc.gpsimd.free_register(bcreg)
        p.barrier(["cbs0", "cbs1"])
        p.release(["cbg", "cbv", "cbcv", "cbs", "cbi", "cbl2", "cbd", "cbu"])
    with ExitStack() as st:
        sx0 = [p.sbuf(f"cbx{i}", [128, D], F32, st) for i in range(2)]
        ss0 = [p.sbuf(f"cbs0_{i}", [128, D], F32, st) for i in range(2)]
        smd = [p.sbuf(f"cbmd{i}", [128, D], F32, st) for i in range(2)]
        sg0 = p.sbuf("cbgsil", [128, D], F32, st)
        for i in range(T // 128):
            bi = i % 2
            LD, SO = f"cbfl{bi}", f"cbfo{bi}"
            if i >= 2:
                p.w(nc.sync, SO)
            p.inc(nc.sync.dma_start(out=sx0[bi][:], in_=io["x0b"][i * 128:(i + 1) * 128, :]), LD, 16)
            p.inc(nc.sync.dma_start(out=ss0[bi][:], in_=io["s0"][i * 128:(i + 1) * 128, :]), LD, 16)
            dv = p.inc(nc.sync.dma_start(out=smd[bi][:], in_=io["modw"][i * 128:(i + 1) * 128, :]), LD, 16)
            p.w(nc.scalar, LD, dv)
            p.w(nc.scalar, "cbfv")
            sv = p.inc(nc.scalar.activation(sg0[:], smd[bi][:], AF.Silu), "cbsl", 1)
            p.w(nc.vector, "cbsl", sv)
            p.w(nc.vector, LD, dv)
            v = p.inc(nc.vector.tensor_tensor(smd[bi][:], ss0[bi][:], sg0[:], ALU.mult), "cbfv", 1)
            p.w(nc.vector, "cbfv", v)
            v = p.inc(nc.vector.tensor_tensor(smd[bi][:], smd[bi][:], sx0[bi][:], ALU.add), "cbfv", 1)
            p.w(nc.sync, "cbfv", v)
            p.inc(nc.sync.dma_start(out=io["x1b"][i * 128:(i + 1) * 128, :], in_=smd[bi][:]), SO, 16)
        p.barrier(["cbfo0", "cbfo1"])
        p.release(["cbf", "cbsl"])



def _bf16(x):
    import ml_dtypes
    return np.asarray(x).astype(ml_dtypes.bfloat16)


def _c(x):
    return np.ascontiguousarray(x)


def _rope_tables_fm(Tn):
    half = np.arange(64, dtype=np.float64) / 64.0
    invf = 1.0 / (10000.0 ** half)
    ang = np.arange(Tn, dtype=np.float64)[:, None] * invf[None, :]
    cos = np.cos(ang).T
    sin = np.sin(ang).T
    cosd = np.concatenate([cos, cos], 0).astype(np.float32)
    sind = np.concatenate([sin, sin], 0).astype(np.float32)
    return _bf16(cosd), _c(sind)


def _rope_tables_tok(Tn):
    half = np.arange(64, dtype=np.float64) / 64.0
    invf = 1.0 / (10000.0 ** half)
    ang = np.arange(Tn, dtype=np.float64)[:, None] * invf[None, :]
    cosM = np.concatenate([np.cos(ang), np.cos(ang)], 1).astype(np.float32)
    sinM = np.concatenate([np.sin(ang), np.sin(ang)], 1).astype(np.float32)
    return _c(cosM), _c(sinM)


def _pmat():
    m = np.zeros((128, 128), np.float32)
    for j in range(64):
        m[64 + j, j] = -1.0
        m[j, 64 + j] = 1.0
    return _bf16(m)


def _tri_masks():
    k = np.arange(128)[:, None]
    q = np.arange(512)[None, :]
    t = np.concatenate([(ks * 128 + k <= q).astype(np.float32) for ks in range(4)], axis=1)
    return _bf16(t)


def common_host_inputs(inputs):
    return {
        "wrt": _c(np.repeat(np.asarray(inputs["L0_ffn_w_router"], np.float32)[None, :], 128, 0)),
        "ident": _bf16(np.eye(128)),
        "identf": np.eye(128, dtype=np.float32),
        "pmat": _pmat(),
        "tri": _tri_masks(),
        "ones_col": _bf16(np.ones((128, 1))),
        "ones_colf": np.ones((128, 1), np.float32),
        "ones_row": np.ones((1, 128), np.float32),
        "consts": _c(np.repeat(np.array([[1e-5, 1e-6, 0.5, 1.5, -0.5, 0, 0, 0]], np.float32), 128, 0)),
    }


def common_io(p):
    return {
        "wrt": p.inp("wrt", [128, D], F32),
        "ident": p.inp("ident", [128, 128], BF16),
        "identf": p.inp("identf", [128, 128], F32),
        "pmat": p.inp("pmat", [128, 128], BF16),
        "tri": p.inp("tri", [128, 4 * 512], BF16),
        "ones_col": p.inp("ones_col", [128, 1], BF16),
        "ones_colf": p.inp("ones_colf", [128, 1], F32),
        "ones_row": p.inp("ones_row", [1, 128], F32),
        "consts": p.inp("consts", [128, 8], F32),
    }


def ab_host_inputs(inputs, core):
    b = core // 2
    x0 = np.asarray(inputs["L0_attn_x"], np.float32)
    cosd, sind = _rope_tables_fm(T)
    d = common_host_inputs(inputs)
    d.update({
        "xb": _c(x0[b]),
        "wq": _c(np.asarray(inputs["L0_attn_wq_w"], np.float32)),
        "wk": _c(np.asarray(inputs["L0_attn_wk_w"], np.float32)),
        "wv": _c(np.asarray(inputs["L0_attn_wv_w"], np.float32)),
        "wo": _c(np.asarray(inputs["L0_attn_o_w"], np.float32)),
        "bq": _c(np.asarray(inputs["L0_attn_wq_b"], np.float32).reshape(16, 128).T),
        "bk": _c(np.asarray(inputs["L0_attn_wk_b"], np.float32).reshape(4, 128).T),
        "bv": _c(np.repeat(np.asarray(inputs["L0_attn_wv_b"], np.float32)[None, :], 128, 0)),
        "bo": _c(np.repeat(np.asarray(inputs["L0_attn_o_b"], np.float32)[None, :], 128, 0)),
        "cosd": cosd,
        "sind": sind,
    })
    return d


def build_ab():
    p = Prog()
    io = common_io(p)
    io.update({
        "xb": p.inp("xb", [T, D], F32),
        "wq": p.inp("wq", [D, D], F32),
        "wk": p.inp("wk", [D, 512], F32),
        "wv": p.inp("wv", [D, 512], F32),
        "wo": p.inp("wo", [D, D], F32),
        "bq": p.inp("bq", [128, 16], F32),
        "bk": p.inp("bk", [128, 4], F32),
        "bv": p.inp("bv", [128, 512], F32),
        "bo": p.inp("bo", [128, D], F32),
        "cosd": p.inp("cosd", [128, T], BF16),
        "sind": p.inp("sind", [128, T], F32),
        "xn": p.outp("xn", [T, D], F32),
        "sc": p.outp("sc", [T, 1], F32),
        "s0": p.outp("s0", [T, D], F32),
        "xnf": p.dram("xnf", [D, T], BF16),
        "qf": p.dram("qf", [D, T], BF16),
        "kf": p.dram("kf", [512, T], BF16),
        "vt": p.dram("vt", [T, 512], BF16),
        "of": p.dram("of", [D, T], BF16),
    })
    p.start()
    C = load_common(p, io)
    emit_ab(p, C, io)
    return p


def mod_host_inputs(inputs, core, scall):
    x0 = np.asarray(inputs["L0_attn_x"], np.float32)
    cosM, sinM = _rope_tables_tok(T)
    e4 = np.zeros((4, 128), np.float32)
    for b in range(4):
        e4[b, 32 * b:32 * (b + 1)] = 0.0
        e4[b, :] = (np.arange(128) // 32 == b).astype(np.float32)
    boff512 = (np.arange(128) // 32 * 512).astype(np.float32).reshape(128, 1)
    boff4096 = (np.arange(128) // 32 * 4096).astype(np.float32).reshape(128, 1)
    wg = np.asarray(inputs["L0_ffn_w_gate"], np.float32)
    wu = np.asarray(inputs["L0_ffn_w_up"], np.float32)
    wgate_r = _c(wg.reshape(16, 128, 64, 128).transpose(2, 1, 0, 3).reshape(64, 128, 2048))
    wup_r = _c(wu.reshape(16, 128, 64, 128).transpose(2, 1, 0, 3).reshape(64, 128, 2048))
    d = common_host_inputs(inputs)
    d.update({
        "x0flat": _c(x0.reshape(B * T, D)),
        "scall": _c(np.asarray(scall, np.float32)),
        "cosM": cosM, "sinM": sinM,
        "e4": _c(e4),
        "e4t": _c(e4.T),
        "boff512": boff512, "boff4096": boff4096,
        "ownsel": _c((np.arange(256, dtype=np.uint32) + core * 256).reshape(256, 1)),
        "ln1w": _c(np.repeat(np.asarray(inputs["L0_ffn_ln1_w"], np.float32)[None, :], 128, 0)),
        "ln2w": _c(np.repeat(np.asarray(inputs["L0_ffn_ln2_w"], np.float32)[None, :], 128, 0)),
        "wq2": _c(np.asarray(inputs["L0_ffn_wq"], np.float32)),
        "wk2": _c(np.asarray(inputs["L0_ffn_wk"], np.float32)),
        "wv2": _c(np.asarray(inputs["L0_ffn_wv"], np.float32)),
        "wo2": _c(np.asarray(inputs["L0_ffn_wo"], np.float32)),
        "bq2": _c(np.asarray(inputs["L0_ffn_bq"], np.float32).reshape(16, 128).T),
        "bk2": _c(np.asarray(inputs["L0_ffn_bk"], np.float32).reshape(8, 128).T),
        "bv2": _c(np.repeat(np.asarray(inputs["L0_ffn_bv"], np.float32)[None, :], 128, 0)),
        "wgate_r": wgate_r, "wup_r": wup_r,
        "wdown": _c(np.asarray(inputs["L0_ffn_w_down"], np.float32)),
    })
    return d


def build_mod():
    p = Prog()
    io = common_io(p)
    io.update({
        "x0flat": p.inp("x0flat", [B * T, D], F32),
        "scall": p.inp("scall", [4, T], F32),
        "cosM": p.inp("cosM", [T, 128], F32),
        "sinM": p.inp("sinM", [T, 128], F32),
        "e4": p.inp("e4", [4, 128], F32),
        "e4t": p.inp("e4t", [128, 4], F32),
        "boff512": p.inp("boff512", [128, 1], F32),
        "boff4096": p.inp("boff4096", [128, 1], F32),
        "ownsel": p.inp("ownsel", [256, 1], U32),
        "ln1w": p.inp("ln1w", [128, D], F32),
        "ln2w": p.inp("ln2w", [128, D], F32),
        "wq2": p.inp("wq2", [D, D], F32),
        "wk2": p.inp("wk2", [D, 1024], F32),
        "wv2": p.inp("wv2", [D, 1024], F32),
        "wo2": p.inp("wo2", [D, D], F32),
        "bq2": p.inp("bq2", [128, 16], F32),
        "bk2": p.inp("bk2", [128, 8], F32),
        "bv2": p.inp("bv2", [128, 1024], F32),
        "wgate_r": p.inp("wgate_r", [64, 128, 2048], F32),
        "wup_r": p.inp("wup_r", [64, 128, 2048], F32),
        "wdown": p.inp("wdown", [8192, D], F32),
        "idxsel": p.outp("idxsel", [S_SEL, 4], F32),
        "gatedo": p.outp("gatedo", [256, D], F32),
        "selx": p.dram("selx", [S_SEL, D], F32),
        "selxn": p.dram("selxn", [S_SEL, D], F32),
        "hseltok": p.dram("hseltok", [S_SEL, D], F32),
        "hselfm": p.dram("hselfm", [D, S_SEL], BF16),
        "costk": p.dram("costk", [S_SEL, 128], F32),
        "sintk": p.dram("sintk", [S_SEL, 128], F32),
        "cosf2": p.dram("cosf2", [128, S_SEL], BF16),
        "sinf2": p.dram("sinf2", [128, S_SEL], F32),
        "qf2": p.dram("qf2", [D, S_SEL], BF16),
        "kf2": p.dram("kf2", [1024, S_SEL], BF16),
        "vt2": p.dram("vt2", [S_SEL, 1024], BF16),
        "of2": p.dram("of2", [D, S_SEL], BF16),
        "x1sel": p.dram("x1sel", [S_SEL, D], F32),
        "x1own": p.dram("x1own", [256, D], F32),
        "selxo": p.dram("selxo", [256, D], F32),
        "idxo": p.dram("idxo", [256, 4], F32),
        "h2own": p.dram("h2own", [256, D], F32),
        "h2f": p.dram("h2f", [D, 256], BF16),
        "procown": p.dram("procown", [256, D], F32),
    })
    p.start()
    C = load_common(p, io)
    io["scall_aps"] = [io["scall"][b, :].rearrange("(jp q) -> jp q", q=128) for b in range(4)]
    emit_mod(p, C, io)
    return p


def comb_host_inputs(inputs, core, xn_b, s0_b, gated_all, idxsel):
    b = core // 2
    x0 = np.asarray(inputs["L0_attn_x"], np.float32)
    d = common_host_inputs(inputs)
    d.update({
        "x0b": _c(x0[b]),
        "xn_in": _c(np.asarray(xn_b, np.float32)),
        "s0_in": _c(np.asarray(s0_b, np.float32)),
        "gated": _c(np.asarray(gated_all, np.float32)),
        "idxsel_in": _c(np.asarray(idxsel, np.float32)),
        "bofftok": np.full((128, 1), b * 4096, np.float32),
    })
    return d


def build_comb():
    p = Prog()
    io = common_io(p)
    io.update({
        "x0b": p.inp("x0b", [T, D], F32),
        "xn": p.inp("xn_in", [T, D], F32),
        "s0": p.inp("s0_in", [T, D], F32),
        "gated": p.inp("gated", [S_SEL, D], F32),
        "idxsel": p.inp("idxsel_in", [S_SEL, 4], F32),
        "bofftok": p.inp("bofftok", [128, 1], F32),
        "x1b": p.outp("x1b", [T, D], F32),
        "modw": p.dram("modw", [T, D], F32),
    })
    p.start()
    C = load_common(p, io)
    p.barrier(["pre"])
    emit_combine(p, C, io)
    return p


_PROGS = {}


def _prog(name, builder):
    if name not in _PROGS:
        _PROGS[name] = builder()
    return _PROGS[name]


def kernel_unfused(**inputs):
    inputs = {k: np.asarray(v) for k, v in inputs.items()}

    pab = _prog("ab", build_ab)
    res = run_bass_kernel_spmd(pab.nc, [ab_host_inputs(inputs, c) for c in range(8)],
                               core_ids=list(range(8)))
    xn_b = [res.results[2 * b]["xn"] for b in range(4)]
    s0_b = [res.results[2 * b]["s0"] for b in range(4)]
    scall = np.stack([res.results[2 * b]["sc"][:, 0] for b in range(4)])

    pc = _prog("mod", build_mod)
    res2 = run_bass_kernel_spmd(pc.nc, [mod_host_inputs(inputs, c, scall) for c in range(8)],
                                core_ids=list(range(8)))
    gated_all = np.concatenate([res2.results[c]["gatedo"] for c in range(8)], axis=0)
    idxsel = res2.results[0]["idxsel"]

    pd_ = _prog("comb", build_comb)
    res3 = run_bass_kernel_spmd(
        pd_.nc, [comb_host_inputs(inputs, c, xn_b[c // 2], s0_b[c // 2], gated_all, idxsel)
                 for c in range(8)], core_ids=list(range(8)))
    x1_b = [res3.results[2 * b]["x1b"] for b in range(4)]

    pe = _prog("l1", build_l1)
    res4 = run_bass_kernel_spmd(pe.nc, [l1_host_inputs(inputs, c, x1_b[c // 2]) for c in range(8)],
                                core_ids=list(range(8)))
    h_b = [np.concatenate([res4.results[2 * b]["hblk"], res4.results[2 * b + 1]["hblk"]], axis=0)
           for b in range(4)]

    pf = _prog("final", build_final)
    in5 = []
    for c in range(8):
        b, par = c // 2, c % 2
        x1o = x1_b[b][par * 2048:(par + 1) * 2048]
        hmine = h_b[b][:, par * 2048:(par + 1) * 2048]
        in5.append(final_host_inputs(inputs, c, x1o, hmine))
    res5 = run_bass_kernel_spmd(pf.nc, in5, core_ids=list(range(8)))
    out = np.zeros((B, T, D), np.float32)
    for c in range(8):
        b, par = c // 2, c % 2
        out[b, par * 2048:(par + 1) * 2048] = res5.results[c]["outb"]
    return out




def emit_proj_fm(p, C, src_v, w_v, dst_v, Tn, nout, pfx):
    nc = p.nc
    MC = nout // 128
    with ExitStack() as st:
        sw = p.sbuf(pfx + "w", [128, KC * nout], BF16, st)
        sxf = [p.sbuf(f"{pfx}x{i}", [128, KC * 512], BF16, st) for i in range(2)]
        so = [p.sbuf(f"{pfx}o{i}", [128, 512], BF16, st) for i in range(2)]
        pk = [p.psum(f"{pfx}p{i}", [128, 512], F32, st) for i in range(2)]
        for k in range(KC):
            p.inc(nc.gpsimd.dma_start(out=sw[:, k * nout:(k + 1) * nout], in_=w_v[k]), pfx + "wl", 16)
        WL = p.val(pfx + "wl")
        for tb in range(Tn // 512):
            bx = tb % 2
            if tb >= 2:
                p.w(nc.sync, pfx + "mm")
            for k in range(KC):
                p.inc(nc.sync.dma_start(out=sxf[bx][:, k * 512:(k + 1) * 512],
                                        in_=src_v[k, :, tb * 512:(tb + 1) * 512]), pfx + "ld", 16)
            ldv = p.val(pfx + "ld")
            for m in range(MC):
                pidx = m % 2
                p.w(nc.tensor, pfx + "ld", ldv)
                if tb == 0 and m == 0:
                    p.w(nc.tensor, pfx + "wl", WL)
                p.w(nc.tensor, pfx + "ev")
                for k in range(KC):
                    ins = nc.tensor.matmul(pk[pidx][:], sw[:, k * nout + m * 128:k * nout + (m + 1) * 128],
                                           sxf[bx][:, k * 512:(k + 1) * 512],
                                           start=(k == 0), stop=(k == KC - 1))
                mmv = p.inc(ins, pfx + "mm", 1)
                p.w(nc.vector, pfx + "mm", mmv)
                p.w(nc.vector, f"{pfx}so{pidx}")
                v = p.inc(nc.vector.tensor_copy(so[pidx][:], pk[pidx][:]), pfx + "ev", 1)
                p.w(nc.sync, pfx + "ev", v)
                p.inc(nc.sync.dma_start(out=dst_v[m, :, tb * 512:(tb + 1) * 512],
                                        in_=so[pidx][:]), f"{pfx}so{pidx}", 16)
        p.barrier([pfx + "so0", pfx + "so1"])
        p.release([pfx])


def emit_scan(p, C, io, xpf_v, hblk_v):
    nc = p.nc
    GATE_F = [AF.Exp, AF.Sigmoid, AF.Sigmoid, AF.Tanh]
    with ExitStack() as st:
        swg = p.sbuf("scw", [128, KC * 2048], BF16, st)
        sbg = p.sbuf("scb", [128, 32], F32, st)
        srw = p.sbuf("scr", [128, 4], F32, st)
        sxf = [p.sbuf(f"scx{i}", [128, KC * 512], BF16, st) for i in range(2)]
        sg = p.sbuf("scg", [128, 16 * 512], F32, st)
        scn = p.sbuf("scc", [128, 4 * 512], F32, st)
        scsq = p.sbuf("scq", [128, 512], F32, st)
        scry = p.sbuf("scy", [128, 8], F32, st)
        srms = p.sbuf("scm", [1, 1024], F32, st)
        srstd = p.sbuf("scs", [128, 512], F32, st)
        sth = p.sbuf("sct", [128, 512], F32, st)
        sho = [p.sbuf(f"sch{i}", [128, 512], BF16, st) for i in range(2)]
        pk = [p.psum(f"scp{i}", [128, 512], F32, st) for i in range(2)]
        pnm = p.psum("scn2", [1, 512], F32, st)
        pbc = p.psum("scb2", [128, 512], F32, st)

        p.inc(nc.sync.dma_start(out=sbg[:], in_=io["bgu"]), "scpre", 16)
        p.inc(nc.sync.dma_start(out=srw[:], in_=io["rmsw"]), "scpre", 16)
        PRE = p.val("scpre")
        p.w(nc.vector, "scpre", PRE)
        v0 = p.inc(nc.vector.memset(scry[:], 0.0), "scv", 1)

        wgu_v = io["wgu"].rearrange("(kc p) g -> kc p g", p=128)
        for u in range(2):
            p.w(nc.gpsimd, "scmm")
            for k in range(KC):
                p.inc(nc.gpsimd.dma_start(out=swg[:, k * 2048:(k + 1) * 2048],
                                          in_=wgu_v[k, :, u * 2048:(u + 1) * 2048]), "scwl", 16)
            WL = p.val("scwl")
            for tb in range(T // 512):
                bx = tb % 2
                if tb >= 2 or u > 0:
                    p.w(nc.sync, "scmm")
                for k in range(KC):
                    p.inc(nc.sync.dma_start(out=sxf[bx][:, k * 512:(k + 1) * 512],
                                            in_=xpf_v[k, :, tb * 512:(tb + 1) * 512]), f"scl{bx}", 16)
                ldv = p.val(f"scl{bx}")
                for tc in range(16):
                    pidx = tc % 2
                    p.w(nc.tensor, f"scl{bx}", ldv)
                    p.w(nc.tensor, "scwl", WL)
                    p.w(nc.tensor, "scev")
                    for k in range(KC):
                        ins = nc.tensor.matmul(pk[pidx][:], swg[:, k * 2048 + tc * 128:k * 2048 + (tc + 1) * 128],
                                               sxf[bx][:, k * 512:(k + 1) * 512],
                                               start=(k == 0), stop=(k == KC - 1))
                    mmv = p.inc(ins, "scmm", 1)
                    p.w(nc.scalar, "scmm", mmv)
                    p.w(nc.scalar, "scbv")
                    p.w(nc.scalar, "sche")
                    p.inc(nc.scalar.activation(sg[:, tc * 512:(tc + 1) * 512], pk[pidx][:],
                                               GATE_F[tc // 4], bias=sbg[:, u * 16 + tc:u * 16 + tc + 1]), "scev", 1)
                EVV = p.val("scev")
                p.w(nc.vector, "scev", EVV)
                v = p.inc(nc.vector.tensor_tensor(sg[:, 12 * 512:16 * 512], sg[:, 0:4 * 512],
                                                  sg[:, 12 * 512:16 * 512], ALU.mult), "scbv", 1)
                for cc in range(4):
                    p.w(nc.vector, "scbv", v)
                    vs = p.inc(nc.vector.tensor_tensor_scan(scn[:, cc * 512:(cc + 1) * 512],
                                                            sg[:, (4 + cc) * 512:(5 + cc) * 512],
                                                            sg[:, (12 + cc) * 512:(13 + cc) * 512],
                                                            scry[:, u * 4 + cc:u * 4 + cc + 1],
                                                            ALU.mult, ALU.add), "scbv", 1)
                    p.w(nc.vector, "scbv", vs)
                    vs = p.inc(nc.vector.tensor_copy(scry[:, u * 4 + cc:u * 4 + cc + 1],
                                                     scn[:, cc * 512 + 511:cc * 512 + 512]), "scbv", 1)
                for cc in range(4):
                    p.w(nc.scalar, "scbv", p.val("scbv"))
                    p.w(nc.scalar, "scn3")
                    sqv = p.inc(nc.scalar.activation(scsq[:], scn[:, cc * 512:(cc + 1) * 512], AF.Square), "scsq", 1)
                    p.w(nc.tensor, "scsq", sqv)
                    p.w(nc.tensor, "scnm")
                    p.inc(nc.tensor.matmul(pnm[:], C["sonesf"][:, 0:1], scsq[:],
                                           start=(cc == 0), stop=(cc == 3)), "scn3", 1)
                if "scdbg" in io and u == 0 and tb == 0:
                    p.w(nc.sync, "scbv", p.val("scbv"))
                    p.inc(nc.sync.dma_start(out=io["scdbg"][:, 0:16 * 512], in_=sg[:]), "scdg", 16)
                    p.inc(nc.sync.dma_start(out=io["scdbg"][:, 16 * 512:20 * 512], in_=scn[:]), "scdg", 16)
                NM3 = p.val("scn3")
                p.w(nc.vector, "scn3", NM3)
                v = p.inc(nc.vector.tensor_scalar(srms[0:1, 0:512], pnm[0:1, :], 1.0 / 512.0, 1e-6,
                                                  ALU.mult, ALU.add), "scnm", 1)
                p.w(nc.scalar, "scnm", v)
                sqv = p.inc(nc.scalar.activation(srms[0:1, 512:1024], srms[0:1, 0:512], AF.Sqrt), "scsq", 1)
                p.w(nc.vector, "scsq", sqv)
                v = p.inc(nc.vector.reciprocal(srms[0:1, 512:1024], srms[0:1, 512:1024]), "scnm", 1)
                p.w(nc.tensor, "scnm", v)
                p.w(nc.tensor, "scbc")
                bv = p.inc(nc.tensor.matmul(pbc[:], C["sonesr"][0:1, :], srms[0:1, 512:1024],
                                            start=True, stop=True), "scb3", 1)
                p.w(nc.vector, "scb3", bv)
                v = p.inc(nc.vector.tensor_copy(srstd[:], pbc[:]), "scbc", 1)
                for cc in range(4):
                    hb = cc % 2
                    p.w(nc.vector, "scbc", v)
                    p.w(nc.vector, "scth")
                    vv = p.inc(nc.vector.scalar_tensor_tensor(sth[:], scn[:, cc * 512:(cc + 1) * 512],
                                                              srw[:, cc:cc + 1], srstd[:],
                                                              ALU.mult, ALU.mult), "scbv", 1)
                    p.w(nc.scalar, "scbv", vv)
                    p.w(nc.scalar, f"scho{hb}")
                    sqv = p.inc(nc.scalar.activation(sth[:], sth[:], AF.Tanh), "scth", 1)
                    if "scdbg2" in io and u == 0 and tb == 0 and cc == 0:
                        p.w(nc.sync, "scth", sqv)
                        p.inc(nc.sync.dma_start(out=io["scdbg2"][:, 0:512], in_=srstd[:]), "scdg2", 16)
                        p.inc(nc.sync.dma_start(out=io["scdbg2"][:, 512:1024], in_=sth[:]), "scdg2", 16)
                    p.w(nc.vector, "scth", sqv)
                    p.w(nc.vector, f"scho{hb}")
                    vv = p.inc(nc.vector.tensor_tensor(sho[hb][:], sg[:, (8 + cc) * 512:(9 + cc) * 512],
                                                       sth[:], ALU.mult), "sche", 1)
                    p.w(nc.sync, "sche", vv)
                    p.inc(nc.sync.dma_start(out=hblk_v[u * 4 + cc, :, tb * 512:(tb + 1) * 512],
                                            in_=sho[hb][:]), f"scho{hb}", 16)
        p.barrier(["scho0", "scho1"])
        p.release(["sc"])


def l1_host_inputs(inputs, core, x1_b):
    b, par = core // 2, core % 2
    d = common_host_inputs(inputs)
    win_t = np.asarray(inputs["L1_attn_W_in"], np.float32).T
    wgt = np.asarray(inputs["L1_attn_W_gate"], np.float32).T
    bg = np.asarray(inputs["L1_attn_b_gate"], np.float32)
    d.update({
        "x1b": _c(np.asarray(x1_b, np.float32)),
        "wint": _c(win_t),
        "wgu": _c(wgt[:, par * 4096:(par + 1) * 4096]),
        "bgu": _c(bg[par * 4096:(par + 1) * 4096].reshape(32, 128).T),
        "rmsw": _c(np.asarray(inputs["L1_attn_rms_w"], np.float32).reshape(4, 128).T),
    })
    return d


def build_l1():
    p = Prog()
    io = common_io(p)
    io.update({
        "x1b": p.inp("x1b", [T, D], F32),
        "wint": p.inp("wint", [D, D], F32),
        "wgu": p.inp("wgu", [D, 4096], F32),
        "bgu": p.inp("bgu", [128, 32], F32),
        "rmsw": p.inp("rmsw", [128, 4], F32),
        "hblk": p.outp("hblk", [1024, T], BF16),
        "xn1": p.dram("xn1", [T, D], F32),
        "xn1f": p.dram("xn1f", [D, T], BF16),
        "xpf": p.dram("xpf", [D, T], BF16),
    })
    p.start()
    C = load_common(p, io)
    p.barrier(["pre"])
    xn1f_v = io["xn1f"].rearrange("(c p) t -> c p t", p=128)
    xpf_v = io["xpf"].rearrange("(c p) t -> c p t", p=128)
    hblk_v = io["hblk"].rearrange("(c p) t -> c p t", p=128)
    emit_ln(p, io["x1b"], io["xn1"], None, None, T, "e1")
    emit_tokmaj_to_fm(p, io["xn1"], xn1f_v, C, T, "e2")
    emit_proj_fm(p, C, xn1f_v, io["wint"].rearrange("(kc p) m -> kc p m", p=128),
                 xpf_v, T, D, "e3")
    emit_scan(p, C, io, xpf_v, hblk_v)
    return p




def emit_router(p, C, io, slog, sw8):
    nc = p.nc
    with ExitStack() as st:
        sxr = [p.sbuf(f"fr{i}", [128, D], F32, st) for i in range(2)]
        sxt32 = p.sbuf("frt", [128, D], F32, st)
        swr = p.sbuf("frw", [128, 16 * 8], F32, st)
        sbr = p.sbuf("frb", [128, 8], F32, st)
        srb = p.sbuf("frb2", [128, 8], F32, st)
        stmp = p.sbuf("frtm", [128, 48], F32, st)
        ptp = [p.psum(f"frp{i}", [128, 512], F32, st) for i in range(2)]
        plog = p.psum("frl", [128, 8], F32, st)
        wr_v = io["wr"].rearrange("(kc p) e -> kc p e", p=128)
        for k in range(KC):
            p.inc(nc.sync.dma_start(out=swr[:, k * 8:(k + 1) * 8], in_=wr_v[k]), "frwl", 16)
        p.inc(nc.sync.dma_start(out=sbr[:], in_=io["brrep"]), "frwl", 16)
        p.inc(nc.sync.dma_start(out=srb[:], in_=io["rbrep"]), "frwl", 16)
        WL = p.val("frwl")

        for i in range(16):
            bi = i % 2
            LD = f"frld{bi}"
            if i >= 2:
                p.w(nc.sync, "frtp", 4 * (i - 1))
            ldv = p.inc(nc.sync.dma_start(out=sxr[bi][:], in_=io["xn1o"][i * 128:(i + 1) * 128, :]), LD, 16)
            p.w(nc.tensor, LD, ldv)
            if i == 0:
                p.w(nc.tensor, "frwl", WL)
            for g4 in range(4):
                pb = g4 % 2
                if i >= 1 or g4 >= 2:
                    p.w(nc.tensor, "frev", p.val("frev") - (1 if g4 >= 2 or i == 0 else 0) if False else max(0, 4 * i + g4 - 1))
                for c in range(4):
                    ins = nc.tensor.transpose(ptp[pb][:, c * 128:(c + 1) * 128],
                                              sxr[bi][:, (g4 * 4 + c) * 128:(g4 * 4 + c + 1) * 128],
                                              C["sidentf"][:])
                tv = p.inc(ins, "frtp", 1)
                p.w(nc.vector, "frtp", tv)
                p.w(nc.vector, "frmm")
                p.inc(nc.vector.tensor_copy(sxt32[:, g4 * 512:(g4 + 1) * 512], ptp[pb][:]), "frev", 1)
            EVV = p.val("frev")
            p.w(nc.tensor, "frev", EVV)
            p.w(nc.tensor, "frlv")
            for k in range(KC):
                ins = nc.tensor.matmul(plog[:], sxt32[:, k * 128:(k + 1) * 128], swr[:, k * 8:(k + 1) * 8],
                                       start=(k == 0), stop=(k == KC - 1))
            mmv = p.inc(ins, "frmm", 1)
            p.w(nc.vector, "frmm", mmv)
            v = p.inc(nc.vector.tensor_tensor(slog[:, i * 8:(i + 1) * 8], plog[:], sbr[:], ALU.add), "frlv", 1)
            p.w(nc.vector, "frlv", v)
            v = p.inc(nc.vector.tensor_tensor(slog[:, i * 8:(i + 1) * 8], slog[:, i * 8:(i + 1) * 8],
                                              srb[:], ALU.add), "frlv", 1)
            sl = slog[:, i * 8:(i + 1) * 8]
            p.w(nc.vector, "frlv", v)
            v = p.inc(nc.vector.reduce_max(stmp[:, 0:1], sl, axis=AX.X), "frlv", 1)
            p.w(nc.vector, "frlv", v)
            v = p.inc(nc.vector.tensor_scalar(stmp[:, 8:16], sl, stmp[:, 0:1], None, ALU.is_equal), "frlv", 1)
            p.w(nc.vector, "frlv", v)
            v = p.inc(nc.vector.scalar_tensor_tensor(stmp[:, 16:24], stmp[:, 8:16], -1.0e9, sl,
                                                     ALU.mult, ALU.add), "frlv", 1)
            p.w(nc.vector, "frlv", v)
            v = p.inc(nc.vector.reduce_max(stmp[:, 1:2], stmp[:, 16:24], axis=AX.X), "frlv", 1)
            p.w(nc.vector, "frlv", v)
            v = p.inc(nc.vector.tensor_scalar(stmp[:, 24:32], stmp[:, 16:24], stmp[:, 1:2], None,
                                              ALU.is_equal), "frlv", 1)
            p.w(nc.scalar, "frlv", v)
            sv = p.inc(nc.scalar.activation(stmp[:, 2:4], stmp[:, 0:2], AF.Sigmoid), "frsg", 1)
            p.w(nc.vector, "frsg", sv)
            v = p.inc(nc.vector.tensor_tensor(stmp[:, 4:5], stmp[:, 2:3], stmp[:, 3:4], ALU.add), "frlv", 1)
            p.w(nc.vector, "frlv", v)
            v = p.inc(nc.vector.reciprocal(stmp[:, 4:5], stmp[:, 4:5]), "frlv", 1)
            p.w(nc.vector, "frlv", v)
            v = p.inc(nc.vector.tensor_tensor(stmp[:, 2:3], stmp[:, 2:3], stmp[:, 4:5], ALU.mult), "frlv", 1)
            p.w(nc.vector, "frlv", v)
            v = p.inc(nc.vector.tensor_tensor(stmp[:, 3:4], stmp[:, 3:4], stmp[:, 4:5], ALU.mult), "frlv", 1)
            p.w(nc.vector, "frlv", v)
            v = p.inc(nc.vector.tensor_scalar(sw8[:, i * 8:(i + 1) * 8], stmp[:, 8:16], stmp[:, 2:3], None,
                                              ALU.mult), "frlv", 1)
            p.w(nc.vector, "frlv", v)
            v = p.inc(nc.vector.scalar_tensor_tensor(sw8[:, i * 8:(i + 1) * 8], stmp[:, 24:32], stmp[:, 3:4],
                                                     sw8[:, i * 8:(i + 1) * 8], ALU.mult, ALU.add), "frlv", 1)
        p.barrier(["frlv"])
        p.release(["fr"])


def emit_moe(p, C, io, sw8):
    nc = p.nc
    H = 1408
    MC = H // 128
    with ExitStack() as st:
        swge = p.sbuf("mew", [128, MC * 2048], BF16, st)
        swue = p.sbuf("meu", [128, MC * 2048], BF16, st)
        swde = p.sbuf("med", [128, MC * 2048], BF16, st)
        sxf = [p.sbuf(f"mex{i}", [128, KC * 512], BF16, st) for i in range(2)]
        shh = p.sbuf("meh", [128, MC * 512], BF16, st)
        sgt = p.sbuf("meg", [128, 512], F32, st)
        sacc = [p.sbuf("mea0", [128, 2048], F32, st)]
        pg = [p.psum(f"mep{i}", [128, 512], F32, st) for i in range(2)]
        pd = [p.psum(f"meq{i}", [128, 512], F32, st) for i in range(4)]
        xn1of_v = io["xn1of"].rearrange("(c p) t -> c p t", p=128)
        slot_cnt = {}
        for e in range(9):
            EW = f"mewl{e % 2}"
            p.w(nc.gpsimd, "memm")
            p.w(nc.gpsimd, "medm")
            for m in range(MC):
                p.inc(nc.gpsimd.dma_start(out=swge[:, m * 2048:(m + 1) * 2048], in_=io["wge_r"][e, m]), EW, 16)
                p.inc(nc.gpsimd.dma_start(out=swue[:, m * 2048:(m + 1) * 2048], in_=io["wue_r"][e, m]), EW, 16)
                p.inc(nc.gpsimd.dma_start(out=swde[:, m * 2048:(m + 1) * 2048],
                                          in_=io["wde"].rearrange("(e mc p) d -> e mc p d", e=9, p=128)[e, m]), EW, 16)
            WLV = p.val(EW)
            for tb4 in range(4):
                bx = tb4 % 2
                p.w(nc.sync, "memm")
                for k in range(KC):
                    p.inc(nc.sync.dma_start(out=sxf[bx][:, k * 512:(k + 1) * 512],
                                            in_=xn1of_v[k, :, tb4 * 512:(tb4 + 1) * 512]), f"mel{bx}", 16)
                ldv = p.val(f"mel{bx}")
                for m in range(MC):
                    p.w(nc.tensor, f"mel{bx}", ldv)
                    if tb4 == 0 and m == 0:
                        p.w(nc.tensor, EW, WLV)
                    p.w(nc.tensor, "mesl")
                    for k in range(KC):
                        ins = nc.tensor.matmul(pg[0][:], swge[:, m * 2048 + k * 128:m * 2048 + (k + 1) * 128],
                                               sxf[bx][:, k * 512:(k + 1) * 512],
                                               start=(k == 0), stop=(k == KC - 1))
                    p.inc(ins, "memm", 1)
                    p.w(nc.tensor, "mehh")
                    for k in range(KC):
                        ins = nc.tensor.matmul(pg[1][:], swue[:, m * 2048 + k * 128:m * 2048 + (k + 1) * 128],
                                               sxf[bx][:, k * 512:(k + 1) * 512],
                                               start=(k == 0), stop=(k == KC - 1))
                    mmv = p.inc(ins, "memm", 1)
                    p.w(nc.scalar, "memm", mmv - 1)
                    p.w(nc.scalar, "mehh")
                    sv = p.inc(nc.scalar.activation(sgt[:], pg[0][:], AF.Silu), "mesl", 1)
                    p.w(nc.vector, "mesl", sv)
                    p.w(nc.vector, "memm", mmv)
                    p.w(nc.vector, "medm")
                    p.inc(nc.vector.tensor_tensor(shh[:, m * 512:(m + 1) * 512], sgt[:], pg[1][:],
                                                  ALU.mult), "mehh", 1)
                HHV = p.val("mehh")
                for ts in range(4):
                    ba = 0
                    p.w(nc.tensor, "mehh", HHV)
                    p.w(nc.tensor, "meac")
                    for m in range(MC):
                        for n in range(4):
                            ins = nc.tensor.matmul(pd[n][:], shh[:, m * 512 + ts * 128:m * 512 + (ts + 1) * 128],
                                                   swde[:, m * 2048 + n * 512:m * 2048 + (n + 1) * 512],
                                                   start=(m == 0), stop=(m == MC - 1))
                    dmv = p.inc(ins, "medm", 1)
                    tile = tb4 * 4 + ts
                    slot = tile
                    p.w(nc.vector, "medm", dmv)
                    p.w(nc.vector, f"meao{ba}")
                    for n in range(4):
                        if e == 0:
                            v = p.inc(nc.vector.tensor_copy(sacc[ba][:, n * 512:(n + 1) * 512], pd[n][:]), "meac", 1)
                        else:
                            v = p.inc(nc.vector.tensor_scalar(sacc[ba][:, n * 512:(n + 1) * 512], pd[n][:],
                                                              sw8[:, tile * 8 + (e - 1):tile * 8 + e], None,
                                                              ALU.mult), "meac", 1)
                    p.w(nc.gpsimd, "meac", v)
                    prev = slot_cnt.get(slot)
                    if prev is not None:
                        p.w(nc.gpsimd, f"meslot{slot}", prev)
                    ins = nc.gpsimd.dma_start(out=io["accd"][tile * 128:(tile + 1) * 128, :],
                                              in_=sacc[ba][:],
                                              accum_op=(ALU.bypass if e == 0 else ALU.add))
                    slot_cnt[slot] = p.inc(ins, f"meslot{slot}", 16)
                    p.inc(ins, f"meao{ba}", 16) if False else p.inc(nc.gpsimd.engine_nop(), f"meao{ba}", 1)
        p.barrier([f"meslot{s}" for s in range(16)])
        p.release(["me"])


def emit_wout_final(p, C, io):
    nc = p.nc
    with ExitStack() as st:
        sw = p.sbuf("fww", [128, KC * D], BF16, st)
        sob = [p.sbuf(f"fwb{i}", [128, KC * 128], BF16, st) for i in range(2)]
        sg1 = [p.sbuf(f"fwg{i}", [128, D], F32, st) for i in range(2)]
        sx1 = [p.sbuf(f"fwx{i}", [128, D], F32, st) for i in range(2)]
        sso = [p.sbuf(f"fws{i}", [128, D], F32, st) for i in range(2)]
        pso = [p.psum(f"fwp{i}", [128, 512], F32, st) for i in range(4)]
        wout_v = io["woutt"].rearrange("(kc p) m -> kc p m", p=128)
        hm_v = io["hmine"].rearrange("(c p) t -> c p t", p=128)
        for k in range(KC):
            p.inc(nc.gpsimd.dma_start(out=sw[:, k * D:(k + 1) * D], in_=wout_v[k]), "fwwl", 16)
        WL = p.val("fwwl")
        for i in range(16):
            bi = i % 2
            LD, SO, AC = f"fwld{bi}", f"fwso{bi}", f"fwac{bi}"
            if i >= 2:
                p.w(nc.sync, "fwmm")
                p.w(nc.sync, "fwev")
            for c in range(KC):
                p.inc(nc.sync.dma_start(out=sob[bi][:, c * 128:(c + 1) * 128],
                                        in_=hm_v[c, :, i * 128:(i + 1) * 128]), LD, 16)
            acv = p.inc(nc.sync.dma_start(out=sg1[bi][:], in_=io["accd"][i * 128:(i + 1) * 128, :]), AC, 16)
            acv = p.inc(nc.sync.dma_start(out=sx1[bi][:], in_=io["x1o"][i * 128:(i + 1) * 128, :]), AC, 16)
            ldv = p.val(LD)
            p.w(nc.scalar, AC, acv)
            p.w(nc.scalar, "fwev")
            sv = p.inc(nc.scalar.activation(sg1[bi][:], sg1[bi][:], AF.Silu), "fwsl", 1)
            p.w(nc.tensor, LD, ldv)
            if i == 0:
                p.w(nc.tensor, "fwwl", WL)
            for n in range(4):
                p.w(nc.tensor, "fwev")
                for k in range(KC):
                    ins = nc.tensor.matmul(pso[n][:], sob[bi][:, k * 128:(k + 1) * 128],
                                           sw[:, k * D + n * 512:k * D + (n + 1) * 512],
                                           start=(k == 0), stop=(k == KC - 1))
                p.inc(ins, "fwmm", 1)
            mmv = p.val("fwmm")
            for n in range(4):
                p.w(nc.vector, "fwmm", mmv - 3 + n)
                p.w(nc.vector, "fwsl", sv)
                if i >= 2:
                    p.w(nc.vector, SO)
                v = p.inc(nc.vector.tensor_tensor(sso[bi][:, n * 512:(n + 1) * 512], pso[n][:],
                                                  sg1[bi][:, n * 512:(n + 1) * 512], ALU.mult), "fwev", 1)
                p.w(nc.vector, "fwev", v)
                v = p.inc(nc.vector.tensor_tensor(sso[bi][:, n * 512:(n + 1) * 512],
                                                  sso[bi][:, n * 512:(n + 1) * 512],
                                                  sx1[bi][:, n * 512:(n + 1) * 512], ALU.add), "fwev", 1)
                p.w(nc.sync, "fwev", v)
                p.inc(nc.sync.dma_start(out=io["outb"][i * 128:(i + 1) * 128, n * 512:(n + 1) * 512],
                                        in_=sso[bi][:, n * 512:(n + 1) * 512]), SO, 16)
        p.barrier(["fwso0", "fwso1"])
        p.release(["fw"])


def emit_final(p, C, io):
    nc = p.nc
    slog = p.sbuf("fslog", [128, 16 * 8], F32)
    sw8 = p.sbuf("fsw8", [128, 16 * 8], F32)
    emit_ln(p, io["x1o"], io["xn1o"], None, None, 2048, "f1")
    xn1of_v = io["xn1of"].rearrange("(c p) t -> c p t", p=128)
    emit_tokmaj_to_fm(p, io["xn1o"], xn1of_v, C, 2048, "f2")
    emit_router(p, C, io, slog, sw8)
    emit_moe(p, C, io, sw8)
    emit_wout_final(p, C, io)


def final_host_inputs(inputs, core, x1o, hmine):
    d = common_host_inputs(inputs)
    wgs = [np.asarray(inputs["L1_ffn_Wg_s"], np.float32)] + \
          [np.asarray(inputs["L1_ffn_Wg"], np.float32)[e] for e in range(8)]
    wus = [np.asarray(inputs["L1_ffn_Wu_s"], np.float32)] + \
          [np.asarray(inputs["L1_ffn_Wu"], np.float32)[e] for e in range(8)]
    wds = [np.asarray(inputs["L1_ffn_Wd_s"], np.float32)] + \
          [np.asarray(inputs["L1_ffn_Wd"], np.float32)[e] for e in range(8)]
    def relay(w):
        return w.reshape(16, 128, 11, 128).transpose(2, 1, 0, 3).reshape(11, 128, 2048)
    wge_r = np.stack([relay(w) for w in wgs])
    wue_r = np.stack([relay(w) for w in wus])
    wde = np.stack(wds).reshape(9 * 1408, 2048)
    d.update({
        "x1o": _c(np.asarray(x1o, np.float32)),
        "hmine": _bf16(np.asarray(hmine)),
        "wr": _c(np.asarray(inputs["L1_ffn_Wr"], np.float32)),
        "brrep": _c(np.repeat(np.asarray(inputs["L1_ffn_br"], np.float32)[None, :], 128, 0)),
        "rbrep": _c(np.repeat(np.asarray(inputs["L1_ffn_rb"], np.float32)[None, :], 128, 0)),
        "wge_r": _c(wge_r), "wue_r": _c(wue_r), "wde": _c(wde),
        "woutt": _c(np.asarray(inputs["L1_attn_W_out"], np.float32).T),
    })
    return d


def build_final():
    p = Prog()
    io = common_io(p)
    io.update({
        "x1o": p.inp("x1o", [2048, D], F32),
        "hmine": p.inp("hmine", [D, 2048], BF16),
        "wr": p.inp("wr", [D, 8], F32),
        "brrep": p.inp("brrep", [128, 8], F32),
        "rbrep": p.inp("rbrep", [128, 8], F32),
        "wge_r": p.inp("wge_r", [9, 11, 128, 2048], F32),
        "wue_r": p.inp("wue_r", [9, 11, 128, 2048], F32),
        "wde": p.inp("wde", [9 * 1408, 2048], F32),
        "woutt": p.inp("woutt", [D, D], F32),
        "outb": p.outp("outb", [2048, D], F32),
        "xn1o": p.dram("xn1o", [2048, D], F32),
        "xn1of": p.dram("xn1of", [D, 2048], BF16),
        "accd": p.dram("accd", [2048, D], F32),
    })
    p.start()
    C = load_common(p, io)
    p.barrier(["pre"])
    emit_final(p, C, io)
    return p




def fused_host_inputs(inputs, core):
    b, par = core // 2, core % 2
    d = ab_host_inputs(inputs, core)
    dm = mod_host_inputs(inputs, core, np.zeros((4, T), np.float32))
    del dm["scall"]
    d.update({k: v for k, v in dm.items() if k not in d})
    dl = l1_host_inputs(inputs, core, np.zeros((2, 2), np.float32))
    del dl["x1b"]
    d.update({k: v for k, v in dl.items() if k not in d})
    df = final_host_inputs(inputs, core, np.zeros((2, 2), np.float32), np.zeros((2, 2), np.float32))
    del df["x1o"], df["hmine"]
    d.update({k: v for k, v in df.items() if k not in d})
    d["bofftok"] = np.full((128, 1), b * 4096, np.float32)
    d["ownrows"] = _c((np.arange(2048, dtype=np.uint32) + par * 2048).reshape(2048, 1))
    d["hrows"] = _c((((np.arange(2048, dtype=np.uint32) + 2048 * b) * 2) + par).reshape(2048, 1))
    return d


def build_fused():
    p = Prog()
    nc = p.nc
    io = common_io(p)
    io.update({
        "xb": p.inp("xb", [T, D], F32),
        "wq": p.inp("wq", [D, D], F32),
        "wk": p.inp("wk", [D, 512], F32),
        "wv": p.inp("wv", [D, 512], F32),
        "wo": p.inp("wo", [D, D], F32),
        "bq": p.inp("bq", [128, 16], F32),
        "bk": p.inp("bk", [128, 4], F32),
        "bv": p.inp("bv", [128, 512], F32),
        "bo": p.inp("bo", [128, D], F32),
        "cosd": p.inp("cosd", [128, T], BF16),
        "sind": p.inp("sind", [128, T], F32),
        "x0flat": p.inp("x0flat", [B * T, D], F32),
        "cosM": p.inp("cosM", [T, 128], F32),
        "sinM": p.inp("sinM", [T, 128], F32),
        "e4": p.inp("e4", [4, 128], F32),
        "e4t": p.inp("e4t", [128, 4], F32),
        "boff512": p.inp("boff512", [128, 1], F32),
        "boff4096": p.inp("boff4096", [128, 1], F32),
        "ownsel": p.inp("ownsel", [256, 1], U32),
        "ln1w": p.inp("ln1w", [128, D], F32),
        "ln2w": p.inp("ln2w", [128, D], F32),
        "wq2": p.inp("wq2", [D, D], F32),
        "wk2": p.inp("wk2", [D, 1024], F32),
        "wv2": p.inp("wv2", [D, 1024], F32),
        "wo2": p.inp("wo2", [D, D], F32),
        "bq2": p.inp("bq2", [128, 16], F32),
        "bk2": p.inp("bk2", [128, 8], F32),
        "bv2": p.inp("bv2", [128, 1024], F32),
        "wgate_r": p.inp("wgate_r", [64, 128, 2048], F32),
        "wup_r": p.inp("wup_r", [64, 128, 2048], F32),
        "wdown": p.inp("wdown", [8192, D], F32),
        "bofftok": p.inp("bofftok", [128, 1], F32),
        "wint": p.inp("wint", [D, D], F32),
        "wgu": p.inp("wgu", [D, 4096], F32),
        "bgu": p.inp("bgu", [128, 32], F32),
        "rmsw": p.inp("rmsw", [128, 4], F32),
        "wr": p.inp("wr", [D, 8], F32),
        "brrep": p.inp("brrep", [128, 8], F32),
        "rbrep": p.inp("rbrep", [128, 8], F32),
        "wge_r": p.inp("wge_r", [9, 11, 128, 2048], F32),
        "wue_r": p.inp("wue_r", [9, 11, 128, 2048], F32),
        "wde": p.inp("wde", [9 * 1408, 2048], F32),
        "woutt": p.inp("woutt", [D, D], F32),
        "ownrows": p.inp("ownrows", [2048, 1], U32),
        "hrows": p.inp("hrows", [2048, 1], U32),
        "outb": p.outp("outb", [2048, D], F32),
    })
    io.update({
        "xn": p.dram("xn", [T, D], F32),
        "s0": p.dram("s0", [T, D], F32),
        "xnf": p.dram("xnf", [D, T], BF16),
        "qf": p.dram("qf", [D, T], BF16),
        "kf": p.dram("kf", [512, T], BF16),
        "vt": p.dram("vt", [T, 512], BF16),
        "of": p.dram("of", [D, T], BF16),
        "idxsel": p.dram("idxsel", [S_SEL, 4], F32),
        "gatedo": p.dram("gatedo_in", [256, D], F32),
        "selx": p.dram("selx", [S_SEL, D], F32),
        "selxn": p.dram("selxn", [S_SEL, D], F32),
        "hseltok": p.dram("hseltok", [S_SEL, D], F32),
        "hselfm": p.dram("hselfm", [D, S_SEL], BF16),
        "costk": p.dram("costk", [S_SEL, 128], F32),
        "sintk": p.dram("sintk", [S_SEL, 128], F32),
        "cosf2": p.dram("cosf2", [128, S_SEL], BF16),
        "sinf2": p.dram("sinf2", [128, S_SEL], F32),
        "qf2": p.dram("qf2", [D, S_SEL], BF16),
        "kf2": p.dram("kf2", [1024, S_SEL], BF16),
        "vt2": p.dram("vt2", [S_SEL, 1024], BF16),
        "of2": p.dram("of2", [D, S_SEL], BF16),
        "x1sel": p.dram("x1sel", [S_SEL, D], F32),
        "x1own": p.dram("x1own", [256, D], F32),
        "selxo": p.dram("selxo", [256, D], F32),
        "idxo": p.dram("idxo", [256, 4], F32),
        "h2own": p.dram("h2own", [256, D], F32),
        "h2f": p.dram("h2f", [D, 256], BF16),
        "procown": p.dram("procown", [256, D], F32),
        "x1b": p.dram("x1b", [T, D], F32),
        "xn1": p.dram("xn1", [T, D], F32),
        "xn1f": p.dram("xn1f", [D, T], BF16),
        "xpf": p.dram("xpf", [D, T], BF16),
        "x1o": p.dram("x1o", [2048, D], F32),
        "hmine": p.dram("hmineg", [D, 2048], BF16),
        "xn1o": p.dram("xn1o", [2048, D], F32),
        "xn1of": p.dram("xn1of", [D, 2048], BF16),
        "accd": p.dram("accd", [2048, D], F32),
    })
    sc_in = p.dram("cc_sc_in", [T, 1], F32)
    sc_out = p.dram("cc_sc_out", [8 * T, 1], F32, addr_space="Shared")
    g_out = p.dram("cc_g_out", [S_SEL, D], F32, addr_space="Shared")
    h_in = p.dram("cc_h_in", [1024, T], BF16)
    h_out = p.dram("cc_h_out", [8 * 1024, T], BF16, addr_space="Shared")
    io["sc"] = sc_in
    io["hblk"] = h_in

    p.start()
    C = load_common(p, io)

    p.barrier(["pre"])

    def _ccsc_hook():
        p.inc(nc.gpsimd.collective_compute(
            "AllGather", ALU.bypass, replica_groups=[list(range(8))],
            ins=[sc_in.opt()], outs=[sc_out.opt()]), "ccsc", 1)
    io["ccsc_hook"] = _ccsc_hook
    emit_ab(p, C, io)

    for eng in (nc.tensor, nc.vector, nc.scalar, nc.gpsimd, nc.sync):
        p.w(eng, "ccsc")
    io["scall_aps"] = [
        sc_out[2 * b * T:(2 * b) * T + T, :].rearrange("(jp q) o -> jp (q o)", q=128)
        for b in range(4)]
    io["gatedo"] = io["gatedo"]
    emit_mod(p, C, io)
    p.inc(nc.gpsimd.collective_compute(
        "AllGather", ALU.bypass, replica_groups=[list(range(8))],
        ins=[io["gatedo"].opt()], outs=[g_out.opt()]), "ccg", 1)
    for eng in (nc.tensor, nc.vector, nc.scalar, nc.gpsimd, nc.sync):
        p.w(eng, "ccg")

    io["gated"] = g_out
    io["modw"] = io["xn"]
    io["x0b"] = io["xb"]
    emit_combine(p, C, io, copy_first=False)

    xn1f_v = io["xn1f"].rearrange("(c p) t -> c p t", p=128)
    xpf_v = io["xpf"].rearrange("(c p) t -> c p t", p=128)
    hblk_v = io["hblk"].rearrange("(c p) t -> c p t", p=128)
    emit_ln(p, io["x1b"], io["xn1"], None, None, T, "e1")
    emit_tokmaj_to_fm(p, io["xn1"], xn1f_v, C, T, "e2")
    emit_proj_fm(p, C, xn1f_v, io["wint"].rearrange("(kc p) m -> kc p m", p=128),
                 xpf_v, T, D, "e3")
    emit_scan(p, C, io, xpf_v, hblk_v)
    p.inc(nc.gpsimd.collective_compute(
        "AllGather", ALU.bypass, replica_groups=[list(range(8))],
        ins=[h_in.opt()], outs=[h_out.opt()]), "cch", 1)

    emit_gather_rows_u32(p, io["ownrows"], io["x1b"], io["x1o"], 2048, D, F32, "gf")
    for eng in (nc.tensor, nc.vector, nc.scalar, nc.gpsimd, nc.sync):
        p.w(eng, "cch")
    h_half = h_out.rearrange("(r h) t -> (r h) t") if False else h_out
    h_half_v = h_out.rearrange("r (h t) -> (r h) t", h=2)
    emit_gather_rows_u32(p, io["hrows"], h_half_v, io["hmine"], 2048, 2048, BF16, "gh")

    emit_final(p, C, io)
    return p


def kernel(**inputs):
    inputs = {k: np.asarray(v) for k, v in inputs.items()}
    pf = _prog("fused", build_fused)
    res = run_bass_kernel_spmd(pf.nc, [fused_host_inputs(inputs, c) for c in range(8)],
                               core_ids=list(range(8)))
    out = np.zeros((B, T, D), np.float32)
    for c in range(8):
        b, par = c // 2, c % 2
        out[b, par * 2048:(par + 1) * 2048] = res.results[c]["outb"]
    return out


# version 03
# speedup vs baseline: 1.2050x; 1.2050x over previous
import numpy as np
from contextlib import ExitStack

import concourse.bass as bass
import concourse.mybir as mybir
from concourse.bass_utils import run_bass_kernel_spmd

F32 = mybir.dt.float32
BF16 = mybir.dt.bfloat16
U32 = mybir.dt.uint32
I32 = mybir.dt.int32
AF = mybir.ActivationFunctionType
ALU = mybir.AluOpType
AX = mybir.AxisListType

B, T, D = 4, 4096, 2048
KC = D // 128
S_SEL = 2048
ISCALE = float(1.0 / np.sqrt(128.0))


class Prog:
    def __init__(self):
        self.nc = bass.Bass("TRN2", target_bir_lowering=False)
        self.ctx = ExitStack()
        self.cnt = {}
        self.sems = {}
        self.block = None

    def start(self):
        self.block = self.ctx.enter_context(self.nc.Block())
        self.sem("fence")

    def sem(self, name):
        if name not in self.sems:
            self.sems[name] = self.ctx.enter_context(self.nc.semaphore(name))
            self.cnt[name] = 0
        return self.sems[name]

    def inc(self, instr, name, n=1):
        instr.then_inc(self.sem(name), n)
        self.cnt[name] += n
        return self.cnt[name]

    def val(self, name):
        return self.cnt.get(name, 0)

    def w(self, eng, name, v=None):
        vv = self.cnt.get(name, 0) if v is None else v
        if vv > 0:
            eng.wait_ge(self.sem(name), vv)

    def sbuf(self, name, shape, dt, stack=None):
        return (stack or self.ctx).enter_context(self.nc.sbuf_tensor(name, shape, dt))

    def psum(self, name, shape, dt=F32, stack=None):
        return (stack or self.ctx).enter_context(self.nc.psum_tensor(name, shape, dt))

    def dram(self, name, shape, dt, **kw):
        return self.nc.dram_tensor(name, shape, dt, **kw).ap()

    def inp(self, name, shape, dt):
        return self.dram(name, shape, dt, kind="ExternalInput")

    def outp(self, name, shape, dt):
        return self.dram(name, shape, dt, kind="ExternalOutput")

    def barrier(self, names):
        nc = self.nc
        for eng in (nc.tensor, nc.vector, nc.scalar, nc.gpsimd, nc.sync):
            for n in names:
                if self.val(n) > 0:
                    eng.wait_ge(self.sem(n), self.cnt[n])

    def release(self, prefixes):
        nc = self.nc
        names = [n for n in self.sems
                 if any(n.startswith(px) for px in prefixes)]
        if not names:
            return
        handles = [self.sems[n] for n in names]
        nc.clear_and_free_semaphores(handles)
        for n in names:
            del self.sems[n]
            del self.cnt[n]
        v = self.inc(nc.gpsimd.engine_nop(), "fence", 1)
        for eng in (nc.tensor, nc.vector, nc.scalar, nc.sync):
            eng.wait_ge(self.sem("fence"), v)

    def close(self):
        self.ctx.close()




def emit_ln(p, xb, xn, sc, swrt, ntok, pfx):
    nc = p.nc
    with ExitStack() as st:
        sxt = [p.sbuf(f"{pfx}x{i}", [128, D], F32, st) for i in range(2)]
        sxn = [p.sbuf(f"{pfx}n{i}", [128, D], F32, st) for i in range(2)]
        sscr = [p.sbuf(f"{pfx}scr{i}", [128, D], F32, st) for i in range(2)]
        stat = [p.sbuf(f"{pfx}st{i}", [128, 6], F32, st) for i in range(2)]
        scol = [p.sbuf(f"{pfx}sc{i}", [128, 1], F32, st) for i in range(2)]
        SQ, VV = pfx + "sq", pfx + "vv"
        vv_end, sq_end = [], []

        for i in range(ntok // 128):
            bi = i % 2
            LD, SO = f"{pfx}ld{bi}", f"{pfx}so{bi}"
            if i >= 2:
                p.w(nc.sync, VV, vv_end[i - 2])
                p.w(nc.sync, SQ, sq_end[i - 2])
            ldv = p.inc(nc.sync.dma_start(out=sxt[bi][:], in_=xb[i * 128:(i + 1) * 128, :]), LD, 16)

            p.w(nc.scalar, LD, ldv)
            if i >= 2:
                p.w(nc.scalar, VV, vv_end[i - 2])
            sqv = p.inc(nc.scalar.activation(sscr[bi][:], sxt[bi][:], AF.Square,
                                             accum_out=stat[bi][:, 4:5]), SQ, 1)

            p.w(nc.vector, LD, ldv)
            if i >= 2:
                p.w(nc.vector, SO)
            v = p.inc(nc.vector.reduce_sum(stat[bi][:, 5:6], sxt[bi][:], axis=AX.X), VV, 1)
            p.w(nc.vector, VV, v)
            nc.vector.tensor_scalar_mul(stat[bi][:, 0:1], stat[bi][:, 5:6], 1.0 / D)
            p.w(nc.vector, SQ, sqv)
            v = p.inc(nc.vector.tensor_scalar_mul(stat[bi][:, 1:2], stat[bi][:, 4:5], 1.0 / D), VV, 1)
            p.w(nc.vector, VV, v)
            v = p.inc(nc.vector.tensor_tensor(stat[bi][:, 2:3], stat[bi][:, 0:1], stat[bi][:, 0:1], ALU.mult), VV, 1)
            p.w(nc.vector, VV, v)
            v = p.inc(nc.vector.tensor_tensor(stat[bi][:, 3:4], stat[bi][:, 1:2], stat[bi][:, 2:3], ALU.subtract), VV, 1)
            p.w(nc.vector, VV, v)
            v = p.inc(nc.vector.tensor_scalar_add(stat[bi][:, 3:4], stat[bi][:, 3:4], 1e-5), VV, 1)

            p.w(nc.scalar, VV, v)
            sqv = p.inc(nc.scalar.activation(stat[bi][:, 2:3], stat[bi][:, 3:4], AF.Sqrt), SQ, 1)

            p.w(nc.vector, SQ, sqv)
            v = p.inc(nc.vector.reciprocal(stat[bi][:, 2:3], stat[bi][:, 2:3]), VV, 1)
            p.w(nc.vector, VV, v)
            v = p.inc(nc.vector.tensor_tensor(stat[bi][:, 1:2], stat[bi][:, 2:3], stat[bi][:, 2:3], ALU.mult), VV, 1)
            p.w(nc.vector, VV, v)
            v = p.inc(nc.vector.tensor_tensor(stat[bi][:, 1:2], stat[bi][:, 1:2], stat[bi][:, 3:4], ALU.mult), VV, 1)
            p.w(nc.vector, VV, v)
            v = p.inc(nc.vector.tensor_scalar(stat[bi][:, 1:2], stat[bi][:, 1:2], -0.5, 1.5, ALU.mult, ALU.add), VV, 1)
            p.w(nc.vector, VV, v)
            v = p.inc(nc.vector.tensor_tensor(stat[bi][:, 3:4], stat[bi][:, 2:3], stat[bi][:, 1:2], ALU.mult), VV, 1)
            p.w(nc.vector, VV, v)
            v = p.inc(nc.vector.tensor_scalar(sxn[bi][:], sxt[bi][:], stat[bi][:, 0:1], stat[bi][:, 3:4],
                                              ALU.subtract, ALU.mult), VV, 1)
            if swrt is not None:
                p.w(nc.vector, VV, v)
                v = p.inc(nc.vector.tensor_tensor(sscr[bi][:], sxn[bi][:], swrt[:], ALU.mult), VV, 1)
                p.w(nc.vector, VV, v)
                v = p.inc(nc.vector.reduce_sum(sscr[bi][:, 0:16],
                                               sscr[bi][:].rearrange("p (c q) -> p c q", c=16), axis=AX.X), VV, 1)
                p.w(nc.vector, VV, v)
                v = p.inc(nc.vector.reduce_sum(scol[bi][:], sscr[bi][:, 0:16], axis=AX.X), VV, 1)

            p.w(nc.sync, VV, v)
            p.inc(nc.sync.dma_start(out=xn[i * 128:(i + 1) * 128, :], in_=sxn[bi][:]), SO, 16)
            if swrt is not None:
                p.inc(nc.sync.dma_start(out=sc[i * 128:(i + 1) * 128, :], in_=scol[bi][:, :]), SO, 16)
            vv_end.append(p.val(VV))
            sq_end.append(p.val(SQ))
        p.barrier([pfx + "so0", pfx + "so1"])
        p.release([pfx])


def emit_rms(p, xb, dst, wrep, eps, ntok, pfx):
    nc = p.nc
    with ExitStack() as st:
        sxt = [p.sbuf(f"{pfx}x{i}", [128, D], F32, st) for i in range(2)]
        sxn = [p.sbuf(f"{pfx}n{i}", [128, D], F32, st) for i in range(2)]
        sscr = p.sbuf(f"{pfx}scr", [128, D], F32, st)
        stat = [p.sbuf(f"{pfx}st{i}", [128, 2], F32, st) for i in range(2)]
        SQ, VV = pfx + "sq", pfx + "vv"
        vv_end, sq_end = [], []
        for i in range(ntok // 128):
            bi = i % 2
            LD, SO = f"{pfx}ld{bi}", f"{pfx}so{bi}"
            if i >= 2:
                p.w(nc.sync, VV, vv_end[i - 2])
                p.w(nc.sync, SQ, sq_end[i - 2])
            ldv = p.inc(nc.sync.dma_start(out=sxt[bi][:], in_=xb[i * 128:(i + 1) * 128, :]), LD, 16)
            p.w(nc.scalar, LD, ldv)
            if i >= 1:
                p.w(nc.scalar, VV)
            sqv = p.inc(nc.scalar.activation(sscr[:], sxt[bi][:], AF.Square,
                                             accum_out=stat[bi][:, 0:1]), SQ, 1)
            p.w(nc.vector, SQ, sqv)
            v = p.inc(nc.vector.tensor_scalar(stat[bi][:, 0:1], stat[bi][:, 0:1], 1.0 / D, eps,
                                              ALU.mult, ALU.add), VV, 1)
            p.w(nc.scalar, VV, v)
            sqv = p.inc(nc.scalar.activation(stat[bi][:, 1:2], stat[bi][:, 0:1], AF.Sqrt), SQ, 1)
            p.w(nc.vector, SQ, sqv)
            if i >= 2:
                p.w(nc.vector, SO)
            v = p.inc(nc.vector.reciprocal(stat[bi][:, 1:2], stat[bi][:, 1:2]), VV, 1)
            p.w(nc.vector, VV, v)
            v = p.inc(nc.vector.scalar_tensor_tensor(sxn[bi][:], sxt[bi][:], stat[bi][:, 1:2],
                                                     wrep[:], ALU.mult, ALU.mult), VV, 1)
            p.w(nc.sync, VV, v)
            p.inc(nc.sync.dma_start(out=dst[i * 128:(i + 1) * 128, :], in_=sxn[bi][:]), SO, 16)
            vv_end.append(p.val(VV))
            sq_end.append(p.val(SQ))
        p.barrier([pfx + "so0", pfx + "so1"])
        p.release([pfx])


def emit_tokmaj_to_fm(p, src, dst_v, C, ntok, pfx, nchunk=KC, dt=BF16):
    nc = p.nc
    ident = C["sident"] if dt == BF16 else C["sidentf"]
    grp = 8 if dt == BF16 else 4
    with ExitStack() as st:
        stok = [p.sbuf(f"{pfx}x{i}", [128, nchunk * 128], dt, st) for i in range(2)]
        sfm = [p.sbuf(f"{pfx}f{i}", [128, nchunk * 128], dt, st) for i in range(2)]
        nh = (nchunk + grp - 1) // grp
        ptp = [p.psum(f"{pfx}p{i}", [128, grp * 128], dt, st) for i in range(min(nh, 2) if nh > 1 else 1)]
        TP, EV = pfx + "tp", pfx + "ev"

        for i in range(ntok // 128):
            bi = i % 2
            LD, SO = f"{pfx}ld{bi}", f"{pfx}so{bi}"
            if i >= 2:
                p.w(nc.gpsimd, TP, nh * (i - 1))
            ldv = p.inc(nc.gpsimd.dma_start(out=stok[bi][:], in_=src[i * 128:(i + 1) * 128, :]), LD, 16)

            p.w(nc.tensor, LD, ldv)
            for h in range(nh):
                hb = h % len(ptp)
                prev = nh * (i - 1) + h + 1 if i >= 1 else (h - len(ptp) + 1 if h >= len(ptp) else 0)
                if h >= len(ptp):
                    prev = nh * i + h - len(ptp) + 1
                if prev > 0:
                    p.w(nc.tensor, EV, prev)
                for c in range(min(grp, nchunk - h * grp)):
                    cc = h * grp + c
                    ins = nc.tensor.transpose(ptp[hb][:, c * 128:(c + 1) * 128],
                                              stok[bi][:, cc * 128:(cc + 1) * 128], ident[:])
                p.inc(ins, TP, 1)
            tpv = p.val(TP)

            for h in range(nh):
                hb = h % len(ptp)
                p.w(nc.vector, TP, tpv - nh + 1 + h)
                if i >= 2:
                    p.w(nc.vector, SO)
                w = min(grp, nchunk - h * grp) * 128
                p.inc(nc.vector.tensor_copy(sfm[bi][:, h * grp * 128:h * grp * 128 + w], ptp[hb][:, 0:w]), EV, 1)
            evv = p.val(EV)

            p.w(nc.sync, EV, evv)
            for c in range(nchunk):
                p.inc(nc.sync.dma_start(out=dst_v[c, :, i * 128:(i + 1) * 128],
                                        in_=sfm[bi][:, c * 128:(c + 1) * 128]), SO, 16)
        p.barrier([pfx + "so0", pfx + "so1"])
        p.release([pfx])


def emit_qkv_rope(p, C, xnf_v, qf_v, kf_v, vt, Tn, nq, nkv, bq_t, bk_t, bv_t,
                  wq_v, wk_v, wv_v, cos_src, sin_src, nrm_slot0, pfx):
    nc = p.nc
    NB = Tn // 512
    snrm = C["snrm"]
    with ExitStack() as st:
        swq = p.sbuf(pfx + "wq", [128, KC * nq * 128], BF16, st)
        swk = p.sbuf(pfx + "wk", [128, KC * nkv * 128], BF16, st)
        swv = p.sbuf(pfx + "wv", [128, KC * nkv * 128], BF16, st)
        scosb = p.sbuf(pfx + "cos", [128, Tn], BF16, st)
        ssinf = p.sbuf(pfx + "sin", [128, Tn], F32, st)
        sxf = [p.sbuf(f"{pfx}x{i}", [128, KC * 512], BF16, st) for i in range(2)]
        srcb = [p.sbuf(f"{pfx}s{i}", [128, 512], BF16, st) for i in range(2)]
        sta = [p.sbuf(f"{pfx}t{i}", [128, 512], BF16, st) for i in range(2)]
        stb = [p.sbuf(f"{pfx}u{i}", [128, 512], BF16, st) for i in range(2)]
        sout = [p.sbuf(f"{pfx}o{i}", [128, 512], BF16, st) for i in range(2)]
        sqsc = p.sbuf(pfx + "q", [128, 512], F32, st)
        svtt = [p.sbuf(f"{pfx}v{i}", [128, 512], BF16, st) for i in range(2)]
        pk = [p.psum(f"{pfx}pk{i}", [128, 512], F32, st) for i in range(2)]
        prot = [p.psum(f"{pfx}pr{i}", [128, 512], F32, st) for i in range(2)]
        pq = [p.psum(f"{pfx}pq{i}", [128, 512], F32, st) for i in range(2)]
        pn = p.psum(pfx + "pn", [1, 512], F32, st)

        wwq, wwk = nq * 128, nkv * 128
        for k in range(KC):
            p.inc(nc.gpsimd.dma_start(out=swq[:, k * wwq:(k + 1) * wwq], in_=wq_v[k]), pfx + "wld", 16)
            p.inc(nc.gpsimd.dma_start(out=swk[:, k * wwk:(k + 1) * wwk], in_=wk_v[k]), pfx + "wld", 16)
            p.inc(nc.gpsimd.dma_start(out=swv[:, k * wwk:(k + 1) * wwk], in_=wv_v[k]), pfx + "wld", 16)
        p.inc(nc.gpsimd.dma_start(out=scosb[:], in_=cos_src), pfx + "wld", 16)
        p.inc(nc.gpsimd.dma_start(out=ssinf[:], in_=sin_src), pfx + "wld", 16)
        WLD = p.val(pfx + "wld")

        for tb in range(NB):
            bx = tb % 2
            if tb >= 2:
                p.w(nc.sync, pfx + "mm")
                p.w(nc.sync, pfx + "vm")
            for k in range(KC):
                p.inc(nc.sync.dma_start(out=sxf[bx][:, k * 512:(k + 1) * 512],
                                        in_=xnf_v[k, :, tb * 512:(tb + 1) * 512]), pfx + "ld", 16)
            ldv = p.val(pfx + "ld")
            cosw = scosb[:, tb * 512:(tb + 1) * 512]
            sinw = ssinf[:, tb * 512:(tb + 1) * 512]

            def rope_chunk(m, wsb, wwid, bias_ap, dst_chunk_v, nslot):
                pidx = m % 2
                p.w(nc.tensor, pfx + "ld", ldv)
                if tb == 0 and m == 0:
                    p.w(nc.tensor, pfx + "wld", WLD)
                p.w(nc.tensor, pfx + "sc")
                for k in range(KC):
                    ins = nc.tensor.matmul(pk[pidx][:],
                                           wsb[:, k * wwid + m * 128:k * wwid + (m + 1) * 128],
                                           sxf[bx][:, k * 512:(k + 1) * 512],
                                           start=(k == 0), stop=(k == KC - 1))
                mmv = p.inc(ins, pfx + "mm", 1)
                p.w(nc.scalar, pfx + "mm", mmv)
                p.w(nc.scalar, pfx + "rp")
                p.w(nc.scalar, pfx + "rt")
                scv = p.inc(nc.scalar.activation(srcb[pidx][:], pk[pidx][:], AF.Identity, bias=bias_ap), pfx + "sc", 1)
                p.w(nc.scalar, pfx + "nq")
                scv2 = p.inc(nc.scalar.activation(sqsc[:], srcb[pidx][:], AF.Square), pfx + "sc", 1)
                p.w(nc.tensor, pfx + "sc", scv2)
                p.w(nc.tensor, pfx + "nm")
                nv = p.inc(nc.tensor.matmul(pn[:], C["sonesf"][:, 0:1], sqsc[:], start=True, stop=True), pfx + "nq", 1)
                p.w(nc.vector, pfx + "nq", nv)
                p.inc(nc.vector.reduce_max(snrm[0:1, nslot * NB + tb:nslot * NB + tb + 1],
                                           pn[0:1, :], axis=AX.X), pfx + "nm", 1)
                p.w(nc.tensor, pfx + "rp")
                rv = p.inc(nc.tensor.matmul(prot[pidx][:], C["spmat"][:], srcb[pidx][:],
                                            start=True, stop=True), pfx + "rt", 1)
                p.w(nc.vector, pfx + "sc", scv)
                p.w(nc.vector, pfx + "so0")
                p.w(nc.vector, pfx + "so1")
                v = p.inc(nc.vector.tensor_tensor(sta[pidx][:], srcb[pidx][:], cosw, ALU.mult), pfx + "rp", 1)
                p.w(nc.vector, pfx + "rt", rv)
                v = p.inc(nc.vector.tensor_tensor(stb[pidx][:], prot[pidx][:], sinw, ALU.mult), pfx + "rp", 1)
                p.w(nc.vector, pfx + "rp", v)
                v = p.inc(nc.vector.tensor_tensor(sout[pidx][:], sta[pidx][:], stb[pidx][:], ALU.add), pfx + "rp", 1)
                p.w(nc.sync, pfx + "rp", v)
                p.inc(nc.sync.dma_start(out=dst_chunk_v[:, tb * 512:(tb + 1) * 512],
                                        in_=sout[pidx][:]), f"{pfx}so{pidx}", 16)

            for m in range(nkv):
                rope_chunk(m, swk, wwk, bk_t[:, m:m + 1], kf_v[m], nq + m)
            for m in range(nq):
                rope_chunk(m, swq, wwq, bq_t[:, m:m + 1], qf_v[m], m)

            nvb = (nkv * 128) // 512 if (nkv * 128) >= 512 else 1
            vw = nkv * 128
            for ts in range(4):
                pidx = ts % 2
                for nb in range(max(1, vw // 512)):
                    wlo = nb * 512
                    wwid2 = min(512, vw - wlo)
                    p.w(nc.tensor, pfx + "ve")
                    for k in range(KC):
                        ins = nc.tensor.matmul(pq[pidx][:, 0:wwid2],
                                               sxf[bx][:, k * 512 + ts * 128:k * 512 + (ts + 1) * 128],
                                               swv[:, k * vw + wlo:k * vw + wlo + wwid2],
                                               start=(k == 0), stop=(k == KC - 1))
                    mmv = p.inc(ins, pfx + "vm", 1)
                    p.w(nc.vector, pfx + "vm", mmv)
                    p.w(nc.vector, pfx + "vo0")
                    p.w(nc.vector, pfx + "vo1")
                    v = p.inc(nc.vector.tensor_tensor(svtt[pidx][:, 0:wwid2], pq[pidx][:, 0:wwid2],
                                                      bv_t[:, wlo:wlo + wwid2], ALU.add), pfx + "ve", 1)
                    p.w(nc.sync, pfx + "ve", v)
                    p.inc(nc.sync.dma_start(
                        out=vt[tb * 512 + ts * 128:tb * 512 + (ts + 1) * 128, wlo:wlo + wwid2],
                        in_=svtt[pidx][:, 0:wwid2]), f"{pfx}vo{pidx}", 16)
        p.barrier([pfx + "so0", pfx + "so1", pfx + "vo0", pfx + "vo1", pfx + "nm"])
        p.release([pfx + c for c in ("wld", "ld", "mm", "sc", "nq", "nm", "rt", "rp", "so", "vm", "ve", "vo")])


def emit_bounds(p, C, nq, nkv, rep, Tn, dst_bnd, nrm_slot0, pfx):
    nc = p.nc
    NB = Tn // 512
    snrm = C["snrm"]
    base = (nrm_slot0 + nq + nkv) * NB
    with ExitStack() as st:
        ptmp = p.psum(pfx + "p", [128, nq], F32, st)
        for h in range(nq):
            nc.vector.reduce_max(snrm[0:1, base + h:base + h + 1],
                                 snrm[0:1, (nrm_slot0 + h) * NB:(nrm_slot0 + h) * NB + NB], axis=AX.X)
        for m in range(nkv):
            nc.vector.reduce_max(snrm[0:1, base + nq + m:base + nq + m + 1],
                                 snrm[0:1, (nrm_slot0 + nq + m) * NB:(nrm_slot0 + nq + m) * NB + NB], axis=AX.X)
        v = p.inc(nc.vector.engine_nop(), pfx + "v", 1)
        p.w(nc.vector, pfx + "v", v)
        for h in range(nq):
            nc.vector.tensor_tensor(snrm[0:1, base + nq + nkv + h:base + nq + nkv + h + 1],
                                    snrm[0:1, base + h:base + h + 1],
                                    snrm[0:1, base + nq + h // rep:base + nq + h // rep + 1], ALU.mult)
        v = p.inc(nc.vector.engine_nop(), pfx + "v", 1)
        p.w(nc.scalar, pfx + "v", v)
        b2 = base + nq + nkv
        sv = p.inc(nc.scalar.activation(snrm[0:1, b2 + nq:b2 + 2 * nq], snrm[0:1, b2:b2 + nq], AF.Sqrt), pfx + "s", 1)
        p.w(nc.vector, pfx + "s", sv)
        v = p.inc(nc.vector.tensor_scalar_mul(snrm[0:1, b2:b2 + nq], snrm[0:1, b2 + nq:b2 + 2 * nq], -ISCALE), pfx + "v", 1)
        p.w(nc.tensor, pfx + "v", v)
        mv = p.inc(nc.tensor.matmul(ptmp[:], C["sonesr"][0:1, :], snrm[0:1, b2:b2 + nq], start=True, stop=True), pfx + "m", 1)
        p.w(nc.vector, pfx + "m", mv)
        p.inc(nc.vector.tensor_copy(dst_bnd[:], ptmp[:]), pfx + "v", 1)
        p.barrier([pfx + "v"])
        p.release([pfx])


def emit_attention(p, C, qf_v, kf_v, vt, of_v, Tn, nq, nkv, sbnd, pfx):
    nc = p.nc
    rep = nq // nkv
    NQC = Tn // 512
    NSB = 3
    with ExitStack() as st:
        sqf = p.sbuf(pfx + "q", [128, nq * 512], BF16, st)
        skb = [p.sbuf(f"{pfx}k{i}", [128, 512], BF16, st) for i in range(2)]
        svb = [p.sbuf(f"{pfx}v{i}", [128, 4 * 128], BF16, st) for i in range(2)]
        spp = [p.sbuf(f"{pfx}p{i}", [128, 512], BF16, st) for i in range(4)]
        ssum = p.sbuf(pfx + "sum", [1, 512], F32, st)
        srcp = p.sbuf(pfx + "rcp", [128, 512], F32, st)
        sof = p.sbuf(pfx + "of", [128, 2 * 512], BF16, st)
        ps = [p.psum(f"{pfx}ps{i}", [128, 512], F32, st) for i in range(NSB)]
        po = [p.psum(f"{pfx}po{i}", [128, 512], F32, st) for i in range(2)]
        psm = [p.psum(f"{pfx}pm{i}", [1, 512], F32, st) for i in range(2)]
        pbc = p.psum(pfx + "bc", [128, 512], F32, st)

        SP, EX, PV, SM = pfx + "sp", pfx + "ex", pfx + "pv", pfx + "sm"
        kv_hist = []
        ex_at_bank = {}
        pvsm_at_spp = {}
        for qc in range(NQC):
            p.w(nc.sync, pfx + "oo")
            for h in range(nq):
                p.inc(nc.sync.dma_start(out=sqf[:, h * 512:(h + 1) * 512],
                                        in_=qf_v[h, :, qc * 512:(qc + 1) * 512]), pfx + "ql", 16)
            qlv = p.val(pfx + "ql")
            for g in range(nq // 2):
                kvh = (2 * g) // rep
                for kb in range(qc + 1):
                    bz = len(kv_hist) % 2
                    AKL = f"{pfx}kl{bz}"
                    if len(kv_hist) >= 2:
                        p.w(nc.gpsimd, SP, kv_hist[-2][0])
                        p.w(nc.gpsimd, PV, kv_hist[-2][1])
                    p.inc(nc.gpsimd.dma_start(out=skb[bz][:],
                                              in_=kf_v[kvh, :, kb * 512:(kb + 1) * 512]), AKL, 16)
                    vt_v = vt[kb * 512:(kb + 1) * 512, kvh * 128:(kvh + 1) * 128].rearrange(
                        "(s p) q -> s p q", p=128)
                    for s4 in range(4):
                        p.inc(nc.gpsimd.dma_start(out=svb[bz][:, s4 * 128:(s4 + 1) * 128],
                                                  in_=vt_v[s4]), AKL, 16)
                    klv = p.val(AKL)

                    units = [(hh, ks) for hh in range(2) for ks in range(4)]
                    meta = []
                    def emit_sp(i):
                        hh, ks = units[i]
                        bank = i % NSB
                        p.w(nc.tensor, AKL, klv)
                        if qc + g + kb == 0 and i == 0:
                            p.w(nc.tensor, pfx + "ql", qlv)
                        elif g == 0 and kb == 0 and i == 0:
                            p.w(nc.tensor, pfx + "ql", qlv)
                        if bank in ex_at_bank:
                            p.w(nc.tensor, EX, ex_at_bank[bank])
                        if kb == 0 and i == 0:
                            p.w(nc.tensor, pfx + "ev")
                        spv = p.inc(nc.tensor.matmul(ps[bank][:], skb[bz][:, ks * 128:(ks + 1) * 128],
                                                     sqf[:, (2 * g + hh) * 512:(2 * g + hh + 1) * 512],
                                                     start=True, stop=True), SP, 1)
                        meta.append([spv, None])
                    def emit_exp(i):
                        hh, ks = units[i]
                        bank = i % NSB
                        p.w(nc.scalar, SP, meta[i][0])
                        if ks in pvsm_at_spp:
                            pvv, smv = pvsm_at_spp[ks]
                            p.w(nc.scalar, PV, pvv)
                            p.w(nc.scalar, SM, smv)
                        exv = p.inc(nc.scalar.activation(spp[ks][:], ps[bank][:], AF.Exp,
                                                         bias=sbnd[:, 2 * g + hh:2 * g + hh + 1],
                                                         scale=ISCALE), EX, 1)
                        if kb == qc:
                            p.w(nc.vector, EX, exv)
                            exv = p.inc(nc.vector.tensor_tensor(spp[ks][:], spp[ks][:],
                                                                C["stri"][:, ks * 512:(ks + 1) * 512],
                                                                ALU.mult), EX, 1)
                        meta[i][1] = exv
                        ex_at_bank[bank] = exv
                    def emit_pv(i):
                        hh, ks = units[i]
                        p.w(nc.tensor, EX, meta[i][1])
                        first = (kb == 0 and ks == 0)
                        last = (kb == qc and ks == 3)
                        pvv = p.inc(nc.tensor.matmul(po[hh][:], svb[bz][:, ks * 128:(ks + 1) * 128],
                                                     spp[ks][:], start=first, stop=last), PV, 1)
                        smv = p.inc(nc.tensor.matmul(psm[hh][:], C["sonesc"][:, 0:1],
                                                     spp[ks][:], start=first, stop=last), SM, 1)
                        pvsm_at_spp[ks] = (pvv, smv)
                    for i in range(8):
                        emit_sp(i)
                        emit_exp(i)
                        if i >= NSB - 1:
                            emit_pv(i - NSB + 1)
                    for i in range(8 - NSB + 1, 8):
                        emit_pv(i)
                    kv_hist.append((p.val(SP), p.val(PV)))
                for hh in range(2):
                    p.w(nc.vector, PV)
                    p.w(nc.vector, SM)
                    p.w(nc.vector, pfx + "bc")
                    v = p.inc(nc.vector.tensor_copy(ssum[0:1, :], psm[hh][0:1, :]), pfx + "ev", 1)
                    p.w(nc.vector, pfx + "ev", v)
                    v = p.inc(nc.vector.reciprocal(ssum[0:1, :], ssum[0:1, :]), pfx + "ev", 1)
                    p.w(nc.tensor, pfx + "ev", v)
                    bv = p.inc(nc.tensor.matmul(pbc[:], C["sonesr"][0:1, :], ssum[0:1, :],
                                                start=True, stop=True), pfx + "bc", 1)
                    p.w(nc.vector, pfx + "bc", bv)
                    v = p.inc(nc.vector.tensor_copy(srcp[:], pbc[:]), pfx + "ev", 1)
                    p.w(nc.vector, pfx + "ev", v)
                    p.w(nc.vector, pfx + "oo")
                    v = p.inc(nc.vector.tensor_tensor(sof[:, hh * 512:(hh + 1) * 512],
                                                      po[hh][:], srcp[:], ALU.mult), pfx + "ev", 1)
                evv = p.val(pfx + "ev")
                p.w(nc.sync, pfx + "ev", evv)
                for hh in range(2):
                    p.inc(nc.sync.dma_start(out=of_v[2 * g + hh, :, qc * 512:(qc + 1) * 512],
                                            in_=sof[:, hh * 512:(hh + 1) * 512]), pfx + "oo", 16)
        p.barrier([pfx + "oo"])
        p.release([pfx])


def emit_proj_tok(p, C, src_v, w_v, nin_c, bias_t, dst, Tn, nout, pfx, add_src=None):
    nc = p.nc
    NBL = nout // 512
    with ExitStack() as st:
        sw = p.sbuf(pfx + "w", [128, nin_c * nout], BF16, st)
        sob = [p.sbuf(f"{pfx}b{i}", [128, nin_c * 128], BF16, st) for i in range(2)]
        sso = [p.sbuf(f"{pfx}s{i}", [128, NBL * 512], F32, st) for i in range(2)]
        sad = [p.sbuf(f"{pfx}a{i}", [128, nout], F32, st) for i in range(2)]
        pso = [p.psum(f"{pfx}p{i}", [128, 512], F32, st) for i in range(4)]
        for k in range(nin_c):
            p.inc(nc.gpsimd.dma_start(out=sw[:, k * nout:(k + 1) * nout], in_=w_v[k]), pfx + "wl", 16)
        WOL = p.val(pfx + "wl")

        for i in range(Tn // 128):
            bi = i % 2
            LD, SO, AL = f"{pfx}ld{bi}", f"{pfx}so{bi}", f"{pfx}al{bi}"
            if i >= 2:
                p.w(nc.sync, pfx + "mm")
            for c in range(nin_c):
                p.inc(nc.sync.dma_start(out=sob[bi][:, c * 128:(c + 1) * 128],
                                        in_=src_v[c, :, i * 128:(i + 1) * 128]), LD, 16)
            olv = p.val(LD)
            if add_src is not None:
                if i >= 2:
                    p.w(nc.sync, pfx + "ev")
                alv = p.inc(nc.sync.dma_start(out=sad[bi][:], in_=add_src[i * 128:(i + 1) * 128, :]), AL, 16)
            p.w(nc.tensor, LD, olv)
            if i == 0:
                p.w(nc.tensor, pfx + "wl", WOL)
            for n in range(NBL):
                p.w(nc.tensor, pfx + "ev")
                for k in range(nin_c):
                    ins = nc.tensor.matmul(pso[n % 4][:], sob[bi][:, k * 128:(k + 1) * 128],
                                           sw[:, k * nout + n * 512:k * nout + (n + 1) * 512],
                                           start=(k == 0), stop=(k == nin_c - 1))
                p.inc(ins, pfx + "mm", 1)
            mmv = p.val(pfx + "mm")
            for n in range(NBL):
                p.w(nc.vector, pfx + "mm", mmv - NBL + 1 + n)
                if i >= 2:
                    p.w(nc.vector, SO)
                if bias_t is not None:
                    v = p.inc(nc.vector.tensor_tensor(sso[bi][:, n * 512:(n + 1) * 512],
                                                      pso[n % 4][:], bias_t[:, n * 512:(n + 1) * 512],
                                                      ALU.add), pfx + "ev", 1)
                elif add_src is not None:
                    p.w(nc.vector, AL, alv)
                    v = p.inc(nc.vector.tensor_tensor(sso[bi][:, n * 512:(n + 1) * 512],
                                                      pso[n % 4][:], sad[bi][:, n * 512:(n + 1) * 512],
                                                      ALU.add), pfx + "ev", 1)
                else:
                    v = p.inc(nc.vector.tensor_copy(sso[bi][:, n * 512:(n + 1) * 512], pso[n % 4][:]), pfx + "ev", 1)
                p.w(nc.sync, pfx + "ev", v)
                p.inc(nc.sync.dma_start(out=dst[i * 128:(i + 1) * 128, n * 512:(n + 1) * 512],
                                        in_=sso[bi][:, n * 512:(n + 1) * 512]), SO, 16)
        p.barrier([pfx + "so0", pfx + "so1"])
        p.release([pfx])




def load_common(p, io):
    nc = p.nc
    C = {}
    C["sident"] = p.sbuf("sident", [128, 128], BF16)
    C["sidentf"] = p.sbuf("sidentf", [128, 128], F32)
    C["stri"] = p.sbuf("stri", [128, 4 * 512], BF16)
    C["sonesc"] = p.sbuf("sonesc", [128, 1], BF16)
    C["sonesf"] = p.sbuf("sonesf", [128, 1], F32)
    C["sonesr"] = p.sbuf("sonesr", [1, 128], F32)
    C["spmat"] = p.sbuf("spmat", [128, 128], BF16)
    C["snrm"] = p.sbuf("snrm", [1, 512], F32)
    C["sconst"] = p.sbuf("sconst", [128, 8], F32)
    nc.const_aps.register(F32, 1e-5, C["sconst"][:, 0:1])
    nc.const_aps.register(F32, 1e-6, C["sconst"][:, 1:2])
    nc.const_aps.register(F32, 0.5, C["sconst"][:, 2:3])
    nc.const_aps.register(F32, 1.5, C["sconst"][:, 3:4])
    nc.const_aps.register(F32, -0.5, C["sconst"][:, 4:5])
    for nm in ("ident", "identf", "tri", "ones_col", "ones_colf", "ones_row", "pmat", "consts"):
        key = {"ident": "sident", "identf": "sidentf", "tri": "stri",
               "ones_col": "sonesc", "ones_colf": "sonesf", "ones_row": "sonesr",
               "pmat": "spmat", "consts": "sconst"}[nm]
        p.inc(nc.sync.dma_start(out=C[key][:], in_=io[nm]), "pre", 16)
    return C


def emit_ab(p, C, io):
    nc = p.nc
    xnf_v = io["xnf"].rearrange("(c p) t -> c p t", p=128)
    qf_v = io["qf"].rearrange("(c p) t -> c p t", p=128)
    kf_v = io["kf"].rearrange("(c p) t -> c p t", p=128)
    of_v = io["of"].rearrange("(c p) t -> c p t", p=128)

    with ExitStack() as stA:
        swrt = p.sbuf("swrt", [128, D], F32, stA)
        sbq = p.sbuf("sbq", [128, 16], F32, stA)
        sbk = p.sbuf("sbk", [128, 4], F32, stA)
        sbv = p.sbuf("sbv", [128, 512], F32, stA)
        sbo = p.sbuf("sbo", [128, D], F32, stA)
        sbnd = p.sbuf("sbnd", [128, 16], F32, stA)
        for nm, dst in (("wrt", swrt), ("bq", sbq), ("bk", sbk), ("bv", sbv), ("bo", sbo)):
            p.inc(nc.sync.dma_start(out=dst[:], in_=io[nm]), "preab", 16)
        p.barrier(["preab"])

        emit_ln(p, io["xb"], io["xn"], io["sc"], swrt, T, "s1")
        if "ccsc_hook" in io:
            io["ccsc_hook"]()
        emit_tokmaj_to_fm(p, io["xn"], xnf_v, C, T, "tp")
        emit_qkv_rope(p, C, xnf_v, qf_v, kf_v, io["vt"], T, 16, 4,
                      sbq, sbk, sbv,
                      io["wq"].rearrange("(kc p) m -> kc p m", p=128),
                      io["wk"].rearrange("(kc p) m -> kc p m", p=128),
                      io["wv"].rearrange("(kc p) m -> kc p m", p=128),
                      io["cosd"], io["sind"], 0, "pj")
        emit_bounds(p, C, 16, 4, 4, T, sbnd, 0, "bd")
        emit_attention(p, C, qf_v, kf_v, io["vt"], of_v, T, 16, 4, sbnd, "at")
        emit_proj_tok(p, C, of_v, io["wo"].rearrange("(kc p) m -> kc p m", p=128),
                      KC, sbo, io["s0"], T, D, "op")
        p.release(["preab"])






def emit_topk(p, C, io, sc_batches, idxsel, pfx="tk"):
    nc = p.nc
    with ExitStack() as st:
        ssc = p.sbuf(pfx + "sc", [128, 128], F32, st)
        sE4 = p.sbuf(pfx + "e4", [4, 128], F32, st)
        sE4T = p.sbuf(pfx + "e4t", [128, 4], F32, st)
        slh = p.sbuf(pfx + "lh", [4, 8], F32, st)
        smid = p.sbuf(pfx + "mid", [128, 1], F32, st)
        scmp = p.sbuf(pfx + "cmp", [128, 128], F32, st)
        scnt = p.sbuf(pfx + "cnt", [128, 1], F32, st)
        sone = p.sbuf(pfx + "one", [128, 128], F32, st)
        spfx = p.sbuf(pfx + "pf", [128, 128], F32, st)
        srow = p.sbuf(pfx + "row", [1, 128], F32, st)
        srow2 = p.sbuf(pfx + "row2", [1, 128], F32, st)
        soff = p.sbuf(pfx + "off", [128, 1], F32, st)
        sit = p.sbuf(pfx + "it", [128, 128], F32, st)
        spk = p.sbuf(pfx + "pk", [128, 512], F32, st)
        sdst = p.sbuf(pfx + "dst", [128, 128], F32, st)
        sdstu = p.sbuf(pfx + "dstu", [128, 128], U32, st)
        sboff = p.sbuf(pfx + "boff", [128, 1], F32, st)
        sboff2 = p.sbuf(pfx + "boff2", [128, 1], F32, st)
        smsk = p.sbuf(pfx + "msk", [4, 2], I32, st)
        scmpi = p.sbuf(pfx + "cmi", [128, 128], I32, st)
        pcnt = p.psum(pfx + "pc", [4, 1], F32, st)
        pbc = p.psum(pfx + "pb", [128, 1], F32, st)
        tmpd = p.dram(pfx + "tmpd", [128], F32)

        VQ = pfx + "v"
        for b in range(4):
            p.inc(nc.sync.dma_start(out=ssc[32 * b:32 * (b + 1), :],
                                    in_=sc_batches[b]), pfx + "ld", 16)
        p.inc(nc.sync.dma_start(out=sE4[:], in_=io["e4"]), pfx + "ld", 16)
        p.inc(nc.sync.dma_start(out=sE4T[:], in_=io["e4t"]), pfx + "ld", 16)
        p.inc(nc.sync.dma_start(out=sboff[:], in_=io["boff512"]), pfx + "ld", 16)
        p.inc(nc.sync.dma_start(out=sboff2[:], in_=io["boff4096"]), pfx + "ld", 16)
        LDV = p.val(pfx + "ld")
        p.w(nc.vector, pfx + "ld", LDV)
        nc.vector.memset(slh[:], 0.0)
        nc.vector.memset(slh[:, 0:1], -20.0)
        nc.vector.memset(slh[:, 1:2], 20.0)
        nc.vector.memset(slh[:, 2:3], 0.0)
        v = p.inc(nc.vector.memset(sone[:], 1.0), VQ, 1)
        p.w(nc.gpsimd, VQ, v)
        gv = p.inc(nc.gpsimd.iota(sit[:], pattern=[[1, 128]], base=0, channel_multiplier=128,
                                  allow_small_or_imprecise_dtypes=True), pfx + "g", 1)

        def count_pass(mid_col):
            p.w(nc.tensor, VQ)
            p.w(nc.tensor, pfx + "cp")
            bv = p.inc(nc.tensor.matmul(pbc[:], sE4[:, :], mid_col, start=True, stop=True), pfx + "bm", 1)
            p.w(nc.vector, pfx + "bm", bv)
            v = p.inc(nc.vector.tensor_copy(smid[:], pbc[:]), pfx + "cp", 1)
            p.w(nc.vector, pfx + "cp", v)
            v = p.inc(nc.vector.tensor_scalar(scmp[:], ssc[:], smid[:, 0:1], None, ALU.is_gt), VQ, 1)
            p.w(nc.vector, VQ, v)
            v = p.inc(nc.vector.reduce_sum(scnt[:], scmp[:], axis=AX.X), VQ, 1)
            p.w(nc.tensor, VQ, v)
            p.w(nc.tensor, pfx + "ct")
            cv = p.inc(nc.tensor.matmul(pcnt[:], sE4T[:], scnt[:, 0:1], start=True, stop=True), pfx + "cm", 1)
            p.w(nc.vector, pfx + "cm", cv)
            v = p.inc(nc.vector.tensor_copy(slh[:, 3:4], pcnt[:]), pfx + "ct", 1)
            p.w(nc.vector, pfx + "ct", v)
            return v

        for it in range(34):
            count_pass(slh[:, 2:3])
            v = p.inc(nc.vector.tensor_scalar(smsk[:, 0:1], slh[:, 3:4], 512.0, None, ALU.is_ge), VQ, 1)
            p.w(nc.vector, VQ, v)
            v = p.inc(nc.vector.tensor_scalar(smsk[:, 1:2], slh[:, 3:4], 512.0, None, ALU.is_lt), VQ, 1)
            p.w(nc.vector, VQ, v)
            v = p.inc(nc.vector.select(slh[:, 0:1], smsk[:, 0:1], slh[:, 2:3], slh[:, 0:1]), VQ, 1)
            p.w(nc.vector, VQ, v)
            v = p.inc(nc.vector.select(slh[:, 1:2], smsk[:, 1:2], slh[:, 2:3], slh[:, 1:2]), VQ, 1)
            p.w(nc.vector, VQ, v)
            v = p.inc(nc.vector.tensor_tensor(slh[:, 2:3], slh[:, 0:1], slh[:, 1:2], ALU.add), VQ, 1)
            p.w(nc.vector, VQ, v)
            v = p.inc(nc.vector.tensor_scalar_mul(slh[:, 2:3], slh[:, 2:3], 0.5), VQ, 1)
        count_pass(slh[:, 0:1])
        v = p.inc(nc.vector.tensor_scalar(smsk[:, 0:1], slh[:, 3:4], 512.0, None, ALU.is_equal), VQ, 1)
        p.w(nc.vector, VQ, v)
        v = p.inc(nc.vector.select(slh[:, 4:5], smsk[:, 0:1], slh[:, 0:1], slh[:, 1:2]), VQ, 1)
        p.w(nc.tensor, VQ, v)
        p.w(nc.tensor, pfx + "cp")
        bv = p.inc(nc.tensor.matmul(pbc[:], sE4[:, :], slh[:, 4:5], start=True, stop=True), pfx + "bm", 1)
        p.w(nc.vector, pfx + "bm", bv)
        v = p.inc(nc.vector.tensor_copy(smid[:], pbc[:]), pfx + "cp", 1)
        p.w(nc.vector, pfx + "cp", v)
        v = p.inc(nc.vector.tensor_scalar(scmp[:], ssc[:], smid[:, 0:1], None, ALU.is_gt), VQ, 1)
        p.w(nc.vector, VQ, v)
        v = p.inc(nc.vector.tensor_tensor_scan(spfx[:], sone[:], scmp[:], 0.0, ALU.mult, ALU.add), VQ, 1)
        p.w(nc.sync, VQ, v)
        dv = p.inc(nc.sync.dma_start(out=tmpd, in_=spfx[:, 127:128]), pfx + "d", 16)
        p.w(nc.sync, pfx + "d", dv)
        dv = p.inc(nc.sync.dma_start(out=srow[0:1, :], in_=tmpd.rearrange("(a b) -> a b", a=1)), pfx + "d", 16)
        p.w(nc.vector, pfx + "d", dv)
        vv = 0
        for b in range(4):
            vv = p.inc(nc.vector.tensor_tensor_scan(srow2[0:1, 32 * b:32 * (b + 1)],
                                                    sone[0:1, 0:32], srow[0:1, 32 * b:32 * (b + 1)],
                                                    0.0, ALU.mult, ALU.add), VQ, 1)
        p.w(nc.vector, VQ, vv)
        v = p.inc(nc.vector.tensor_tensor(srow2[0:1, :], srow2[0:1, :], srow[0:1, :], ALU.subtract), VQ, 1)
        p.w(nc.sync, VQ, v)
        dv = p.inc(nc.sync.dma_start(out=tmpd, in_=srow2[0:1, :]), pfx + "d", 16)
        p.w(nc.sync, pfx + "d", dv)
        dv = p.inc(nc.sync.dma_start(out=soff[:], in_=tmpd.rearrange("(a b) -> a b", b=1)), pfx + "d", 16)
        p.w(nc.vector, pfx + "d", dv)
        v = p.inc(nc.vector.scalar_tensor_tensor(spfx[:], spfx[:], soff[:, 0:1], scmp[:],
                                                 ALU.add, ALU.subtract), VQ, 1)
        p.w(nc.vector, VQ, v)
        v = p.inc(nc.vector.tensor_scalar(spfx[:], spfx[:], sboff[:, 0:1], None, ALU.add), VQ, 1)
        p.w(nc.vector, VQ, v)
        v = p.inc(nc.vector.memset(sdst[:], 1.0e9), VQ, 1)
        p.w(nc.vector, VQ, v)
        v = p.inc(nc.vector.tensor_copy(scmpi[:], scmp[:]), VQ, 1)
        p.w(nc.vector, VQ, v)
        v = p.inc(nc.vector.select(sdst[:], scmpi[:], spfx[:], sdst[:]), VQ, 1)
        p.w(nc.vector, VQ, v)
        v = p.inc(nc.vector.tensor_copy(sdstu[:], sdst[:]), VQ, 1)
        spk_v = spk[:].rearrange("p (q f) -> p q f", f=4)
        nc.vector.memset(spk[:], 0.0)
        nc.vector.tensor_copy(spk_v[:, :, 0:1], ssc[:].rearrange("p (q o) -> p q o", o=1))
        p.w(nc.vector, pfx + "g", gv)
        nc.vector.tensor_copy(spk_v[:, :, 1:2], sit[:].rearrange("p (q o) -> p q o", o=1))
        v = p.inc(nc.vector.tensor_scalar(scnt[:].tensor[:, 0:1] if False else scmp[:], sit[:], sboff2[:, 0:1], None, ALU.subtract), VQ, 1)
        p.w(nc.vector, VQ, v)
        v = p.inc(nc.vector.tensor_copy(spk_v[:, :, 2:3], scmp[:].rearrange("p (q o) -> p q o", o=1)), VQ, 1)
        if "tkdbg" in io:
            p.w(nc.sync, VQ, v)
            p.inc(nc.sync.dma_start(out=io["tkdbg"][:, 0:128], in_=scmp[:]), pfx + "dbg", 16)
            p.inc(nc.sync.dma_start(out=io["tkdbg"][:, 128:256], in_=spfx[:]), pfx + "dbg", 16)
            p.inc(nc.sync.dma_start(out=io["tkdbg"][:, 256:384], in_=sdst[:]), pfx + "dbg", 16)
            p.inc(nc.sync.dma_start(out=io["tkdbg"][0:4, 384:392], in_=slh[:]), pfx + "dbg", 16)
            p.inc(nc.sync.dma_start(out=io["tkdbg"][:, 392:393], in_=soff[:, 0:1], allow_slow_non_contiguous=True), pfx + "dbg", 16)
            p.inc(nc.sync.dma_start(out=io["tkdbg"][:, 400:528], in_=sit[:]), pfx + "dbg", 16)
            p.inc(nc.sync.dma_start(out=io["tkdbg"][:, 528:656], in_=ssc[:]), pfx + "dbg", 16)
        p.w(nc.gpsimd, VQ, v)
        bcreg = nc.gpsimd.alloc_register(pfx + "bc")
        nc.gpsimd.reg_mov(bcreg, 2047)
        for q in range(128):
            p.inc(nc.gpsimd.indirect_dma_start(
                out=idxsel, out_offset=bass.IndirectOffsetOnAxis(sdstu[:, q:q + 1], 0),
                in_=spk[:, 4 * q:4 * q + 4], in_offset=None,
                bounds_check=bcreg, oob_is_err=False), pfx + "sca", 16)
        nc.gpsimd.free_register(bcreg)
        p.barrier([pfx + "sca"])
        p.release([pfx])


def emit_gather_rows(p, idxsel, col, src_dram, dst_dram, nrows, width, dt, pfx):
    nc = p.nc
    with ExitStack() as st:
        sidxf = p.sbuf(pfx + "if", [128, 4], F32, st)
        sidxu = p.sbuf(pfx + "iu", [128, 1], U32, st)
        sg = [p.sbuf(f"{pfx}g{i}", [128, width], dt, st) for i in range(2)]
        for rc in range(nrows // 128):
            bi = rc % 2
            GL, GO = f"{pfx}l{bi}", f"{pfx}o{bi}"
            p.w(nc.sync, pfx + "cv")
            dv = p.inc(nc.sync.dma_start(out=sidxf[:], in_=idxsel[rc * 128:(rc + 1) * 128, :]), pfx + "ix", 16)
            p.w(nc.vector, pfx + "ix", dv)
            v = p.inc(nc.vector.tensor_copy(sidxu[:], sidxf[:, col:col + 1]), pfx + "cv", 1)
            p.w(nc.gpsimd, pfx + "cv", v)
            p.w(nc.gpsimd, GO)
            gv = p.inc(nc.gpsimd.indirect_dma_start(
                out=sg[bi][:], out_offset=None,
                in_=src_dram, in_offset=bass.IndirectOffsetOnAxis(sidxu[:, 0:1], 0)), GL, 16)
            p.w(nc.sync, GL, gv)
            p.inc(nc.sync.dma_start(out=dst_dram[rc * 128:(rc + 1) * 128, :], in_=sg[bi][:]), GO, 16)
        p.barrier([pfx + "o0", pfx + "o1"])
        p.release([pfx])


def emit_gather_rows_u32(p, idx_dram, src_dram, dst_dram, nrows, width, dt, pfx):
    nc = p.nc
    with ExitStack() as st:
        sidxu = p.sbuf(pfx + "iu", [128, 1], U32, st)
        sg = [p.sbuf(f"{pfx}g{i}", [128, width], dt, st) for i in range(2)]
        for rc in range(nrows // 128):
            bi = rc % 2
            GL, GO = f"{pfx}l{bi}", f"{pfx}o{bi}"
            p.w(nc.gpsimd, pfx + "ix")
            dv = p.inc(nc.sync.dma_start(out=sidxu[:], in_=idx_dram[rc * 128:(rc + 1) * 128, :]), pfx + "ix", 16)
            p.w(nc.gpsimd, pfx + "ix", dv)
            p.w(nc.gpsimd, GO)
            gv = p.inc(nc.gpsimd.indirect_dma_start(
                out=sg[bi][:], out_offset=None,
                in_=src_dram, in_offset=bass.IndirectOffsetOnAxis(sidxu[:, 0:1], 0)), GL, 16)
            p.w(nc.sync, GL, gv)
            p.inc(nc.sync.dma_start(out=dst_dram[rc * 128:(rc + 1) * 128, :], in_=sg[bi][:]), GO, 16)
        p.barrier([pfx + "o0", pfx + "o1"])
        p.release([pfx])


def emit_mlp(p, C, io, h2f_v):
    nc = p.nc
    I = 8192
    IC = I // 128
    with ExitStack() as stM:
      shh = p.sbuf("mgh", [128, IC * 256], BF16, stM)
      with ExitStack() as stA:
        sh2 = p.sbuf("mgx", [128, KC * 256], BF16, stA)
        sgt = p.sbuf("mgg", [128, 256], F32, stA)
        swgu = [p.sbuf(f"mgw{i}", [128, 2 * 2048], BF16, stA) for i in range(2)]
        pg = [p.psum(f"mgp{i}", [128, 256], F32, stA) for i in range(2)]
        for k in range(KC):
            p.inc(nc.sync.dma_start(out=sh2[:, k * 256:(k + 1) * 256], in_=h2f_v[k, :, :]), "mgxl", 16)
        XLV = p.val("mgxl")
        mm_hist = []
        for m in range(IC):
            bz = m % 2
            GWL = f"mgwl{bz}"
            if m >= 2:
                p.w(nc.gpsimd, "mgmm", mm_hist[m - 2])
            p.inc(nc.gpsimd.dma_start(out=swgu[bz][:, 0:2048], in_=io["wgate_r"][m]), GWL, 16)
            p.inc(nc.gpsimd.dma_start(out=swgu[bz][:, 2048:4096], in_=io["wup_r"][m]), GWL, 16)
            wlv = p.val(GWL)
            p.w(nc.tensor, GWL, wlv)
            if m == 0:
                p.w(nc.tensor, "mgxl", XLV)
            p.w(nc.tensor, "mgsl")
            for k in range(KC):
                ins = nc.tensor.matmul(pg[0][:], swgu[bz][:, k * 128:(k + 1) * 128],
                                       sh2[:, k * 256:(k + 1) * 256],
                                       start=(k == 0), stop=(k == KC - 1))
            p.inc(ins, "mgmm", 1)
            p.w(nc.tensor, "mghh")
            for k in range(KC):
                ins = nc.tensor.matmul(pg[1][:], swgu[bz][:, 2048 + k * 128:2048 + (k + 1) * 128],
                                       sh2[:, k * 256:(k + 1) * 256],
                                       start=(k == 0), stop=(k == KC - 1))
            mmv = p.inc(ins, "mgmm", 1)
            mm_hist.append(mmv)
            p.w(nc.scalar, "mgmm", mmv - 1)
            p.w(nc.scalar, "mghh")
            sv = p.inc(nc.scalar.activation(sgt[:], pg[0][:], AF.Silu), "mgsl", 1)
            p.w(nc.vector, "mgsl", sv)
            p.w(nc.vector, "mgmm", mmv)
            v = p.inc(nc.vector.tensor_tensor(shh[:, m * 256:(m + 1) * 256], sgt[:], pg[1][:],
                                              ALU.mult), "mghh", 1)
        p.barrier(["mghh"])
      with ExitStack() as stB:
        swd = [p.sbuf(f"mgd{i}", [128, 2048], BF16, stB) for i in range(2)]
        sx1 = p.sbuf("mga", [128, D], F32, stB)
        sso = p.sbuf("mgo", [128, D], F32, stB)
        pd = [p.psum(f"mgq{i}", [128, 512], F32, stB) for i in range(8)]
        wd_v = io["wdown"].rearrange("(kc p) m -> kc p m", p=128)
        dm_hist = []
        for m in range(IC):
            bz = m % 2
            DWL = f"mgdl{bz}"
            if m >= 2:
                p.w(nc.gpsimd, "mgdm", dm_hist[m - 2])
            p.inc(nc.gpsimd.dma_start(out=swd[bz][:], in_=wd_v[m]), DWL, 16)
            wlv = p.val(DWL)
            p.w(nc.tensor, DWL, wlv)
            if m == 0:
                p.w(nc.tensor, "mgde")
            for ts2 in range(2):
                for n in range(4):
                    ins = nc.tensor.matmul(pd[ts2 * 4 + n][:], shh[:, m * 256 + ts2 * 128:m * 256 + (ts2 + 1) * 128],
                                           swd[bz][:, n * 512:(n + 1) * 512],
                                           start=(m == 0), stop=(m == IC - 1))
            dmv = p.inc(ins, "mgdm", 1)
            dm_hist.append(dmv)
        DMV = p.val("mgdm")
        for ts2 in range(2):
            ALD = f"mgal{ts2}"
            alv = p.inc(nc.sync.dma_start(out=sx1[:], in_=io["x1own"][ts2 * 128:(ts2 + 1) * 128, :]), ALD, 16)
            for n in range(4):
                p.w(nc.vector, "mgdm", DMV)
                p.w(nc.vector, ALD, alv)
                if n == 0 and ts2 == 1:
                    p.w(nc.vector, "mgoo")
                v = p.inc(nc.vector.tensor_tensor(sso[:, n * 512:(n + 1) * 512], pd[ts2 * 4 + n][:],
                                                  sx1[:, n * 512:(n + 1) * 512], ALU.add), "mgde", 1)
            p.w(nc.sync, "mgde", p.val("mgde"))
            p.inc(nc.sync.dma_start(out=io["procown"][ts2 * 128:(ts2 + 1) * 128, :], in_=sso[:]), "mgoo", 16)
            p.w(nc.sync, "mgde", p.val("mgde"))
        p.barrier(["mgoo"])
        p.release(["mg"])


def emit_gather_own(p, C, io):
    nc = p.nc
    with ExitStack() as st:
        sidxu = p.sbuf("goi", [128, 1], U32, st)
        sg = p.sbuf("gog", [128, D], F32, st)
        sg2 = p.sbuf("gos", [128, 4], F32, st)
        for rc in range(2):
            p.w(nc.sync, "goo")
            dv = p.inc(nc.sync.dma_start(out=sidxu[:], in_=io["ownsel"][rc * 128:(rc + 1) * 128, :]), "goix", 16)
            for (srcd, dstd, wid, stile) in ((io["x1sel"], io["x1own"], D, sg),
                                             (io["selxn"], io["selxo"], D, sg),
                                             (io["idxsel"], io["idxo"], 4, sg2)):
                p.w(nc.gpsimd, "goix", dv)
                p.w(nc.gpsimd, "goo")
                gv = p.inc(nc.gpsimd.indirect_dma_start(
                    out=stile[:, 0:wid], out_offset=None,
                    in_=srcd, in_offset=bass.IndirectOffsetOnAxis(sidxu[:, 0:1], 0)), "gol", 16)
                p.w(nc.sync, "gol", gv)
                p.inc(nc.sync.dma_start(out=dstd[rc * 128:(rc + 1) * 128, :], in_=stile[:, 0:wid]), "goo", 16)
        p.barrier(["goo"])
        p.release(["go"])


def emit_gating(p, C, io):
    nc = p.nc
    with ExitStack() as st:
        spr = p.sbuf("ggp", [128, D], F32, st)
        ssl = p.sbuf("ggs", [128, D], F32, st)
        six = p.sbuf("ggi", [128, 4], F32, st)
        sgg = p.sbuf("ggg", [128, 1], F32, st)
        for rc in range(2):
            p.w(nc.sync, "ggo")
            dv = p.inc(nc.sync.dma_start(out=spr[:], in_=io["procown"][rc * 128:(rc + 1) * 128, :]), "ggl", 16)
            dv = p.inc(nc.sync.dma_start(out=ssl[:], in_=io["selxo"][rc * 128:(rc + 1) * 128, :]), "ggl", 16)
            dv = p.inc(nc.sync.dma_start(out=six[:], in_=io["idxo"][rc * 128:(rc + 1) * 128, :]), "ggl", 16)
            p.w(nc.scalar, "ggl", dv)
            sv = p.inc(nc.scalar.activation(sgg[:], six[:, 0:1], AF.Sigmoid), "ggsg", 1)
            p.w(nc.vector, "ggl", dv)
            p.w(nc.vector, "ggsg", sv)
            v = p.inc(nc.vector.tensor_tensor(spr[:], spr[:], ssl[:], ALU.subtract), "ggv", 1)
            p.w(nc.vector, "ggv", v)
            v = p.inc(nc.vector.tensor_scalar(spr[:], spr[:], sgg[:, 0:1], None, ALU.mult), "ggv", 1)
            p.w(nc.vector, "ggv", v)
            v = p.inc(nc.vector.tensor_tensor(spr[:], spr[:], ssl[:], ALU.add), "ggv", 1)
            p.w(nc.sync, "ggv", v)
            p.inc(nc.sync.dma_start(out=io["gatedo"][rc * 128:(rc + 1) * 128, :], in_=spr[:]), "ggo", 16)
        p.barrier(["ggo"])
        p.release(["gg"])


def emit_mod(p, C, io):
    nc = p.nc
    sbq2 = p.sbuf("sbq2", [128, 16], F32)
    sbk2 = p.sbuf("sbk2", [128, 8], F32)
    sbv2 = p.sbuf("sbv2", [128, 1024], F32)
    sbnd2 = p.sbuf("sbnd2", [128, 16], F32)
    for nm, dst in (("bq2", sbq2), ("bk2", sbk2), ("bv2", sbv2)):
        p.inc(nc.sync.dma_start(out=dst[:], in_=io[nm]), "pre2", 16)
    p.barrier(["pre2"])

    sc_b = io["scall_aps"]
    emit_topk(p, C, io, sc_b, io["idxsel"], "tk")
    emit_gather_rows(p, io["idxsel"], 1, io["x0flat"], io["selx"], S_SEL, D, F32, "gx")
    emit_gather_rows(p, io["idxsel"], 2, io["cosM"], io["costk"], S_SEL, 128, F32, "gc")
    emit_gather_rows(p, io["idxsel"], 2, io["sinM"], io["sintk"], S_SEL, 128, F32, "gs")
    emit_ln(p, io["selx"], io["selxn"], None, None, S_SEL, "c1")
    with ExitStack() as stw:
        sln1w = p.sbuf("sln1w", [128, D], F32, stw)
        v = p.inc(nc.sync.dma_start(out=sln1w[:], in_=io["ln1w"]), "pre3", 16)
        p.w(nc.scalar, "pre3", v)
        p.w(nc.vector, "pre3", v)
        emit_rms(p, io["selxn"], io["hseltok"], sln1w, 1e-6, S_SEL, "c2")
    hself_v = io["hselfm"].rearrange("(c p) t -> c p t", p=128)
    cosf_v = io["cosf2"].rearrange("(c p) t -> c p t", p=128)
    sinf_v = io["sinf2"].rearrange("(c p) t -> c p t", p=128)
    emit_tokmaj_to_fm(p, io["hseltok"], hself_v, C, S_SEL, "c4")
    emit_tokmaj_to_fm(p, io["costk"], cosf_v, C, S_SEL, "c5", nchunk=1)
    emit_tokmaj_to_fm(p, io["sintk"], sinf_v, C, S_SEL, "c6", nchunk=1, dt=F32)
    qf2_v = io["qf2"].rearrange("(c p) t -> c p t", p=128)
    kf2_v = io["kf2"].rearrange("(c p) t -> c p t", p=128)
    of2_v = io["of2"].rearrange("(c p) t -> c p t", p=128)
    emit_qkv_rope(p, C, hself_v, qf2_v, kf2_v, io["vt2"], S_SEL, 16, 8,
                  sbq2, sbk2, sbv2,
                  io["wq2"].rearrange("(kc p) m -> kc p m", p=128),
                  io["wk2"].rearrange("(kc p) m -> kc p m", p=128),
                  io["wv2"].rearrange("(kc p) m -> kc p m", p=128),
                  io["cosf2"], io["sinf2"], 0, "mj")
    emit_bounds(p, C, 16, 8, 2, S_SEL, sbnd2, 0, "me")
    emit_attention(p, C, qf2_v, kf2_v, io["vt2"], of2_v, S_SEL, 16, 8, sbnd2, "ma")
    emit_proj_tok(p, C, of2_v, io["wo2"].rearrange("(kc p) m -> kc p m", p=128),
                  KC, None, io["x1sel"], S_SEL, D, "mo", add_src=io["selxn"])
    emit_gather_own(p, C, io)
    with ExitStack() as stw:
        sln2w = p.sbuf("sln2w", [128, D], F32, stw)
        v = p.inc(nc.sync.dma_start(out=sln2w[:], in_=io["ln2w"]), "pre4", 16)
        p.w(nc.scalar, "pre4", v)
        p.w(nc.vector, "pre4", v)
        emit_rms(p, io["x1own"], io["h2own"], sln2w, 1e-6, 256, "c7")
    h2f_v = io["h2f"].rearrange("(c p) t -> c p t", p=128)
    emit_tokmaj_to_fm(p, io["h2own"], h2f_v, C, 256, "c8")
    emit_mlp(p, C, io, h2f_v)
    emit_gating(p, C, io)
    p.release(["pre2", "pre3", "pre4"])




def emit_combine(p, C, io, copy_first=True):
    nc = p.nc
    if copy_first:
        with ExitStack() as st:
            scp = [p.sbuf(f"cbc{i}", [128, D], F32, st) for i in range(2)]
            for i in range(T // 128):
                bi = i % 2
                CP, CO = f"cbp{bi}", f"cbo{bi}"
                if i >= 2:
                    p.w(nc.sync, CO)
                dv = p.inc(nc.sync.dma_start(out=scp[bi][:], in_=io["xn"][i * 128:(i + 1) * 128, :]), CP, 16)
                p.w(nc.sync, CP, dv)
                p.inc(nc.sync.dma_start(out=io["modw"][i * 128:(i + 1) * 128, :], in_=scp[bi][:]), CO, 16)
            p.barrier(["cbo0", "cbo1"])
            p.release(["cbp", "cbo"])
    with ExitStack() as st:
        sgat = [p.sbuf(f"cbg{i}", [128, D], F32, st) for i in range(2)]
        sif = p.sbuf("cbi", [128, 4], F32, st)
        sdf = p.sbuf("cbd", [128, 1], F32, st)
        sm = p.sbuf("cbm", [128, 1], F32, st)
        sdu = [p.sbuf(f"cbu{i}", [128, 1], U32, st) for i in range(2)]
        sbofft = p.sbuf("cbb2", [128, 1], F32, st)
        p.inc(nc.sync.dma_start(out=sbofft[:], in_=io["bofftok"]), "cbl2", 16)
        BL2 = p.val("cbl2")
        bcreg = nc.gpsimd.alloc_register("cbbc")
        nc.gpsimd.reg_mov(bcreg, T - 1)
        for rc in range(S_SEL // 128):
            bi = rc % 2
            GL, SCT = f"cbgl{bi}", f"cbs{bi}"
            if rc >= 2:
                p.w(nc.sync, SCT)
            dv = p.inc(nc.sync.dma_start(out=sgat[bi][:], in_=io["gated"][rc * 128:(rc + 1) * 128, :]), GL, 16)
            p.w(nc.sync, "cbcv")
            dv2 = p.inc(nc.sync.dma_start(out=sif[:], in_=io["idxsel"][rc * 128:(rc + 1) * 128, :]), GL, 16)
            p.w(nc.vector, GL, p.val(GL))
            if rc == 0:
                p.w(nc.vector, "cbl2", BL2)
            v = p.inc(nc.vector.tensor_scalar(sdf[:], sif[:, 1:2], sbofft[:, 0:1], None, ALU.subtract), "cbv", 1)
            p.w(nc.vector, "cbv", v)
            v = p.inc(nc.vector.tensor_scalar(sm[:], sdf[:], 0.0, None, ALU.is_ge), "cbv", 1)
            p.w(nc.vector, "cbv", v)
            v = p.inc(nc.vector.select(sdf[:], sm[:], sdf[:], sm[:]) if False else
                      nc.vector.scalar_tensor_tensor(sdf[:], sm[:], 1.0e9, sdf[:], ALU.bitwise_not if False else ALU.subtract, ALU.mult) if False else
                      nc.vector.engine_nop(), "cbv", 1)
            p.w(nc.vector, "cbv", v)
            v = p.inc(nc.vector.tensor_scalar(sm[:], sm[:], -1.0, 1.0, ALU.mult, ALU.add), "cbv", 1)
            p.w(nc.vector, "cbv", v)
            v = p.inc(nc.vector.tensor_scalar(sm[:], sm[:], 1.0e9, None, ALU.mult), "cbv", 1)
            p.w(nc.vector, "cbv", v)
            v = p.inc(nc.vector.tensor_tensor(sdf[:], sdf[:], sm[:], ALU.add), "cbv", 1)
            p.w(nc.vector, "cbv", v)
            v = p.inc(nc.vector.tensor_copy(sdu[bi][:], sdf[:]), "cbcv", 1)
            p.w(nc.gpsimd, "cbcv", v)
            p.w(nc.gpsimd, GL, p.val(GL))
            p.inc(nc.gpsimd.indirect_dma_start(
                out=io["modw"], out_offset=bass.IndirectOffsetOnAxis(sdu[bi][:, 0:1], 0),
                in_=sgat[bi][:], in_offset=None,
                bounds_check=bcreg, oob_is_err=False), SCT, 16)
        nc.gpsimd.free_register(bcreg)
        p.barrier(["cbs0", "cbs1"])
        p.release(["cbg", "cbv", "cbcv", "cbs", "cbi", "cbl2", "cbd", "cbu"])
    with ExitStack() as st:
        sx0 = [p.sbuf(f"cbx{i}", [128, D], F32, st) for i in range(2)]
        ss0 = [p.sbuf(f"cbs0_{i}", [128, D], F32, st) for i in range(2)]
        smd = [p.sbuf(f"cbmd{i}", [128, D], F32, st) for i in range(2)]
        sg0 = p.sbuf("cbgsil", [128, D], F32, st)
        for i in range(T // 128):
            bi = i % 2
            LD, SO = f"cbfl{bi}", f"cbfo{bi}"
            if i >= 2:
                p.w(nc.sync, SO)
            p.inc(nc.sync.dma_start(out=sx0[bi][:], in_=io["x0b"][i * 128:(i + 1) * 128, :]), LD, 16)
            p.inc(nc.sync.dma_start(out=ss0[bi][:], in_=io["s0"][i * 128:(i + 1) * 128, :]), LD, 16)
            dv = p.inc(nc.sync.dma_start(out=smd[bi][:], in_=io["modw"][i * 128:(i + 1) * 128, :]), LD, 16)
            p.w(nc.scalar, LD, dv)
            p.w(nc.scalar, "cbfv")
            sv = p.inc(nc.scalar.activation(sg0[:], smd[bi][:], AF.Silu), "cbsl", 1)
            p.w(nc.vector, "cbsl", sv)
            p.w(nc.vector, LD, dv)
            v = p.inc(nc.vector.tensor_tensor(smd[bi][:], ss0[bi][:], sg0[:], ALU.mult), "cbfv", 1)
            p.w(nc.vector, "cbfv", v)
            v = p.inc(nc.vector.tensor_tensor(smd[bi][:], smd[bi][:], sx0[bi][:], ALU.add), "cbfv", 1)
            p.w(nc.sync, "cbfv", v)
            p.inc(nc.sync.dma_start(out=io["x1b"][i * 128:(i + 1) * 128, :], in_=smd[bi][:]), SO, 16)
        p.barrier(["cbfo0", "cbfo1"])
        p.release(["cbf", "cbsl"])



def _bf16(x):
    import ml_dtypes
    return np.asarray(x).astype(ml_dtypes.bfloat16)


def _c(x):
    return np.ascontiguousarray(x)


def _rope_tables_fm(Tn):
    half = np.arange(64, dtype=np.float64) / 64.0
    invf = 1.0 / (10000.0 ** half)
    ang = np.arange(Tn, dtype=np.float64)[:, None] * invf[None, :]
    cos = np.cos(ang).T
    sin = np.sin(ang).T
    cosd = np.concatenate([cos, cos], 0).astype(np.float32)
    sind = np.concatenate([sin, sin], 0).astype(np.float32)
    return _bf16(cosd), _c(sind)


def _rope_tables_tok(Tn):
    half = np.arange(64, dtype=np.float64) / 64.0
    invf = 1.0 / (10000.0 ** half)
    ang = np.arange(Tn, dtype=np.float64)[:, None] * invf[None, :]
    cosM = np.concatenate([np.cos(ang), np.cos(ang)], 1).astype(np.float32)
    sinM = np.concatenate([np.sin(ang), np.sin(ang)], 1).astype(np.float32)
    return _c(cosM), _c(sinM)


def _pmat():
    m = np.zeros((128, 128), np.float32)
    for j in range(64):
        m[64 + j, j] = -1.0
        m[j, 64 + j] = 1.0
    return _bf16(m)


def _tri_masks():
    k = np.arange(128)[:, None]
    q = np.arange(512)[None, :]
    t = np.concatenate([(ks * 128 + k <= q).astype(np.float32) for ks in range(4)], axis=1)
    return _bf16(t)


def common_host_inputs(inputs):
    return {
        "wrt": _c(np.repeat(np.asarray(inputs["L0_ffn_w_router"], np.float32)[None, :], 128, 0)),
        "ident": _bf16(np.eye(128)),
        "identf": np.eye(128, dtype=np.float32),
        "pmat": _pmat(),
        "tri": _tri_masks(),
        "ones_col": _bf16(np.ones((128, 1))),
        "ones_colf": np.ones((128, 1), np.float32),
        "ones_row": np.ones((1, 128), np.float32),
        "consts": _c(np.repeat(np.array([[1e-5, 1e-6, 0.5, 1.5, -0.5, 0, 0, 0]], np.float32), 128, 0)),
    }


def common_io(p):
    return {
        "wrt": p.inp("wrt", [128, D], F32),
        "ident": p.inp("ident", [128, 128], BF16),
        "identf": p.inp("identf", [128, 128], F32),
        "pmat": p.inp("pmat", [128, 128], BF16),
        "tri": p.inp("tri", [128, 4 * 512], BF16),
        "ones_col": p.inp("ones_col", [128, 1], BF16),
        "ones_colf": p.inp("ones_colf", [128, 1], F32),
        "ones_row": p.inp("ones_row", [1, 128], F32),
        "consts": p.inp("consts", [128, 8], F32),
    }


def ab_host_inputs(inputs, core):
    b = core // 2
    x0 = np.asarray(inputs["L0_attn_x"], np.float32)
    cosd, sind = _rope_tables_fm(T)
    d = common_host_inputs(inputs)
    d.update({
        "xb": _c(x0[b]),
        "wq": _c(np.asarray(inputs["L0_attn_wq_w"], np.float32)),
        "wk": _c(np.asarray(inputs["L0_attn_wk_w"], np.float32)),
        "wv": _c(np.asarray(inputs["L0_attn_wv_w"], np.float32)),
        "wo": _c(np.asarray(inputs["L0_attn_o_w"], np.float32)),
        "bq": _c(np.asarray(inputs["L0_attn_wq_b"], np.float32).reshape(16, 128).T),
        "bk": _c(np.asarray(inputs["L0_attn_wk_b"], np.float32).reshape(4, 128).T),
        "bv": _c(np.repeat(np.asarray(inputs["L0_attn_wv_b"], np.float32)[None, :], 128, 0)),
        "bo": _c(np.repeat(np.asarray(inputs["L0_attn_o_b"], np.float32)[None, :], 128, 0)),
        "cosd": cosd,
        "sind": sind,
    })
    return d


def build_ab():
    p = Prog()
    io = common_io(p)
    io.update({
        "xb": p.inp("xb", [T, D], F32),
        "wq": p.inp("wq", [D, D], F32),
        "wk": p.inp("wk", [D, 512], F32),
        "wv": p.inp("wv", [D, 512], F32),
        "wo": p.inp("wo", [D, D], F32),
        "bq": p.inp("bq", [128, 16], F32),
        "bk": p.inp("bk", [128, 4], F32),
        "bv": p.inp("bv", [128, 512], F32),
        "bo": p.inp("bo", [128, D], F32),
        "cosd": p.inp("cosd", [128, T], BF16),
        "sind": p.inp("sind", [128, T], F32),
        "xn": p.outp("xn", [T, D], F32),
        "sc": p.outp("sc", [T, 1], F32),
        "s0": p.outp("s0", [T, D], F32),
        "xnf": p.dram("xnf", [D, T], BF16),
        "qf": p.dram("qf", [D, T], BF16),
        "kf": p.dram("kf", [512, T], BF16),
        "vt": p.dram("vt", [T, 512], BF16),
        "of": p.dram("of", [D, T], BF16),
    })
    p.start()
    C = load_common(p, io)
    emit_ab(p, C, io)
    return p


def mod_host_inputs(inputs, core, scall):
    x0 = np.asarray(inputs["L0_attn_x"], np.float32)
    cosM, sinM = _rope_tables_tok(T)
    e4 = np.zeros((4, 128), np.float32)
    for b in range(4):
        e4[b, 32 * b:32 * (b + 1)] = 0.0
        e4[b, :] = (np.arange(128) // 32 == b).astype(np.float32)
    boff512 = (np.arange(128) // 32 * 512).astype(np.float32).reshape(128, 1)
    boff4096 = (np.arange(128) // 32 * 4096).astype(np.float32).reshape(128, 1)
    wg = np.asarray(inputs["L0_ffn_w_gate"], np.float32)
    wu = np.asarray(inputs["L0_ffn_w_up"], np.float32)
    wgate_r = _c(wg.reshape(16, 128, 64, 128).transpose(2, 1, 0, 3).reshape(64, 128, 2048))
    wup_r = _c(wu.reshape(16, 128, 64, 128).transpose(2, 1, 0, 3).reshape(64, 128, 2048))
    d = common_host_inputs(inputs)
    d.update({
        "x0flat": _c(x0.reshape(B * T, D)),
        "scall": _c(np.asarray(scall, np.float32)),
        "cosM": cosM, "sinM": sinM,
        "e4": _c(e4),
        "e4t": _c(e4.T),
        "boff512": boff512, "boff4096": boff4096,
        "ownsel": _c((np.arange(256, dtype=np.uint32) + core * 256).reshape(256, 1)),
        "ln1w": _c(np.repeat(np.asarray(inputs["L0_ffn_ln1_w"], np.float32)[None, :], 128, 0)),
        "ln2w": _c(np.repeat(np.asarray(inputs["L0_ffn_ln2_w"], np.float32)[None, :], 128, 0)),
        "wq2": _c(np.asarray(inputs["L0_ffn_wq"], np.float32)),
        "wk2": _c(np.asarray(inputs["L0_ffn_wk"], np.float32)),
        "wv2": _c(np.asarray(inputs["L0_ffn_wv"], np.float32)),
        "wo2": _c(np.asarray(inputs["L0_ffn_wo"], np.float32)),
        "bq2": _c(np.asarray(inputs["L0_ffn_bq"], np.float32).reshape(16, 128).T),
        "bk2": _c(np.asarray(inputs["L0_ffn_bk"], np.float32).reshape(8, 128).T),
        "bv2": _c(np.repeat(np.asarray(inputs["L0_ffn_bv"], np.float32)[None, :], 128, 0)),
        "wgate_r": wgate_r, "wup_r": wup_r,
        "wdown": _c(np.asarray(inputs["L0_ffn_w_down"], np.float32)),
    })
    return d


def build_mod():
    p = Prog()
    io = common_io(p)
    io.update({
        "x0flat": p.inp("x0flat", [B * T, D], F32),
        "scall": p.inp("scall", [4, T], F32),
        "cosM": p.inp("cosM", [T, 128], F32),
        "sinM": p.inp("sinM", [T, 128], F32),
        "e4": p.inp("e4", [4, 128], F32),
        "e4t": p.inp("e4t", [128, 4], F32),
        "boff512": p.inp("boff512", [128, 1], F32),
        "boff4096": p.inp("boff4096", [128, 1], F32),
        "ownsel": p.inp("ownsel", [256, 1], U32),
        "ln1w": p.inp("ln1w", [128, D], F32),
        "ln2w": p.inp("ln2w", [128, D], F32),
        "wq2": p.inp("wq2", [D, D], F32),
        "wk2": p.inp("wk2", [D, 1024], F32),
        "wv2": p.inp("wv2", [D, 1024], F32),
        "wo2": p.inp("wo2", [D, D], F32),
        "bq2": p.inp("bq2", [128, 16], F32),
        "bk2": p.inp("bk2", [128, 8], F32),
        "bv2": p.inp("bv2", [128, 1024], F32),
        "wgate_r": p.inp("wgate_r", [64, 128, 2048], F32),
        "wup_r": p.inp("wup_r", [64, 128, 2048], F32),
        "wdown": p.inp("wdown", [8192, D], F32),
        "idxsel": p.outp("idxsel", [S_SEL, 4], F32),
        "gatedo": p.outp("gatedo", [256, D], F32),
        "selx": p.dram("selx", [S_SEL, D], F32),
        "selxn": p.dram("selxn", [S_SEL, D], F32),
        "hseltok": p.dram("hseltok", [S_SEL, D], F32),
        "hselfm": p.dram("hselfm", [D, S_SEL], BF16),
        "costk": p.dram("costk", [S_SEL, 128], F32),
        "sintk": p.dram("sintk", [S_SEL, 128], F32),
        "cosf2": p.dram("cosf2", [128, S_SEL], BF16),
        "sinf2": p.dram("sinf2", [128, S_SEL], F32),
        "qf2": p.dram("qf2", [D, S_SEL], BF16),
        "kf2": p.dram("kf2", [1024, S_SEL], BF16),
        "vt2": p.dram("vt2", [S_SEL, 1024], BF16),
        "of2": p.dram("of2", [D, S_SEL], BF16),
        "x1sel": p.dram("x1sel", [S_SEL, D], F32),
        "x1own": p.dram("x1own", [256, D], F32),
        "selxo": p.dram("selxo", [256, D], F32),
        "idxo": p.dram("idxo", [256, 4], F32),
        "h2own": p.dram("h2own", [256, D], F32),
        "h2f": p.dram("h2f", [D, 256], BF16),
        "procown": p.dram("procown", [256, D], F32),
    })
    p.start()
    C = load_common(p, io)
    io["scall_aps"] = [io["scall"][b, :].rearrange("(jp q) -> jp q", q=128) for b in range(4)]
    emit_mod(p, C, io)
    return p


def comb_host_inputs(inputs, core, xn_b, s0_b, gated_all, idxsel):
    b = core // 2
    x0 = np.asarray(inputs["L0_attn_x"], np.float32)
    d = common_host_inputs(inputs)
    d.update({
        "x0b": _c(x0[b]),
        "xn_in": _c(np.asarray(xn_b, np.float32)),
        "s0_in": _c(np.asarray(s0_b, np.float32)),
        "gated": _c(np.asarray(gated_all, np.float32)),
        "idxsel_in": _c(np.asarray(idxsel, np.float32)),
        "bofftok": np.full((128, 1), b * 4096, np.float32),
    })
    return d


def build_comb():
    p = Prog()
    io = common_io(p)
    io.update({
        "x0b": p.inp("x0b", [T, D], F32),
        "xn": p.inp("xn_in", [T, D], F32),
        "s0": p.inp("s0_in", [T, D], F32),
        "gated": p.inp("gated", [S_SEL, D], F32),
        "idxsel": p.inp("idxsel_in", [S_SEL, 4], F32),
        "bofftok": p.inp("bofftok", [128, 1], F32),
        "x1b": p.outp("x1b", [T, D], F32),
        "modw": p.dram("modw", [T, D], F32),
    })
    p.start()
    C = load_common(p, io)
    p.barrier(["pre"])
    emit_combine(p, C, io)
    return p


_PROGS = {}


def _prog(name, builder):
    if name not in _PROGS:
        _PROGS[name] = builder()
    return _PROGS[name]


def kernel_unfused(**inputs):
    inputs = {k: np.asarray(v) for k, v in inputs.items()}

    pab = _prog("ab", build_ab)
    res = run_bass_kernel_spmd(pab.nc, [ab_host_inputs(inputs, c) for c in range(8)],
                               core_ids=list(range(8)))
    xn_b = [res.results[2 * b]["xn"] for b in range(4)]
    s0_b = [res.results[2 * b]["s0"] for b in range(4)]
    scall = np.stack([res.results[2 * b]["sc"][:, 0] for b in range(4)])

    pc = _prog("mod", build_mod)
    res2 = run_bass_kernel_spmd(pc.nc, [mod_host_inputs(inputs, c, scall) for c in range(8)],
                                core_ids=list(range(8)))
    gated_all = np.concatenate([res2.results[c]["gatedo"] for c in range(8)], axis=0)
    idxsel = res2.results[0]["idxsel"]

    pd_ = _prog("comb", build_comb)
    res3 = run_bass_kernel_spmd(
        pd_.nc, [comb_host_inputs(inputs, c, xn_b[c // 2], s0_b[c // 2], gated_all, idxsel)
                 for c in range(8)], core_ids=list(range(8)))
    x1_b = [res3.results[2 * b]["x1b"] for b in range(4)]

    pe = _prog("l1", build_l1)
    res4 = run_bass_kernel_spmd(pe.nc, [l1_host_inputs(inputs, c, x1_b[c // 2]) for c in range(8)],
                                core_ids=list(range(8)))
    h_b = [np.concatenate([res4.results[2 * b]["hblk"], res4.results[2 * b + 1]["hblk"]], axis=0)
           for b in range(4)]

    pf = _prog("final", build_final)
    in5 = []
    for c in range(8):
        b, par = c // 2, c % 2
        x1o = x1_b[b][par * 2048:(par + 1) * 2048]
        hmine = h_b[b][:, par * 2048:(par + 1) * 2048]
        in5.append(final_host_inputs(inputs, c, x1o, hmine))
    res5 = run_bass_kernel_spmd(pf.nc, in5, core_ids=list(range(8)))
    out = np.zeros((B, T, D), np.float32)
    for c in range(8):
        b, par = c // 2, c % 2
        out[b, par * 2048:(par + 1) * 2048] = res5.results[c]["outb"]
    return out




def emit_proj_fm(p, C, src_v, w_v, dst_v, Tn, nout, pfx):
    nc = p.nc
    MC = nout // 128
    with ExitStack() as st:
        sw = p.sbuf(pfx + "w", [128, KC * nout], BF16, st)
        sxf = [p.sbuf(f"{pfx}x{i}", [128, KC * 512], BF16, st) for i in range(2)]
        so = [p.sbuf(f"{pfx}o{i}", [128, 512], BF16, st) for i in range(2)]
        pk = [p.psum(f"{pfx}p{i}", [128, 512], F32, st) for i in range(2)]
        for k in range(KC):
            p.inc(nc.gpsimd.dma_start(out=sw[:, k * nout:(k + 1) * nout], in_=w_v[k]), pfx + "wl", 16)
        WL = p.val(pfx + "wl")
        for tb in range(Tn // 512):
            bx = tb % 2
            if tb >= 2:
                p.w(nc.sync, pfx + "mm")
            for k in range(KC):
                p.inc(nc.sync.dma_start(out=sxf[bx][:, k * 512:(k + 1) * 512],
                                        in_=src_v[k, :, tb * 512:(tb + 1) * 512]), pfx + "ld", 16)
            ldv = p.val(pfx + "ld")
            for m in range(MC):
                pidx = m % 2
                p.w(nc.tensor, pfx + "ld", ldv)
                if tb == 0 and m == 0:
                    p.w(nc.tensor, pfx + "wl", WL)
                p.w(nc.tensor, pfx + "ev")
                for k in range(KC):
                    ins = nc.tensor.matmul(pk[pidx][:], sw[:, k * nout + m * 128:k * nout + (m + 1) * 128],
                                           sxf[bx][:, k * 512:(k + 1) * 512],
                                           start=(k == 0), stop=(k == KC - 1))
                mmv = p.inc(ins, pfx + "mm", 1)
                p.w(nc.vector, pfx + "mm", mmv)
                p.w(nc.vector, f"{pfx}so{pidx}")
                v = p.inc(nc.vector.tensor_copy(so[pidx][:], pk[pidx][:]), pfx + "ev", 1)
                p.w(nc.sync, pfx + "ev", v)
                p.inc(nc.sync.dma_start(out=dst_v[m, :, tb * 512:(tb + 1) * 512],
                                        in_=so[pidx][:]), f"{pfx}so{pidx}", 16)
        p.barrier([pfx + "so0", pfx + "so1"])
        p.release([pfx])


def emit_scan(p, C, io, xpf_v, hblk_v):
    nc = p.nc
    GATE_F = [AF.Exp, AF.Sigmoid, AF.Sigmoid, AF.Tanh]
    with ExitStack() as st:
        swg = p.sbuf("scw", [128, KC * 2048], BF16, st)
        sbg = p.sbuf("scb", [128, 32], F32, st)
        srw = p.sbuf("scr", [128, 4], F32, st)
        sxf = [p.sbuf(f"scx{i}", [128, KC * 512], BF16, st) for i in range(2)]
        sg = p.sbuf("scg", [128, 16 * 512], F32, st)
        scn = p.sbuf("scc", [128, 4 * 512], F32, st)
        scsq = p.sbuf("scq", [128, 512], F32, st)
        scry = p.sbuf("scy", [128, 8], F32, st)
        srms = p.sbuf("scm", [1, 1024], F32, st)
        srstd = p.sbuf("scs", [128, 512], F32, st)
        sth = p.sbuf("sct", [128, 512], F32, st)
        sho = [p.sbuf(f"sch{i}", [128, 512], BF16, st) for i in range(2)]
        pk = [p.psum(f"scp{i}", [128, 512], F32, st) for i in range(2)]
        pnm = p.psum("scn2", [1, 512], F32, st)
        pbc = p.psum("scb2", [128, 512], F32, st)

        p.inc(nc.sync.dma_start(out=sbg[:], in_=io["bgu"]), "scpre", 16)
        p.inc(nc.sync.dma_start(out=srw[:], in_=io["rmsw"]), "scpre", 16)
        PRE = p.val("scpre")
        p.w(nc.vector, "scpre", PRE)
        v0 = p.inc(nc.vector.memset(scry[:], 0.0), "scv", 1)

        wgu_v = io["wgu"].rearrange("(kc p) g -> kc p g", p=128)
        for u in range(2):
            p.w(nc.gpsimd, "scmm")
            for k in range(KC):
                p.inc(nc.gpsimd.dma_start(out=swg[:, k * 2048:(k + 1) * 2048],
                                          in_=wgu_v[k, :, u * 2048:(u + 1) * 2048]), "scwl", 16)
            WL = p.val("scwl")
            for tb in range(T // 512):
                bx = tb % 2
                if tb >= 2 or u > 0:
                    p.w(nc.sync, "scmm")
                for k in range(KC):
                    p.inc(nc.sync.dma_start(out=sxf[bx][:, k * 512:(k + 1) * 512],
                                            in_=xpf_v[k, :, tb * 512:(tb + 1) * 512]), f"scl{bx}", 16)
                ldv = p.val(f"scl{bx}")
                for tc in range(16):
                    pidx = tc % 2
                    p.w(nc.tensor, f"scl{bx}", ldv)
                    p.w(nc.tensor, "scwl", WL)
                    p.w(nc.tensor, "scev")
                    for k in range(KC):
                        ins = nc.tensor.matmul(pk[pidx][:], swg[:, k * 2048 + tc * 128:k * 2048 + (tc + 1) * 128],
                                               sxf[bx][:, k * 512:(k + 1) * 512],
                                               start=(k == 0), stop=(k == KC - 1))
                    mmv = p.inc(ins, "scmm", 1)
                    p.w(nc.scalar, "scmm", mmv)
                    p.w(nc.scalar, "scbv")
                    p.w(nc.scalar, "sche")
                    p.inc(nc.scalar.activation(sg[:, tc * 512:(tc + 1) * 512], pk[pidx][:],
                                               GATE_F[tc // 4], bias=sbg[:, u * 16 + tc:u * 16 + tc + 1]), "scev", 1)
                EVV = p.val("scev")
                p.w(nc.vector, "scev", EVV)
                v = p.inc(nc.vector.tensor_tensor(sg[:, 12 * 512:16 * 512], sg[:, 0:4 * 512],
                                                  sg[:, 12 * 512:16 * 512], ALU.mult), "scbv", 1)
                for cc in range(4):
                    p.w(nc.vector, "scbv", v)
                    vs = p.inc(nc.vector.tensor_tensor_scan(scn[:, cc * 512:(cc + 1) * 512],
                                                            sg[:, (4 + cc) * 512:(5 + cc) * 512],
                                                            sg[:, (12 + cc) * 512:(13 + cc) * 512],
                                                            scry[:, u * 4 + cc:u * 4 + cc + 1],
                                                            ALU.mult, ALU.add), "scbv", 1)
                    p.w(nc.vector, "scbv", vs)
                    vs = p.inc(nc.vector.tensor_copy(scry[:, u * 4 + cc:u * 4 + cc + 1],
                                                     scn[:, cc * 512 + 511:cc * 512 + 512]), "scbv", 1)
                for cc in range(4):
                    p.w(nc.scalar, "scbv", p.val("scbv"))
                    p.w(nc.scalar, "scn3")
                    sqv = p.inc(nc.scalar.activation(scsq[:], scn[:, cc * 512:(cc + 1) * 512], AF.Square), "scsq", 1)
                    p.w(nc.tensor, "scsq", sqv)
                    p.w(nc.tensor, "scnm")
                    p.inc(nc.tensor.matmul(pnm[:], C["sonesf"][:, 0:1], scsq[:],
                                           start=(cc == 0), stop=(cc == 3)), "scn3", 1)
                if "scdbg" in io and u == 0 and tb == 0:
                    p.w(nc.sync, "scbv", p.val("scbv"))
                    p.inc(nc.sync.dma_start(out=io["scdbg"][:, 0:16 * 512], in_=sg[:]), "scdg", 16)
                    p.inc(nc.sync.dma_start(out=io["scdbg"][:, 16 * 512:20 * 512], in_=scn[:]), "scdg", 16)
                NM3 = p.val("scn3")
                p.w(nc.vector, "scn3", NM3)
                v = p.inc(nc.vector.tensor_scalar(srms[0:1, 0:512], pnm[0:1, :], 1.0 / 512.0, 1e-6,
                                                  ALU.mult, ALU.add), "scnm", 1)
                p.w(nc.scalar, "scnm", v)
                sqv = p.inc(nc.scalar.activation(srms[0:1, 512:1024], srms[0:1, 0:512], AF.Sqrt), "scsq", 1)
                p.w(nc.vector, "scsq", sqv)
                v = p.inc(nc.vector.reciprocal(srms[0:1, 512:1024], srms[0:1, 512:1024]), "scnm", 1)
                p.w(nc.tensor, "scnm", v)
                p.w(nc.tensor, "scbc")
                bv = p.inc(nc.tensor.matmul(pbc[:], C["sonesr"][0:1, :], srms[0:1, 512:1024],
                                            start=True, stop=True), "scb3", 1)
                p.w(nc.vector, "scb3", bv)
                v = p.inc(nc.vector.tensor_copy(srstd[:], pbc[:]), "scbc", 1)
                for cc in range(4):
                    hb = cc % 2
                    p.w(nc.vector, "scbc", v)
                    p.w(nc.vector, "scth")
                    vv = p.inc(nc.vector.scalar_tensor_tensor(sth[:], scn[:, cc * 512:(cc + 1) * 512],
                                                              srw[:, cc:cc + 1], srstd[:],
                                                              ALU.mult, ALU.mult), "scbv", 1)
                    p.w(nc.scalar, "scbv", vv)
                    p.w(nc.scalar, f"scho{hb}")
                    sqv = p.inc(nc.scalar.activation(sth[:], sth[:], AF.Tanh), "scth", 1)
                    if "scdbg2" in io and u == 0 and tb == 0 and cc == 0:
                        p.w(nc.sync, "scth", sqv)
                        p.inc(nc.sync.dma_start(out=io["scdbg2"][:, 0:512], in_=srstd[:]), "scdg2", 16)
                        p.inc(nc.sync.dma_start(out=io["scdbg2"][:, 512:1024], in_=sth[:]), "scdg2", 16)
                    p.w(nc.vector, "scth", sqv)
                    p.w(nc.vector, f"scho{hb}")
                    vv = p.inc(nc.vector.tensor_tensor(sho[hb][:], sg[:, (8 + cc) * 512:(9 + cc) * 512],
                                                       sth[:], ALU.mult), "sche", 1)
                    p.w(nc.sync, "sche", vv)
                    p.inc(nc.sync.dma_start(out=hblk_v[u * 4 + cc, :, tb * 512:(tb + 1) * 512],
                                            in_=sho[hb][:]), f"scho{hb}", 16)
        p.barrier(["scho0", "scho1"])
        p.release(["sc"])


def l1_host_inputs(inputs, core, x1_b):
    b, par = core // 2, core % 2
    d = common_host_inputs(inputs)
    win_t = np.asarray(inputs["L1_attn_W_in"], np.float32).T
    wgt = np.asarray(inputs["L1_attn_W_gate"], np.float32).T
    bg = np.asarray(inputs["L1_attn_b_gate"], np.float32)
    d.update({
        "x1b": _c(np.asarray(x1_b, np.float32)),
        "wint": _c(win_t),
        "wgu": _c(wgt[:, par * 4096:(par + 1) * 4096]),
        "bgu": _c(bg[par * 4096:(par + 1) * 4096].reshape(32, 128).T),
        "rmsw": _c(np.asarray(inputs["L1_attn_rms_w"], np.float32).reshape(4, 128).T),
    })
    return d


def build_l1():
    p = Prog()
    io = common_io(p)
    io.update({
        "x1b": p.inp("x1b", [T, D], F32),
        "wint": p.inp("wint", [D, D], F32),
        "wgu": p.inp("wgu", [D, 4096], F32),
        "bgu": p.inp("bgu", [128, 32], F32),
        "rmsw": p.inp("rmsw", [128, 4], F32),
        "hblk": p.outp("hblk", [1024, T], BF16),
        "xn1": p.dram("xn1", [T, D], F32),
        "xn1f": p.dram("xn1f", [D, T], BF16),
        "xpf": p.dram("xpf", [D, T], BF16),
    })
    p.start()
    C = load_common(p, io)
    p.barrier(["pre"])
    xn1f_v = io["xn1f"].rearrange("(c p) t -> c p t", p=128)
    xpf_v = io["xpf"].rearrange("(c p) t -> c p t", p=128)
    hblk_v = io["hblk"].rearrange("(c p) t -> c p t", p=128)
    emit_ln(p, io["x1b"], io["xn1"], None, None, T, "e1")
    emit_tokmaj_to_fm(p, io["xn1"], xn1f_v, C, T, "e2")
    emit_proj_fm(p, C, xn1f_v, io["wint"].rearrange("(kc p) m -> kc p m", p=128),
                 xpf_v, T, D, "e3")
    emit_scan(p, C, io, xpf_v, hblk_v)
    return p




def emit_router(p, C, io, slog, sw8):
    nc = p.nc
    with ExitStack() as st:
        sxr = [p.sbuf(f"fr{i}", [128, D], F32, st) for i in range(2)]
        sxt32 = p.sbuf("frt", [128, D], F32, st)
        swr = p.sbuf("frw", [128, 16 * 8], F32, st)
        sbr = p.sbuf("frb", [128, 8], F32, st)
        srb = p.sbuf("frb2", [128, 8], F32, st)
        stmp = p.sbuf("frtm", [128, 48], F32, st)
        ptp = [p.psum(f"frp{i}", [128, 512], F32, st) for i in range(2)]
        plog = p.psum("frl", [128, 8], F32, st)
        wr_v = io["wr"].rearrange("(kc p) e -> kc p e", p=128)
        for k in range(KC):
            p.inc(nc.sync.dma_start(out=swr[:, k * 8:(k + 1) * 8], in_=wr_v[k]), "frwl", 16)
        p.inc(nc.sync.dma_start(out=sbr[:], in_=io["brrep"]), "frwl", 16)
        p.inc(nc.sync.dma_start(out=srb[:], in_=io["rbrep"]), "frwl", 16)
        WL = p.val("frwl")

        for i in range(16):
            bi = i % 2
            LD = f"frld{bi}"
            if i >= 2:
                p.w(nc.sync, "frtp", 4 * (i - 1))
            ldv = p.inc(nc.sync.dma_start(out=sxr[bi][:], in_=io["xn1o"][i * 128:(i + 1) * 128, :]), LD, 16)
            p.w(nc.tensor, LD, ldv)
            if i == 0:
                p.w(nc.tensor, "frwl", WL)
            for g4 in range(4):
                pb = g4 % 2
                if i >= 1 or g4 >= 2:
                    p.w(nc.tensor, "frev", p.val("frev") - (1 if g4 >= 2 or i == 0 else 0) if False else max(0, 4 * i + g4 - 1))
                for c in range(4):
                    ins = nc.tensor.transpose(ptp[pb][:, c * 128:(c + 1) * 128],
                                              sxr[bi][:, (g4 * 4 + c) * 128:(g4 * 4 + c + 1) * 128],
                                              C["sidentf"][:])
                tv = p.inc(ins, "frtp", 1)
                p.w(nc.vector, "frtp", tv)
                p.w(nc.vector, "frmm")
                p.inc(nc.vector.tensor_copy(sxt32[:, g4 * 512:(g4 + 1) * 512], ptp[pb][:]), "frev", 1)
            EVV = p.val("frev")
            p.w(nc.tensor, "frev", EVV)
            p.w(nc.tensor, "frlv")
            for k in range(KC):
                ins = nc.tensor.matmul(plog[:], sxt32[:, k * 128:(k + 1) * 128], swr[:, k * 8:(k + 1) * 8],
                                       start=(k == 0), stop=(k == KC - 1))
            mmv = p.inc(ins, "frmm", 1)
            p.w(nc.vector, "frmm", mmv)
            v = p.inc(nc.vector.tensor_tensor(slog[:, i * 8:(i + 1) * 8], plog[:], sbr[:], ALU.add), "frlv", 1)
            p.w(nc.vector, "frlv", v)
            v = p.inc(nc.vector.tensor_tensor(slog[:, i * 8:(i + 1) * 8], slog[:, i * 8:(i + 1) * 8],
                                              srb[:], ALU.add), "frlv", 1)
            sl = slog[:, i * 8:(i + 1) * 8]
            p.w(nc.vector, "frlv", v)
            v = p.inc(nc.vector.reduce_max(stmp[:, 0:1], sl, axis=AX.X), "frlv", 1)
            p.w(nc.vector, "frlv", v)
            v = p.inc(nc.vector.tensor_scalar(stmp[:, 8:16], sl, stmp[:, 0:1], None, ALU.is_equal), "frlv", 1)
            p.w(nc.vector, "frlv", v)
            v = p.inc(nc.vector.scalar_tensor_tensor(stmp[:, 16:24], stmp[:, 8:16], -1.0e9, sl,
                                                     ALU.mult, ALU.add), "frlv", 1)
            p.w(nc.vector, "frlv", v)
            v = p.inc(nc.vector.reduce_max(stmp[:, 1:2], stmp[:, 16:24], axis=AX.X), "frlv", 1)
            p.w(nc.vector, "frlv", v)
            v = p.inc(nc.vector.tensor_scalar(stmp[:, 24:32], stmp[:, 16:24], stmp[:, 1:2], None,
                                              ALU.is_equal), "frlv", 1)
            p.w(nc.scalar, "frlv", v)
            sv = p.inc(nc.scalar.activation(stmp[:, 2:4], stmp[:, 0:2], AF.Sigmoid), "frsg", 1)
            p.w(nc.vector, "frsg", sv)
            v = p.inc(nc.vector.tensor_tensor(stmp[:, 4:5], stmp[:, 2:3], stmp[:, 3:4], ALU.add), "frlv", 1)
            p.w(nc.vector, "frlv", v)
            v = p.inc(nc.vector.reciprocal(stmp[:, 4:5], stmp[:, 4:5]), "frlv", 1)
            p.w(nc.vector, "frlv", v)
            v = p.inc(nc.vector.tensor_tensor(stmp[:, 2:3], stmp[:, 2:3], stmp[:, 4:5], ALU.mult), "frlv", 1)
            p.w(nc.vector, "frlv", v)
            v = p.inc(nc.vector.tensor_tensor(stmp[:, 3:4], stmp[:, 3:4], stmp[:, 4:5], ALU.mult), "frlv", 1)
            p.w(nc.vector, "frlv", v)
            v = p.inc(nc.vector.tensor_scalar(sw8[:, i * 8:(i + 1) * 8], stmp[:, 8:16], stmp[:, 2:3], None,
                                              ALU.mult), "frlv", 1)
            p.w(nc.vector, "frlv", v)
            v = p.inc(nc.vector.scalar_tensor_tensor(sw8[:, i * 8:(i + 1) * 8], stmp[:, 24:32], stmp[:, 3:4],
                                                     sw8[:, i * 8:(i + 1) * 8], ALU.mult, ALU.add), "frlv", 1)
        p.barrier(["frlv"])
        p.release(["fr"])


def emit_moe(p, C, io, sw8):
    nc = p.nc
    H = 1408
    MC = H // 128
    with ExitStack() as st:
        swge = p.sbuf("mew", [128, MC * 2048], BF16, st)
        swue = p.sbuf("meu", [128, MC * 2048], BF16, st)
        swde = p.sbuf("med", [128, MC * 2048], BF16, st)
        sxf = [p.sbuf(f"mex{i}", [128, KC * 512], BF16, st) for i in range(2)]
        shh = p.sbuf("meh", [128, MC * 512], BF16, st)
        sgt = p.sbuf("meg", [128, 512], F32, st)
        sacc = [p.sbuf("mea0", [128, 2048], F32, st)]
        pg = [p.psum(f"mep{i}", [128, 512], F32, st) for i in range(2)]
        pd = [p.psum(f"meq{i}", [128, 512], F32, st) for i in range(4)]
        xn1of_v = io["xn1of"].rearrange("(c p) t -> c p t", p=128)
        slot_cnt = {}
        for e in range(9):
            EW = f"mewl{e % 2}"
            p.w(nc.gpsimd, "memm")
            p.w(nc.gpsimd, "medm")
            for m in range(MC):
                p.inc(nc.gpsimd.dma_start(out=swge[:, m * 2048:(m + 1) * 2048], in_=io["wge_r"][e, m]), EW, 16)
                p.inc(nc.gpsimd.dma_start(out=swue[:, m * 2048:(m + 1) * 2048], in_=io["wue_r"][e, m]), EW, 16)
                p.inc(nc.gpsimd.dma_start(out=swde[:, m * 2048:(m + 1) * 2048],
                                          in_=io["wde"].rearrange("(e mc p) d -> e mc p d", e=9, p=128)[e, m]), EW, 16)
            WLV = p.val(EW)
            for tb4 in range(4):
                bx = tb4 % 2
                p.w(nc.sync, "memm")
                for k in range(KC):
                    p.inc(nc.sync.dma_start(out=sxf[bx][:, k * 512:(k + 1) * 512],
                                            in_=xn1of_v[k, :, tb4 * 512:(tb4 + 1) * 512]), f"mel{bx}", 16)
                ldv = p.val(f"mel{bx}")
                for m in range(MC):
                    p.w(nc.tensor, f"mel{bx}", ldv)
                    if tb4 == 0 and m == 0:
                        p.w(nc.tensor, EW, WLV)
                    p.w(nc.tensor, "mesl")
                    for k in range(KC):
                        ins = nc.tensor.matmul(pg[0][:], swge[:, m * 2048 + k * 128:m * 2048 + (k + 1) * 128],
                                               sxf[bx][:, k * 512:(k + 1) * 512],
                                               start=(k == 0), stop=(k == KC - 1))
                    p.inc(ins, "memm", 1)
                    p.w(nc.tensor, "mehh")
                    for k in range(KC):
                        ins = nc.tensor.matmul(pg[1][:], swue[:, m * 2048 + k * 128:m * 2048 + (k + 1) * 128],
                                               sxf[bx][:, k * 512:(k + 1) * 512],
                                               start=(k == 0), stop=(k == KC - 1))
                    mmv = p.inc(ins, "memm", 1)
                    p.w(nc.scalar, "memm", mmv - 1)
                    p.w(nc.scalar, "mehh")
                    sv = p.inc(nc.scalar.activation(sgt[:], pg[0][:], AF.Silu), "mesl", 1)
                    p.w(nc.vector, "mesl", sv)
                    p.w(nc.vector, "memm", mmv)
                    p.w(nc.vector, "medm")
                    p.inc(nc.vector.tensor_tensor(shh[:, m * 512:(m + 1) * 512], sgt[:], pg[1][:],
                                                  ALU.mult), "mehh", 1)
                HHV = p.val("mehh")
                for ts in range(4):
                    ba = 0
                    p.w(nc.tensor, "mehh", HHV)
                    p.w(nc.tensor, "meac")
                    for m in range(MC):
                        for n in range(4):
                            ins = nc.tensor.matmul(pd[n][:], shh[:, m * 512 + ts * 128:m * 512 + (ts + 1) * 128],
                                                   swde[:, m * 2048 + n * 512:m * 2048 + (n + 1) * 512],
                                                   start=(m == 0), stop=(m == MC - 1))
                    dmv = p.inc(ins, "medm", 1)
                    tile = tb4 * 4 + ts
                    slot = tile
                    p.w(nc.vector, "medm", dmv)
                    p.w(nc.vector, f"meao{ba}")
                    for n in range(4):
                        if e == 0:
                            v = p.inc(nc.vector.tensor_copy(sacc[ba][:, n * 512:(n + 1) * 512], pd[n][:]), "meac", 1)
                        else:
                            v = p.inc(nc.vector.tensor_scalar(sacc[ba][:, n * 512:(n + 1) * 512], pd[n][:],
                                                              sw8[:, tile * 8 + (e - 1):tile * 8 + e], None,
                                                              ALU.mult), "meac", 1)
                    p.w(nc.gpsimd, "meac", v)
                    prev = slot_cnt.get(slot)
                    if prev is not None:
                        p.w(nc.gpsimd, f"meslot{slot}", prev)
                    ins = nc.gpsimd.dma_start(out=io["accd"][tile * 128:(tile + 1) * 128, :],
                                              in_=sacc[ba][:],
                                              accum_op=(ALU.bypass if e == 0 else ALU.add))
                    slot_cnt[slot] = p.inc(ins, f"meslot{slot}", 16)
                    p.inc(ins, f"meao{ba}", 16) if False else p.inc(nc.gpsimd.engine_nop(), f"meao{ba}", 1)
        p.barrier([f"meslot{s}" for s in range(16)])
        p.release(["me"])


def emit_wout_final(p, C, io):
    nc = p.nc
    with ExitStack() as st:
        sw = p.sbuf("fww", [128, KC * D], BF16, st)
        sob = [p.sbuf(f"fwb{i}", [128, KC * 128], BF16, st) for i in range(2)]
        sg1 = [p.sbuf(f"fwg{i}", [128, D], F32, st) for i in range(2)]
        sx1 = [p.sbuf(f"fwx{i}", [128, D], F32, st) for i in range(2)]
        sso = [p.sbuf(f"fws{i}", [128, D], F32, st) for i in range(2)]
        pso = [p.psum(f"fwp{i}", [128, 512], F32, st) for i in range(4)]
        wout_v = io["woutt"].rearrange("(kc p) m -> kc p m", p=128)
        hm_v = io["hmine"].rearrange("(c p) t -> c p t", p=128)
        for k in range(KC):
            p.inc(nc.gpsimd.dma_start(out=sw[:, k * D:(k + 1) * D], in_=wout_v[k]), "fwwl", 16)
        WL = p.val("fwwl")
        for i in range(16):
            bi = i % 2
            LD, SO, AC = f"fwld{bi}", f"fwso{bi}", f"fwac{bi}"
            if i >= 2:
                p.w(nc.sync, "fwmm")
                p.w(nc.sync, "fwev")
            for c in range(KC):
                p.inc(nc.sync.dma_start(out=sob[bi][:, c * 128:(c + 1) * 128],
                                        in_=hm_v[c, :, i * 128:(i + 1) * 128]), LD, 16)
            acv = p.inc(nc.sync.dma_start(out=sg1[bi][:], in_=io["accd"][i * 128:(i + 1) * 128, :]), AC, 16)
            acv = p.inc(nc.sync.dma_start(out=sx1[bi][:], in_=io["x1o"][i * 128:(i + 1) * 128, :]), AC, 16)
            ldv = p.val(LD)
            p.w(nc.scalar, AC, acv)
            p.w(nc.scalar, "fwev")
            sv = p.inc(nc.scalar.activation(sg1[bi][:], sg1[bi][:], AF.Silu), "fwsl", 1)
            p.w(nc.tensor, LD, ldv)
            if i == 0:
                p.w(nc.tensor, "fwwl", WL)
            for n in range(4):
                p.w(nc.tensor, "fwev")
                for k in range(KC):
                    ins = nc.tensor.matmul(pso[n][:], sob[bi][:, k * 128:(k + 1) * 128],
                                           sw[:, k * D + n * 512:k * D + (n + 1) * 512],
                                           start=(k == 0), stop=(k == KC - 1))
                p.inc(ins, "fwmm", 1)
            mmv = p.val("fwmm")
            for n in range(4):
                p.w(nc.vector, "fwmm", mmv - 3 + n)
                p.w(nc.vector, "fwsl", sv)
                if i >= 2:
                    p.w(nc.vector, SO)
                v = p.inc(nc.vector.tensor_tensor(sso[bi][:, n * 512:(n + 1) * 512], pso[n][:],
                                                  sg1[bi][:, n * 512:(n + 1) * 512], ALU.mult), "fwev", 1)
                p.w(nc.vector, "fwev", v)
                v = p.inc(nc.vector.tensor_tensor(sso[bi][:, n * 512:(n + 1) * 512],
                                                  sso[bi][:, n * 512:(n + 1) * 512],
                                                  sx1[bi][:, n * 512:(n + 1) * 512], ALU.add), "fwev", 1)
                p.w(nc.sync, "fwev", v)
                p.inc(nc.sync.dma_start(out=io["outb"][i * 128:(i + 1) * 128, n * 512:(n + 1) * 512],
                                        in_=sso[bi][:, n * 512:(n + 1) * 512]), SO, 16)
        p.barrier(["fwso0", "fwso1"])
        p.release(["fw"])


def emit_final(p, C, io):
    nc = p.nc
    slog = p.sbuf("fslog", [128, 16 * 8], F32)
    sw8 = p.sbuf("fsw8", [128, 16 * 8], F32)
    emit_ln(p, io["x1o"], io["xn1o"], None, None, 2048, "f1")
    xn1of_v = io["xn1of"].rearrange("(c p) t -> c p t", p=128)
    emit_tokmaj_to_fm(p, io["xn1o"], xn1of_v, C, 2048, "f2")
    emit_router(p, C, io, slog, sw8)
    emit_moe(p, C, io, sw8)
    emit_wout_final(p, C, io)


def final_host_inputs(inputs, core, x1o, hmine):
    d = common_host_inputs(inputs)
    wgs = [np.asarray(inputs["L1_ffn_Wg_s"], np.float32)] + \
          [np.asarray(inputs["L1_ffn_Wg"], np.float32)[e] for e in range(8)]
    wus = [np.asarray(inputs["L1_ffn_Wu_s"], np.float32)] + \
          [np.asarray(inputs["L1_ffn_Wu"], np.float32)[e] for e in range(8)]
    wds = [np.asarray(inputs["L1_ffn_Wd_s"], np.float32)] + \
          [np.asarray(inputs["L1_ffn_Wd"], np.float32)[e] for e in range(8)]
    def relay(w):
        return w.reshape(16, 128, 11, 128).transpose(2, 1, 0, 3).reshape(11, 128, 2048)
    wge_r = np.stack([relay(w) for w in wgs])
    wue_r = np.stack([relay(w) for w in wus])
    wde = np.stack(wds).reshape(9 * 1408, 2048)
    d.update({
        "x1o": _c(np.asarray(x1o, np.float32)),
        "hmine": _bf16(np.asarray(hmine)),
        "wr": _c(np.asarray(inputs["L1_ffn_Wr"], np.float32)),
        "brrep": _c(np.repeat(np.asarray(inputs["L1_ffn_br"], np.float32)[None, :], 128, 0)),
        "rbrep": _c(np.repeat(np.asarray(inputs["L1_ffn_rb"], np.float32)[None, :], 128, 0)),
        "wge_r": _c(wge_r), "wue_r": _c(wue_r), "wde": _c(wde),
        "woutt": _c(np.asarray(inputs["L1_attn_W_out"], np.float32).T),
    })
    return d


def build_final():
    p = Prog()
    io = common_io(p)
    io.update({
        "x1o": p.inp("x1o", [2048, D], F32),
        "hmine": p.inp("hmine", [D, 2048], BF16),
        "wr": p.inp("wr", [D, 8], F32),
        "brrep": p.inp("brrep", [128, 8], F32),
        "rbrep": p.inp("rbrep", [128, 8], F32),
        "wge_r": p.inp("wge_r", [9, 11, 128, 2048], F32),
        "wue_r": p.inp("wue_r", [9, 11, 128, 2048], F32),
        "wde": p.inp("wde", [9 * 1408, 2048], F32),
        "woutt": p.inp("woutt", [D, D], F32),
        "outb": p.outp("outb", [2048, D], F32),
        "xn1o": p.dram("xn1o", [2048, D], F32),
        "xn1of": p.dram("xn1of", [D, 2048], BF16),
        "accd": p.dram("accd", [2048, D], F32),
    })
    p.start()
    C = load_common(p, io)
    p.barrier(["pre"])
    emit_final(p, C, io)
    return p




def fused_host_inputs(inputs, core):
    b, par = core // 2, core % 2
    d = ab_host_inputs(inputs, core)
    dm = mod_host_inputs(inputs, core, np.zeros((4, T), np.float32))
    del dm["scall"]
    d.update({k: v for k, v in dm.items() if k not in d})
    dl = l1_host_inputs(inputs, core, np.zeros((2, 2), np.float32))
    del dl["x1b"]
    d.update({k: v for k, v in dl.items() if k not in d})
    df = final_host_inputs(inputs, core, np.zeros((2, 2), np.float32), np.zeros((2, 2), np.float32))
    del df["x1o"], df["hmine"]
    d.update({k: v for k, v in df.items() if k not in d})
    d["bofftok"] = np.full((128, 1), b * 4096, np.float32)
    d["ownrows"] = _c((np.arange(2048, dtype=np.uint32) + par * 2048).reshape(2048, 1))
    d["hrows"] = _c((((np.arange(2048, dtype=np.uint32) + 2048 * b) * 2) + par).reshape(2048, 1))
    return d


def build_fused():
    p = Prog()
    nc = p.nc
    io = common_io(p)
    io.update({
        "xb": p.inp("xb", [T, D], F32),
        "wq": p.inp("wq", [D, D], F32),
        "wk": p.inp("wk", [D, 512], F32),
        "wv": p.inp("wv", [D, 512], F32),
        "wo": p.inp("wo", [D, D], F32),
        "bq": p.inp("bq", [128, 16], F32),
        "bk": p.inp("bk", [128, 4], F32),
        "bv": p.inp("bv", [128, 512], F32),
        "bo": p.inp("bo", [128, D], F32),
        "cosd": p.inp("cosd", [128, T], BF16),
        "sind": p.inp("sind", [128, T], F32),
        "x0flat": p.inp("x0flat", [B * T, D], F32),
        "cosM": p.inp("cosM", [T, 128], F32),
        "sinM": p.inp("sinM", [T, 128], F32),
        "e4": p.inp("e4", [4, 128], F32),
        "e4t": p.inp("e4t", [128, 4], F32),
        "boff512": p.inp("boff512", [128, 1], F32),
        "boff4096": p.inp("boff4096", [128, 1], F32),
        "ownsel": p.inp("ownsel", [256, 1], U32),
        "ln1w": p.inp("ln1w", [128, D], F32),
        "ln2w": p.inp("ln2w", [128, D], F32),
        "wq2": p.inp("wq2", [D, D], F32),
        "wk2": p.inp("wk2", [D, 1024], F32),
        "wv2": p.inp("wv2", [D, 1024], F32),
        "wo2": p.inp("wo2", [D, D], F32),
        "bq2": p.inp("bq2", [128, 16], F32),
        "bk2": p.inp("bk2", [128, 8], F32),
        "bv2": p.inp("bv2", [128, 1024], F32),
        "wgate_r": p.inp("wgate_r", [64, 128, 2048], F32),
        "wup_r": p.inp("wup_r", [64, 128, 2048], F32),
        "wdown": p.inp("wdown", [8192, D], F32),
        "bofftok": p.inp("bofftok", [128, 1], F32),
        "wint": p.inp("wint", [D, D], F32),
        "wgu": p.inp("wgu", [D, 4096], F32),
        "bgu": p.inp("bgu", [128, 32], F32),
        "rmsw": p.inp("rmsw", [128, 4], F32),
        "wr": p.inp("wr", [D, 8], F32),
        "brrep": p.inp("brrep", [128, 8], F32),
        "rbrep": p.inp("rbrep", [128, 8], F32),
        "wge_r": p.inp("wge_r", [9, 11, 128, 2048], F32),
        "wue_r": p.inp("wue_r", [9, 11, 128, 2048], F32),
        "wde": p.inp("wde", [9 * 1408, 2048], F32),
        "woutt": p.inp("woutt", [D, D], F32),
        "ownrows": p.inp("ownrows", [2048, 1], U32),
        "hrows": p.inp("hrows", [2048, 1], U32),
        "outb": p.outp("outb", [2048, D], F32),
    })
    io.update({
        "xn": p.dram("xn", [T, D], F32),
        "s0": p.dram("s0", [T, D], F32),
        "xnf": p.dram("xnf", [D, T], BF16),
        "qf": p.dram("qf", [D, T], BF16),
        "kf": p.dram("kf", [512, T], BF16),
        "vt": p.dram("vt", [T, 512], BF16),
        "of": p.dram("of", [D, T], BF16),
        "idxsel": p.dram("idxsel", [S_SEL, 4], F32),
        "gatedo": p.dram("gatedo_in", [256, D], F32),
        "selx": p.dram("selx", [S_SEL, D], F32),
        "selxn": p.dram("selxn", [S_SEL, D], F32),
        "hseltok": p.dram("hseltok", [S_SEL, D], F32),
        "hselfm": p.dram("hselfm", [D, S_SEL], BF16),
        "costk": p.dram("costk", [S_SEL, 128], F32),
        "sintk": p.dram("sintk", [S_SEL, 128], F32),
        "cosf2": p.dram("cosf2", [128, S_SEL], BF16),
        "sinf2": p.dram("sinf2", [128, S_SEL], F32),
        "qf2": p.dram("qf2", [D, S_SEL], BF16),
        "kf2": p.dram("kf2", [1024, S_SEL], BF16),
        "vt2": p.dram("vt2", [S_SEL, 1024], BF16),
        "of2": p.dram("of2", [D, S_SEL], BF16),
        "x1sel": p.dram("x1sel", [S_SEL, D], F32),
        "x1own": p.dram("x1own", [256, D], F32),
        "selxo": p.dram("selxo", [256, D], F32),
        "idxo": p.dram("idxo", [256, 4], F32),
        "h2own": p.dram("h2own", [256, D], F32),
        "h2f": p.dram("h2f", [D, 256], BF16),
        "procown": p.dram("procown", [256, D], F32),
        "x1b": p.dram("x1b", [T, D], F32),
        "xn1": p.dram("xn1", [T, D], F32),
        "xn1f": p.dram("xn1f", [D, T], BF16),
        "xpf": p.dram("xpf", [D, T], BF16),
        "x1o": p.dram("x1o", [2048, D], F32),
        "hmine": p.dram("hmineg", [D, 2048], BF16),
        "xn1o": p.dram("xn1o", [2048, D], F32),
        "xn1of": p.dram("xn1of", [D, 2048], BF16),
        "accd": p.dram("accd", [2048, D], F32),
    })
    sc_in = p.dram("cc_sc_in", [T, 1], F32)
    sc_out = p.dram("cc_sc_out", [8 * T, 1], F32, addr_space="Shared")
    g_out = p.dram("cc_g_out", [S_SEL, D], F32, addr_space="Shared")
    h_in = p.dram("cc_h_in", [1024, T], BF16)
    h_out = p.dram("cc_h_out", [8 * 1024, T], BF16, addr_space="Shared")
    io["sc"] = sc_in
    io["hblk"] = h_in

    p.start()
    C = load_common(p, io)

    p.barrier(["pre"])

    def _ccsc_hook():
        p.inc(nc.gpsimd.collective_compute(
            "AllGather", ALU.bypass, replica_groups=[list(range(8))],
            ins=[sc_in.opt()], outs=[sc_out.opt()]), "ccsc", 1)
    io["ccsc_hook"] = _ccsc_hook
    emit_ab(p, C, io)

    for eng in (nc.tensor, nc.vector, nc.scalar, nc.gpsimd, nc.sync):
        p.w(eng, "ccsc")
    io["scall_aps"] = [
        sc_out[2 * b * T:(2 * b) * T + T, :].rearrange("(jp q) o -> jp (q o)", q=128)
        for b in range(4)]
    io["gatedo"] = io["gatedo"]
    emit_mod(p, C, io)
    p.inc(nc.gpsimd.collective_compute(
        "AllGather", ALU.bypass, replica_groups=[list(range(8))],
        ins=[io["gatedo"].opt()], outs=[g_out.opt()]), "ccg", 1)
    for eng in (nc.tensor, nc.vector, nc.scalar, nc.gpsimd, nc.sync):
        p.w(eng, "ccg")

    io["gated"] = g_out
    io["modw"] = io["xn"]
    io["x0b"] = io["xb"]
    emit_combine(p, C, io, copy_first=False)

    xn1f_v = io["xn1f"].rearrange("(c p) t -> c p t", p=128)
    xpf_v = io["xpf"].rearrange("(c p) t -> c p t", p=128)
    hblk_v = io["hblk"].rearrange("(c p) t -> c p t", p=128)
    emit_ln(p, io["x1b"], io["xn1"], None, None, T, "e1")
    emit_tokmaj_to_fm(p, io["xn1"], xn1f_v, C, T, "e2")
    emit_proj_fm(p, C, xn1f_v, io["wint"].rearrange("(kc p) m -> kc p m", p=128),
                 xpf_v, T, D, "e3")
    emit_scan(p, C, io, xpf_v, hblk_v)
    p.inc(nc.gpsimd.collective_compute(
        "AllGather", ALU.bypass, replica_groups=[list(range(8))],
        ins=[h_in.opt()], outs=[h_out.opt()]), "cch", 1)

    emit_gather_rows_u32(p, io["ownrows"], io["x1b"], io["x1o"], 2048, D, F32, "gf")
    for eng in (nc.tensor, nc.vector, nc.scalar, nc.gpsimd, nc.sync):
        p.w(eng, "cch")
    h_half = h_out.rearrange("(r h) t -> (r h) t") if False else h_out
    h_half_v = h_out.rearrange("r (h t) -> (r h) t", h=2)
    emit_gather_rows_u32(p, io["hrows"], h_half_v, io["hmine"], 2048, 2048, BF16, "gh")

    emit_final(p, C, io)
    return p


def kernel(**inputs):
    inputs = {k: np.asarray(v) for k, v in inputs.items()}
    pf = _prog("fused", build_fused)
    res = run_bass_kernel_spmd(pf.nc, [fused_host_inputs(inputs, c) for c in range(8)],
                               core_ids=list(range(8)))
    out = np.zeros((B, T, D), np.float32)
    for c in range(8):
        b, par = c // 2, c % 2
        out[b, par * 2048:(par + 1) * 2048] = res.results[c]["outb"]
    return out
